# version 25
# baseline (speedup 1.0000x reference)
import numpy as np

import concourse.bass as bass
import concourse.mybir as mybir
from concourse.bass_utils import run_bass_kernel_spmd

F32 = mybir.dt.float32
BF16 = mybir.dt.bfloat16
AF = mybir.ActivationFunctionType
ALU = mybir.AluOpType

D = 1024
DC = 8
FH = 4096
FC = 32
HD = 64
NH = 16
EPS = 1e-5
NEGB = -20.0


def _sigma_vec():
    f = np.arange(FH)
    b = [(f >> i) & 1 for i in range(12)]
    return (b[0] | (b[1] << 1)
            | (b[7] << 2) | (b[8] << 3) | (b[9] << 4) | (b[10] << 5) | (b[11] << 6)
            | (b[2] << 7) | (b[3] << 8) | (b[4] << 9) | (b[5] << 10) | (b[6] << 11))


_SIG = _sigma_vec()
_PSTRIDE = {s: 1 << (2 + s - 7) for s in range(7, 12)}


class KB:
    def __init__(self, nc):
        self.nc = nc
        self.sems = {}
        self.cnt = {}
        self.hw = {}

    def sem(self, name):
        self.sems[name] = self.nc.alloc_semaphore(name)
        self.cnt[name] = 0

    def will_inc(self, name, n=1):
        self.cnt[name] += n
        return self.cnt[name]

    def run(self, lists):
        with self.nc.Block() as blk:
            for ename, ops in lists.items():
                if not ops:
                    continue

                def make(ops, ename):
                    def body(eng):
                        for waits, fn, incs in ops:
                            for (sname, val) in waits:
                                key = (ename, sname)
                                if self.hw.get(key, -1) >= val:
                                    continue
                                self.hw[key] = val
                                eng.wait_ge(self.sems[sname], val)
                            instr = fn(eng) if fn is not None else None
                            for (sname, n) in incs:
                                instr.then_inc(self.sems[sname], n)
                    return body

                getattr(blk, ename)(make(ops, ename))

    def barrier(self):
        n = self.will_inc("bar", 5)
        flush = {"gpsimd": ("g0", "g1", "g2", "g3", "g4", "g5", "g6", "g7", "gcv", "gw", "g8", "g9"),
                 "sync": ("y0", "y1", "y2", "y3", "yo", "sa0", "sa1", "sb0", "sb1", "sc0", "sc1", "sd0", "sd1", "se0", "se1", "h0", "h1")}
        ops = {}
        for ename in ("tensor", "scalar", "vector", "gpsimd", "sync"):
            def mk(ename):
                def body(eng):
                    for sn in flush.get(ename, ()):
                        val = self.cnt[sn]
                        key = (ename, sn)
                        if val and self.hw.get(key, -1) < val:
                            self.hw[key] = val
                            eng.wait_ge(self.sems[sn], val)
                    eng.sem_inc(self.sems["bar"], 1)
                    eng.wait_ge(self.sems["bar"], n)
                return body
            ops[ename] = mk(ename)
        with self.nc.Block() as blk:
            for ename, body in ops.items():
                getattr(blk, ename)(body)


class Ops:
    def __init__(self, kb):
        self.kb = kb
        self.lists = {k: [] for k in ("tensor", "scalar", "vector", "gpsimd", "sync")}

    def fence(self, engines=("tensor", "scalar", "vector", "gpsimd", "sync"),
              sems=("t", "s", "v")):
        w = [(sn, self.kb.cnt[sn]) for sn in sems if self.kb.cnt[sn]]
        for eng in engines:
            self.add(eng, None, wait=list(w))

    def add(self, eng, fn, wait=(), inc=()):
        out = []
        inc2 = []
        for (s, n) in inc:
            inc2.append((s, n))
            out.append((s, self.kb.will_inc(s, n)))
        self.lists[eng].append((list(wait), fn, inc2))
        return out

    def run(self):
        self.kb.run(self.lists)


def build_program(S):
    TT = min(512, S)
    NT = S // TT
    SC = S // 128
    PJ = TT // 128

    nc = bass.Bass("TRN2", target_bir_lowering=False)
    kb = KB(nc)

    def din(name, shape):
        return nc.dram_tensor(name, list(shape), F32, kind="ExternalInput")

    xT = din("xT", [D, S])
    yT = nc.dram_tensor("yT", [D, S], F32, kind="ExternalOutput")

    wq_pan = din("wq_pan", [DC, 128, DC, 128])
    wk_pan = din("wk_pan", [2, 128, DC, 128])
    wvT = din("wvT", [D, 256])
    wo_pan = din("wo_pan", [DC, 128, DC, 128])
    bq_c = din("bq_c", [128, DC])
    bk_c = din("bk_c", [128, 2])
    bv_r = din("bv_r", [1, 256])
    bo_c = din("bo_c", [128, DC])

    w12_pan = din("w12_pan", [FC, 128, 2 * DC, 128])
    h1_pan = din("h1_pan", [FC, 128, FC, 128])
    h2_pan = din("h2_pan", [FC, 128, FC, 128])
    w3_pan = din("w3_pan", [DC, 128, FC, 128])
    s1_c = din("s1_c", [128, FC])
    s2_c = din("s2_c", [128, FC])
    sh1_c = din("sh1_c", [128, FC])
    sh2_c = din("sh2_c", [128, FC])
    s3_c = din("s3_c", [128, DC])

    qk1_pan = din("qk1_pan", [16, 128, DC, 128])
    wv1T = din("wv1T", [D, D])
    wo1_pan = din("wo1_pan", [DC, 128, DC, 128])

    wexpTaug = din("wexpTaug", [FH, D + 1])
    wconP = din("wconP", [FH, D])
    upb_c = din("upb_c", [128, FC])
    bcon_c = din("bcon_c", [128, DC])
    dbias_c = din("dbias_c", [128, DC])
    ub_init = din("ub_init", [128, 32, 128])
    ub_lhs = din("ub_lhs", [6, 128, 32, 128])
    up_init = din("up_init", [128, 32, 128])
    up_lhs = din("up_lhs", [4, 128, 32, 128])
    db_init = din("db_init", [128, 32, 128])
    db_lhs = din("db_lhs", [6, 128, 32, 128])
    dp_init = din("dp_init", [128, 32, 128])
    dp_lhs = din("dp_lhs", [4, 128, 32, 128])

    cos0 = din("cos0", [128, S])
    sin0 = din("sin0", [128, S])
    cos1 = din("cos1", [128, S])
    sin1 = din("sin1", [128, S])
    pi0T = din("pi0T", [128, 128])
    pi1T = din("pi1T", [128, 128])
    identw = din("identw", [128, 128])
    cmaskx = din("cmaskx", [128, TT + 128 * (PJ - 1)])

    t1s = nc.dram_tensor("t1s", [FH, D + 1], BF16, kind="Internal")
    u1s = nc.dram_tensor("u1s", [FH, D], BF16, kind="Internal")
    wup_pan = nc.dram_tensor("wup_pan", [FC, 128, DC, 128], BF16, kind="Internal")
    wcon_pan = nc.dram_tensor("wcon_pan", [DC, 128, FC, 128], BF16, kind="Internal")
    w12b = nc.dram_tensor("w12b", [FC, 128, 2 * DC, 128], BF16, kind="Internal")
    h1b = nc.dram_tensor("h1b", [FC, 128, FC, 128], BF16, kind="Internal")
    h2b = nc.dram_tensor("h2b", [FC, 128, FC, 128], BF16, kind="Internal")
    w3b = nc.dram_tensor("w3b", [DC, 128, FC, 128], BF16, kind="Internal")
    ub_initb = nc.dram_tensor("ub_initb", [128, 32, 128], BF16, kind="Internal")
    ub_lhsb = nc.dram_tensor("ub_lhsb", [6, 128, 32, 128], BF16, kind="Internal")
    up_initb = nc.dram_tensor("up_initb", [128, 32, 128], BF16, kind="Internal")
    up_lhsb = nc.dram_tensor("up_lhsb", [4, 128, 32, 128], BF16, kind="Internal")
    db_initb = nc.dram_tensor("db_initb", [128, 32, 128], BF16, kind="Internal")
    db_lhsb = nc.dram_tensor("db_lhsb", [6, 128, 32, 128], BF16, kind="Internal")
    dp_initb = nc.dram_tensor("dp_initb", [128, 32, 128], BF16, kind="Internal")
    dp_lhsb = nc.dram_tensor("dp_lhsb", [4, 128, 32, 128], BF16, kind="Internal")
    wexpb = nc.dram_tensor("wexpb", [FH, D + 1], BF16, kind="Internal")
    wconb = nc.dram_tensor("wconb", [FH, D], BF16, kind="Internal")

    AS = max(DC * S, FC * TT, 12560)
    X = nc.alloc_sbuf_tensor("X", [128, DC * S], F32)
    A = nc.alloc_sbuf_tensor("A", [128, AS], BF16)
    Bb = nc.alloc_sbuf_tensor("Bb", [128, AS], BF16)
    C = nc.alloc_sbuf_tensor("C", [128, max(DC * S, 8500)], BF16)
    DA = nc.alloc_sbuf_tensor("DA", [128, max(SC * 16 * 65, 4 * FC * 128)], BF16)
    E = nc.alloc_sbuf_tensor("E", [128, 7744], BF16)

    E_E0 = 0
    E_CM = 2048
    E_XB = 2944
    E_NT = 3968
    E_ROW = 4992
    E_COLS = 6528
    E_BVB = 6980
    E_ID = 7236
    E_PI = 7364
    E_ONE = 7492
    E_OR = 7493
    E_EPS = 7622
    E_NEGB = 7624

    def Ef32(off, n):
        return E[:, off:off + 2 * n].bitcast(F32)

    COLS = {}
    _co = E_COLS
    for nm, w in [("bq", DC), ("bk", 2), ("bo", DC), ("s1", FC), ("s2", FC),
                  ("sh1", FC), ("sh2", FC), ("s3", DC), ("bppp", FC),
                  ("bout", DC), ("upb", FC)]:
        COLS[nm] = _co
        _co += 2 * w
    assert _co <= E_BVB, _co

    def col(nm, j, w=1):
        off = COLS[nm]
        return E[:, off + 2 * j: off + 2 * (j + w)].bitcast(F32)

    def lowp(f):
        def g(e):
            with nc.allow_low_precision(reason="intended bf16 operand rounding"):
                return f(e)
        return g

    PS4 = nc.alloc_psum_tensor("PS4", [128, 2048], F32)
    PA = nc.alloc_psum_tensor("PA", [128, 512], F32)
    PB = nc.alloc_psum_tensor("PB", [128, 512], F32)
    PC = nc.alloc_psum_tensor("PC", [128, 512], F32)
    PD = nc.alloc_psum_tensor("PD", [128, 512], F32)
    P = [PS4[:, 0:512], PS4[:, 512:1024], PS4[:, 1024:1536], PS4[:, 1536:2048],
         PA[:], PB[:]]
    PTS = [PD[:, 0:64].bitcast(BF16), PC[:, 0:64].bitcast(BF16)]

    for s in ("t", "s", "v", "g0", "g1", "g2", "g3", "gw", "y0", "y1", "yo", "bar", "g4", "g5", "g6", "g7", "gcv", "y2", "y3", "sa0", "sa1", "sb0", "sb1", "sc0", "sc1", "sd0", "sd1", "se0", "se1", "g8", "g9", "h0", "h1"):
        kb.sem(s)

    ones_c = E[:, E_ONE:E_ONE + 1]
    ones_r = E[0:1, E_OR:E_OR + 128]
    xb = E[:, E_XB:E_XB + 512]
    rowE = E[0:1, E_ROW:E_ROW + 512]
    rowT = E[0:1, E_ROW + 512:E_ROW + 1024]
    rowD = E[0:1, E_ROW + 1024:E_ROW + 1536]
    ntmp = Ef32(E_NT, 512)

    def atile(arena, c, tt):
        return arena[:, c * S + tt * TT:c * S + (tt + 1) * TT]

    def xsrc(c, tt):
        return X[:, c * S + tt * TT:c * S + (tt + 1) * TT]

    o = Ops(kb)
    o.add("sync", lambda e: e.dma_start(
        X[:].rearrange("p (c s) -> p c s", c=DC),
        xT.rearrange("(c p) s -> p c s", p=128)), inc=[("y0", 16)])
    o.add("gpsimd", lambda e: e.dma_start(E[:, E_ID:E_ID + 128], identw[:]),
          inc=[("gw", 16)])
    o.add("gpsimd", lambda e: e.dma_start(
        E[:, E_CM:E_CM + TT + 128 * (PJ - 1)], cmaskx[:]), inc=[("gw", 16)])
    GW_ALL = kb.cnt["gw"]
    for nm, src in [("bq", bq_c), ("bk", bk_c), ("bo", bo_c), ("s1", s1_c),
                    ("s2", s2_c), ("sh1", sh1_c), ("sh2", sh2_c), ("s3", s3_c),
                    ("upb", upb_c)]:
        w = src.shape[1]
        o.add("sync", lambda e, s=src, n=nm, w=w: e.dma_start(col(n, 0, w), s[:]),
              inc=[("y0", 16)])
    o.add("sync", lambda e: e.dma_start(Ef32(E_NT, DC), bcon_c[:]), inc=[("y0", 16)])
    o.add("sync", lambda e: e.dma_start(Ef32(E_NT + 2 * DC, DC), dbias_c[:]),
          inc=[("y0", 16)])
    y0c = kb.cnt["y0"]
    o.add("vector", lambda e: e.tensor_add(col("bout", 0, DC), Ef32(E_NT, DC),
                                           Ef32(E_NT + 2 * DC, DC)),
          wait=[("y0", y0c)], inc=[("v", 1)])
    o.add("vector", lambda e: e.memset(ones_c, 1.0), inc=[("v", 1)])
    o.add("vector", lambda e: e.memset(ones_r, 1.0), inc=[("v", 1)])
    o.add("vector", lambda e: e.memset(Ef32(E_EPS, 1), EPS), inc=[("v", 1)])
    o.add("vector", lambda e: e.memset(Ef32(E_NEGB, 1), NEGB), inc=[("v", 1)])
    o.run()
    kb.barrier()

    o = Ops(kb)
    o.add("sync", lambda e: e.dma_start(Ef32(E_NT, 256)[0:1, :], bv_r[:]),
          inc=[("y0", 16)])
    bvw = ("y0", kb.cnt["y0"])
    cp = o.add("scalar", lambda e: e.activation(
        E[0:1, E_XB:E_XB + 256], Ef32(E_NT, 256)[0:1, :], AF.Copy),
        wait=[bvw], inc=[("s", 1)])
    mm = o.add("tensor", lambda e: e.matmul(
        P[0][:, 0:256], ones_r, E[0:1, E_XB:E_XB + 256], start=True, stop=True),
        wait=[cp[0]], inc=[("t", 1)])
    o.add("vector", lambda e: e.tensor_copy(E[:, E_BVB:E_BVB + 256], P[0][:, 0:256]),
          wait=[mm[0]], inc=[("v", 1)])
    o.run()
    kb.barrier()

    LNST = {"relA": [None, None], "relB": [None, None]}

    def ln_core(o, src_tile, TTw, dst_store, wait0=(), par=0):
        xbuf = [E[:, E_XB:E_XB + 512], E[:, E_XB + 512:E_XB + 1024]]
        sbuf = [E[:, E_E0:E_E0 + 512], E[:, E_E0 + 512:E_E0 + 1024]]
        m2buf = E[0:1, E_E0 + 1024 + 512 * par:E_E0 + 1536 + 512 * par]
        q0, q1 = P[2 * par], P[2 * par + 1]
        relA = LNST["relA"]
        relB = LNST["relB"]
        for c in range(DC):
            pb = c % 2
            wcp = list(wait0) if c == 0 else []
            if relA[pb] is not None:
                wcp.append(relA[pb])
            cp = o.add("scalar", lambda e, c=c, pb=pb: e.activation(
                xbuf[pb][:, 0:TTw], src_tile(c), AF.Copy),
                wait=wcp, inc=[("s", 1)])
            wsq = [cp[0]]
            if relB[pb] is not None:
                wsq.append(relB[pb])
            sq = o.add("scalar", lambda e, pb=pb: e.activation(
                sbuf[pb][:, 0:TTw], xbuf[pb][:, 0:TTw], AF.Square),
                wait=wsq, inc=[("s", 1)])
            mA = o.add("tensor", lambda e, c=c, pb=pb, q0=q0: e.matmul(
                q0[0:1, 0:TTw], ones_c, xbuf[pb][:, 0:TTw],
                start=(c == 0), stop=(c == DC - 1)),
                wait=[cp[0]] + (list(wait0) if c == 0 else []), inc=[("t", 1)])
            relA[pb] = mA[0]
            mB = o.add("tensor", lambda e, c=c, pb=pb, q1=q1: e.matmul(
                q1[0:1, 0:TTw], ones_c, sbuf[pb][:, 0:TTw],
                start=(c == 0), stop=(c == DC - 1)),
                wait=[sq[0]] + ([("v", kb.cnt["v"])] if c == 0 else []),
                inc=[("t", 1)])
            relB[pb] = mB[0]
        tdone = ("t", kb.cnt["t"])
        rowEp = E[0:1, E_ROW + 512 * par:E_ROW + 512 * par + 512]
        rowDp = E[0:1, E_ROW + 1024:E_ROW + 1536]
        me = o.add("scalar", lambda e, q0=q0: e.activation(
            rowEp[:, 0:TTw], q0[0:1, 0:TTw], AF.Copy, scale=1.0 / D),
            wait=[tdone], inc=[("s", 1)])
        m2s = o.add("scalar", lambda e: e.activation(
            m2buf[:, 0:TTw], rowEp[:, 0:TTw], AF.Square),
            wait=[me[0]], inc=[("s", 1)])
        va = o.add("vector", lambda e, q1=q1: e.scalar_tensor_tensor(
            PC[0:1, 0:TTw], q1[0:1, 0:TTw], 1.0 / D, m2buf[:, 0:TTw],
            ALU.mult, ALU.subtract), wait=[tdone, m2s[0]], inc=[("v", 1)])
        sd = o.add("scalar", lambda e: e.activation(
            PD[0:1, 0:TTw], PC[0:1, 0:TTw], AF.Sqrt,
            bias=Ef32(E_EPS, 1)[0:1, :]),
            wait=[va[0]], inc=[("s", 1)])
        rs = o.add("vector", lowp(lambda e: e.reciprocal(
            rowDp[:, 0:TTw], PD[0:1, 0:TTw])), wait=[sd[0]], inc=[("v", 1)])
        bm = o.add("tensor", lambda e: e.matmul(
            PA[:, 0:TTw], ones_r, rowEp[:, 0:TTw], start=True, stop=True),
            wait=[me[0], ("v", kb.cnt["v"])], inc=[("t", 1)])
        br = o.add("tensor", lambda e: e.matmul(
            PB[:, 0:TTw], ones_r, rowDp[:, 0:TTw], start=True, stop=True),
            wait=[rs[0]], inc=[("t", 1)])
        last_v = None
        for c in range(DC):
            w = [bm[0], br[0]] + ([last_v] if last_v else [])
            sb = o.add("vector", lambda e, c=c: e.tensor_sub(
                ntmp[:, 0:TTw], src_tile(c), PA[:, 0:TTw]),
                wait=w, inc=[("v", 1)])
            last_v = dst_store(c, sb[0])
        return me[0], last_v

    def emit_ln(o, src, dst, wait0=()):
        for tt in range(NT):
            def mk_store(tt):
                def store(c, sb):
                    r = o.add("vector", lambda e, c=c: e.tensor_mul(
                        dst(c, tt), ntmp[:, 0:TT], PB[:, 0:TT]),
                        wait=[sb], inc=[("v", 1)])
                    return r[0]
                return store
            ln_core(o, lambda c, tt=tt: src(c, tt), TT, mk_store(tt),
                    par=tt % 2, wait0=wait0 if tt == 0 else ())

    def emit_ln_res(src_of_c, p, TTw, o=None, wait0=(), bar=True):
        own = o is None
        if own:
            o = Ops(kb)

        def store(c, sb):
            mu = o.add("vector", lambda e: e.tensor_mul(
                ntmp[:, 0:TTw], ntmp[:, 0:TTw], PB[:, 0:TTw]),
                wait=[sb], inc=[("v", 1)])
            r = o.add("vector", lambda e, c=c: e.tensor_add(
                X[:, c * S + p * TTw:c * S + (p + 1) * TTw],
                X[:, c * S + p * TTw:c * S + (p + 1) * TTw], ntmp[:, 0:TTw]),
                wait=[mu[0]], inc=[("v", 1)])
            return r[0]
        ln_core(o, src_of_c, TTw, store, wait0=wait0, par=p % 2)
        if own:
            o.run()
        if bar:
            kb.barrier()

    def gemm_stage(o, pans, KC, MC, rhs, evac, arena, slot_off, slot_sz,
                   slot_sems, nt=None, nbank=2, bank0=0, init_rel=None,
                   post_panel=None, first_mm_wait=(), dma_eng="gpsimd",
                   slot_offs=None):
        nt = NT if nt is None else nt
        ns = len(slot_sems)
        if slot_offs is None:
            slot_offs = [slot_off + i * slot_sz for i in range(ns)]
        if init_rel is None:
            init_rel = (None,) * ns
        rel = []
        for x in init_rel:
            if x is None:
                rel.append(None)
            elif isinstance(x, list):
                rel.append(list(x))
            else:
                rel.append([x])
        while len(rel) < ns:
            rel.append(None)
        brel = [None] * nbank
        pdone = [None] * ns
        for m in range(MC):
            sl = m % ns
            sem = slot_sems[sl]
            w = list(rel[sl]) if rel[sl] is not None else []
            slot = arena[:, slot_offs[sl]: slot_offs[sl] + slot_sz]
            pdone[sl] = o.add(dma_eng, lambda e, s=pans[m], d=slot: e.dma_start(
                d.rearrange("p (k o) -> p k o", o=128), s),
                wait=w, inc=[(sem, 16)])[0]
            extra_rel = []
            if post_panel is not None:
                extra_rel = post_panel(o, m, slot, pdone[sl])
            for tt in range(nt):
                bi = (m * nt + tt) % nbank
                bk = bank0 + bi
                wt = [pdone[sl]]
                if m == 0 and tt == 0:
                    wt += list(first_mm_wait)
                if brel[bi] is not None:
                    wt.append(brel[bi])
                mm = None
                for k in range(KC):
                    lhs = slot[:, k * 128:(k + 1) * 128]
                    mm = o.add("tensor", lambda e, lhs=lhs, k=k, tt=tt, bk=bk:
                               e.matmul(P[bk][:, 0:TT], lhs, rhs(k, tt),
                                        start=(k == 0), stop=(k == KC - 1)),
                               wait=wt if k == 0 else [],
                               inc=[("t", 1)] if k == KC - 1 else [])
                ev = evac(o, m, tt, P[bk][:, 0:TT], mm[0])
                brel[bi] = ev
                if tt == nt - 1:
                    rel[sl] = [ev] + extra_rel
        return rel, ("t", kb.cnt["t"])

    def ev_act(dst, func, scale_col=None, bias_col=None):
        def f(o, m, tt, ps, wt):
            kw = {}
            if scale_col is not None:
                kw["scale"] = scale_col(m)
            if bias_col is not None:
                kw["bias"] = bias_col(m)
            r = o.add("scalar", lambda e, d=dst(m, tt), ps=ps, kw=kw: e.activation(
                d, ps, func, **kw), wait=[wt], inc=[("s", 1)])
            return r[0]
        return f

    def emit_attention(lay):
        o = Ops(kb)
        emit_ln(o, xsrc, lambda c, tt: atile(A, c, tt))
        ln_tv = [("t", kb.cnt["t"]), ("v", kb.cnt["v"])]
        o.fence()

        nhv = 4 if lay == 0 else NH
        vw = 64 * nhv
        Vt = DA[:, 0:SC * nhv * 65].rearrange("p (s h d) -> p s h d", h=nhv, d=65)
        for k in range(DC):
            src = wvT[k * 128:(k + 1) * 128, :] if lay == 0 else \
                wv1T[k * 128:(k + 1) * 128, :]
            o.add("gpsimd", lambda e, s=src, k=k: e.dma_start(
                Bb[:, k * vw:(k + 1) * vw], s), inc=[("g2", 16)])
        vld = ("g2", kb.cnt["g2"])
        ms = o.add("vector", lambda e: e.memset(Vt[:, :, :, 64:65], 1.0),
                   inc=[("v", 1)])
        wpsum = min(512, vw)
        hph = wpsum // 64
        nbk = min(4, SC)
        brel = [None] * nbk
        for hf in range(max(1, vw // 512)):
            for g in range((SC + nbk - 1) // nbk):
                for sci in range(min(nbk, SC - g * nbk)):
                    sc = g * nbk + sci
                    mm = None
                    for k in range(DC):
                        w0 = []
                        if k == 0:
                            w0 = [vld]
                            if brel[sci] is not None:
                                w0.append(brel[sci])
                        mm = o.add("tensor", lambda e, k=k, sc=sc, hf=hf, sci=sci:
                                   e.matmul(P[sci][:, 0:wpsum],
                                            A[:, k * S + sc * 128:k * S + (sc + 1) * 128],
                                            Bb[:, k * vw + hf * 512:
                                               k * vw + hf * 512 + wpsum],
                                            start=(k == 0), stop=(k == DC - 1)),
                                   wait=w0, inc=[("t", 1)] if k == DC - 1 else [])
                    dst = Vt[:, sc, hf * hph:(hf + 1) * hph, 0:64]
                    _ = ms
                    if lay == 0:
                        ev = o.add("vector", lambda e, d=dst, sci=sci: e.tensor_tensor(
                            d, P[sci][:, 0:wpsum].rearrange("p (h d) -> p h d", d=64),
                            E[:, E_BVB:E_BVB + 256].rearrange("p (h d) -> p h d", d=64),
                            ALU.add), wait=[mm[0]], inc=[("v", 1)])
                    else:
                        ev = o.add("vector", lambda e, d=dst, sci=sci: e.tensor_copy(
                            d, P[sci][:, 0:wpsum].rearrange("p (h d) -> p h d", d=64)),
                            wait=[mm[0]], inc=[("v", 1)])
                    brel[sci] = ev[0]
        o.fence()

        if lay == 0:
            r, _ = gemm_stage(o, [wq_pan[m] for m in range(DC)], DC, DC,
                              lambda k, tt: atile(A, k, tt),
                              ev_act(lambda m, tt: atile(Bb, m, tt), AF.Identity,
                                     bias_col=lambda m: col("bq", m)),
                              E, E_E0, DC * 128, ("g0", "g1"),
                              init_rel=(ln_tv, ln_tv))
            gemm_stage(o, [wk_pan[m] for m in range(2)], DC, 2,
                       lambda k, tt: atile(A, k, tt),
                       ev_act(lambda m, tt: atile(C, m + 6, tt), AF.Identity,
                              bias_col=lambda m: col("bk", m)),
                       E, E_E0, DC * 128, ("g0", "g1"), nbank=2, bank0=2,
                       init_rel=tuple(r))
        else:
            gemm_stage(o, [qk1_pan[m] for m in range(16)], DC, 16,
                       lambda k, tt: (atile(A, k, tt)),
                       ev_act(lambda m, tt: (atile(Bb, m, tt) if m < 8
                                             else atile(C, m - 8, tt)), AF.Copy),
                       E, E_E0, DC * 128, ("g0", "g1"))
        o.fence()

        cosd, sind, pid = (cos0, sin0, pi0T) if lay == 0 else (cos1, sin1, pi1T)
        o.add("sync", lambda e: e.dma_start(A[:, 0:2 * S].bitcast(F32), cosd[:]),
              wait=[("t", kb.cnt["t"])], inc=[("y0", 16)])
        o.add("sync", lambda e: e.dma_start(A[:, 2 * S:4 * S].bitcast(F32), sind[:]),
              inc=[("y0", 16)])
        o.add("gpsimd", lambda e: e.dma_start(E[:, E_PI:E_PI + 128], pid[:]),
              inc=[("g2", 16)])
        tabw = ("y0", kb.cnt["y0"])
        piw = ("g2", kb.cnt["g2"])
        ctab = A[:, 0:2 * S].bitcast(F32)
        stab = A[:, 2 * S:4 * S].bitcast(F32)
        kch = [6, 7] if lay == 0 else list(range(DC))
        chunks = [(Bb, c) for c in range(DC)] + [(C, c) for c in kch]
        t1b = [E[:, E_XB:E_XB + 512], E[:, E_XB + 512:E_XB + 1024],
               E[:, E_E0:E_E0 + 512], E[:, E_E0 + 512:E_E0 + 1024]]
        brel = [None] * 4
        t1rel = [None] * 4
        iters = []
        for i, (arena, c) in enumerate(chunks):
            for tt in range(NT):
                iters.append((arena, c, tt))
        stash = [None, None]
        for i, (arena, c, tt) in enumerate(iters):
            bk = i % 4
            tsl = i % 4
            mm = o.add("tensor", lambda e, a=arena, c=c, tt=tt, bk=bk: e.matmul(
                P[bk][:, 0:TT], E[:, E_PI:E_PI + 128], atile(a, c, tt),
                start=True, stop=True),
                wait=[piw] + ([brel[bk]] if brel[bk] else []), inc=[("t", 1)])
            t1 = o.add("vector", lambda e, a=arena, c=c, tt=tt, tsl=tsl:
                       e.tensor_mul(t1b[tsl][:, 0:TT], atile(a, c, tt),
                                    ctab[:, tt * TT:(tt + 1) * TT]),
                wait=[tabw] + ([t1rel[tsl]] if t1rel[tsl] else []),
                inc=[("v", 1)])
            if stash[1] is not None:
                stash[1]()
            def mk_tail(a=arena, c=c, tt=tt, bk=bk, tsl=tsl, mmr=mm[0], t1r=t1[0]):
                def tail():
                    t2 = o.add("vector", lambda e: e.tensor_mul(
                        P[bk][:, 0:TT], P[bk][:, 0:TT],
                        stab[:, tt * TT:(tt + 1) * TT]),
                        wait=[mmr], inc=[("v", 1)])
                    ev = o.add("vector", lambda e: e.tensor_add(
                        atile(a, c, tt), t1b[tsl][:, 0:TT], P[bk][:, 0:TT]),
                        wait=[t1r, t2[0]], inc=[("v", 1)])
                    brel[bk] = ev[0]
                    t1rel[tsl] = ev[0]
                return tail
            stash[1] = stash[0]
            stash[0] = mk_tail()
        for st_ in (stash[1], stash[0]):
            if st_ is not None:
                st_()
        o.fence()

        if lay == 0:
            for wave in (range(6), range(6, DC)):
                for cp in wave:
                    kv = cp // 2
                    scc, so = 6 + kv // 2, 64 * (kv % 2)
                    for half in range(2):
                        if scc == cp and so == 64 * half:
                            continue
                        o.add("sync", lambda e, cp=cp, half=half, scc=scc, so=so:
                              e.dma_start(C[64 * half:64 * half + 64,
                                            cp * S:(cp + 1) * S],
                                          C[so:so + 64, scc * S:(scc + 1) * S]),
                              wait=[("y1", kb.cnt["y1"])] if cp == 6 else [],
                              inc=[("y1", 16)])
            o.add("sync", None, wait=[("y1", kb.cnt["y1"])])
            o.fence()

        if lay == 0:
            for m in range(FC):
                o.add("gpsimd", lambda e, m=m: e.dma_start(w12b[m], w12_pan[m]),
                      inc=[("gcv", 16)])
            for m in range(FC):
                o.add("gpsimd", lambda e, m=m: e.dma_start(h1b[m], h1_pan[m]),
                      inc=[("gcv", 16)])
            for m in range(FC):
                o.add("gpsimd", lambda e, m=m: e.dma_start(h2b[m], h2_pan[m]),
                      inc=[("gcv", 16)])
            for m in range(DC):
                o.add("gpsimd", lambda e, m=m: e.dma_start(w3b[m], w3_pan[m]),
                      inc=[("gcv", 16)])
        else:
            for dst, src in ((ub_initb, ub_init), (up_initb, up_init),
                             (db_initb, db_init), (dp_initb, dp_init)):
                o.add("gpsimd", lambda e, dst=dst, src=src: e.dma_start(
                    dst[:], src[:]), inc=[("gcv", 16)])
            for dst, src, n in ((ub_lhsb, ub_lhs, 6), (up_lhsb, up_lhs, 4),
                                (db_lhsb, db_lhs, 6), (dp_lhsb, dp_lhs, 4)):
                for j in range(n):
                    o.add("gpsimd", lambda e, dst=dst, src=src, j=j:
                          e.dma_start(dst[j], src[j]), inc=[("gcv", 16)])
            for r in range(8):
                o.add("gpsimd", lambda e, r=r: e.dma_start(
                    wexpb[r * 512:(r + 1) * 512, :],
                    wexpTaug[r * 512:(r + 1) * 512, :]), inc=[("gcv", 16)])
            for r in range(8):
                o.add("gpsimd", lambda e, r=r: e.dma_start(
                    wconb[r * 512:(r + 1) * 512, :],
                    wconP[r * 512:(r + 1) * 512, :]), inc=[("gcv", 16)])
        yrep = ("y1", kb.cnt["y1"])
        EB = [E[:, E_E0:E_E0 + 1024], E[:, E_E0 + 1024:E_E0 + 2048],
              E[:, E_NT:E_NT + 1024]]
        EROWF = E[:, E_ROW:E_ROW + 1024].bitcast(F32)
        ones_rf = E[:, E_BVB:E_BVB + 256].bitcast(F32)[0:33, 0:128]
        o.add("vector", lambda e: e.memset(PC[0:33, 0:TT], 1.0),
              wait=[("t", kb.cnt["t"]), ("s", kb.cnt["s"])], inc=[("v", 1)])
        ntmp_init = ("v", kb.cnt["v"])
        o.add("vector", lambda e: e.memset(E[32:33, E_OR:E_OR + 128], 1.0),
              inc=[("v", 1)])
        ones_init = ("v", kb.cnt["v"])
        pcrel = None
        psrel = [None, None]
        erel = [None, None, None]
        ctrel = [None, None]
        bcrel = None
        rowrel = None
        jobs = []
        for hp in range(DC):
            for tt in range(NT):
                jobs.append((hp, tt))
        pend = None
        for j, (hp, tt) in enumerate(jobs):
            kvA = hp // 2 if lay == 0 else 2 * hp
            kvB = hp // 2 if lay == 0 else 2 * hp + 1
            scmax = SC if lay == 0 else min(SC, (tt + 1) * PJ)
            expd = [None] * scmax
            lastpv = None

            def sc_mms(sc):
                g = sc % 2
                w0 = [psrel[g]] if psrel[g] else []
                if j == 0 and sc == 0:
                    w0.append(yrep)
                o.add("tensor", lambda e, sc=sc, g=g, hp=hp, tt=tt: e.matmul(
                    P[2 * g][:, 0:TT],
                    C[0:64, hp * S + sc * 128:hp * S + (sc + 1) * 128],
                    Bb[0:64, hp * S + tt * TT:hp * S + (tt + 1) * TT],
                    start=True, stop=True), wait=w0, inc=[("t", 1)])
                return o.add("tensor", lambda e, sc=sc, g=g, hp=hp, tt=tt: e.matmul(
                    P[2 * g + 1][:, 0:TT],
                    C[64:128, hp * S + sc * 128:hp * S + (sc + 1) * 128],
                    Bb[64:128, hp * S + tt * TT:hp * S + (tt + 1) * TT],
                    start=True, stop=True), wait=[], inc=[("t", 1)])

            def exp_op(sc, mmrel):
                g = sc % 2
                eb = sc % 3
                we = [mmrel]
                if erel[eb] is not None:
                    we.append(erel[eb])
                ex = o.add("scalar", lambda e, g=g, eb=eb: e.activation(
                    EB[eb].rearrange("p (u n) -> p u n", u=2)[:, :, 0:TT],
                    PS4[:, g * 1024:(g + 1) * 1024]
                    .rearrange("p (u n) -> p u n", u=2)[:, :, 0:TT],
                    AF.Exp, scale=0.125, bias=Ef32(E_NEGB, 1)),
                    wait=we, inc=[("s", 1)])
                psrel[g] = ex[0]
                ed = ex[0]
                if lay == 1 and sc >= tt * PJ:
                    off = 128 * (PJ - 1) - 128 * (sc - tt * PJ)
                    mk = o.add("vector", lambda e, eb=eb, off=off: e.tensor_mul(
                        EB[eb][:, 0:TT], EB[eb][:, 0:TT],
                        E[:, E_CM + off:E_CM + off + TT]),
                        wait=[ed, ("gw", GW_ALL)], inc=[("v", 1)])
                    mk = o.add("vector", lambda e, eb=eb, off=off: e.tensor_mul(
                        EB[eb][:, 512:512 + TT], EB[eb][:, 512:512 + TT],
                        E[:, E_CM + off:E_CM + off + TT]),
                        wait=[], inc=[("v", 1)])
                    ed = mk[0]
                return ed

            def pv_mms(sc, ed):
                eb = sc % 3
                first = (sc == 0)
                last = (sc == scmax - 1)
                pvw = [ed]
                if first and ctrel[0] is not None:
                    pvw.extend(ctrel[0])
                o.add("tensor", lambda e, sc=sc, eb=eb, kvA=kvA,
                      first=first, last=last: e.matmul(
                          PA[0:65, 0:TT], Vt[:, sc, kvA, :], EB[eb][:, 0:TT],
                          start=first, stop=last),
                      wait=pvw, inc=[("t", 1)])
                w2 = []
                if first and ctrel[1] is not None:
                    w2.extend(ctrel[1])
                r = o.add("tensor", lambda e, sc=sc, eb=eb, kvB=kvB,
                          first=first, last=last: e.matmul(
                              PB[0:65, 0:TT], Vt[:, sc, kvB, :],
                              EB[eb][:, 512:512 + TT],
                              start=first, stop=last),
                          wait=w2, inc=[("t", 1)])
                erel[eb] = r[0]
                return r

            mm0 = sc_mms(0)
            if pend is not None:
                pend[0]()
            expd[0] = exp_op(0, mm0[0])
            for sc in range(1, scmax):
                mm = sc_mms(sc)
                if sc == 1 and pend is not None:
                    pend[1]()
                    pend = None
                lastpv = pv_mms(sc - 1, expd[sc - 1])
                expd[sc] = exp_op(sc, mm[0])
            lastpv = pv_mms(scmax - 1, expd[scmax - 1])

            def make_epi(jj=j, hp=hp, tt=tt, pvrel=lastpv[0]):
                hold = {}

                def epi_scalar():
                    nonlocal rowrel, pcrel
                    cpA = o.add("vector", lambda e, hp=hp, tt=tt: e.tensor_copy(
                        A[0:64, hp * S + tt * TT:hp * S + (tt + 1) * TT],
                        PA[0:64, 0:TT]), wait=[pvrel], inc=[("v", 1)])
                    cpB = o.add("vector", lambda e, hp=hp, tt=tt: e.tensor_copy(
                        A[64:128, hp * S + tt * TT:hp * S + (tt + 1) * TT],
                        PB[0:64, 0:TT]), wait=[], inc=[("v", 1)])
                    if lay == 0:
                        wcd = [pvrel, ntmp_init]
                        if pcrel is not None:
                            wcd.append(pcrel)
                        cdA = o.add("vector", lambda e: e.tensor_copy(
                            PC[0:1, 0:TT], PA[64:65, 0:TT]),
                            wait=wcd, inc=[("v", 1)])
                        cdB = o.add("vector", lambda e: e.tensor_copy(
                            PC[32:33, 0:TT], PB[64:65, 0:TT]),
                            wait=[], inc=[("v", 1)])
                        wrc = [cdB[0]]
                        if rowrel is not None:
                            wrc.append(rowrel)
                        rc = o.add("vector", lowp(lambda e: e.reciprocal(
                            E[0:33, E_ROW:E_ROW + TT], PC[0:33, 0:TT])),
                            wait=wrc, inc=[("v", 1)])
                        pcrel = rc[0]
                        hold["exA"] = rc[0]
                        hold["exB"] = rc[0]
                        ctrel[0] = [cpA[0], cdA[0]]
                        ctrel[1] = [cpB[0], cdB[0]]
                        return
                    wln = [pvrel]
                    if rowrel is not None:
                        wln.append(rowrel)
                    lnA = o.add("scalar", lambda e: e.activation(
                        EROWF[64:65, 0:TT], PA[64:65, 0:TT], AF.Ln),
                        wait=wln, inc=[("s", 1)])
                    hold["exA"] = o.add("scalar", lambda e: e.activation(
                        E[0:1, E_ROW:E_ROW + TT], EROWF[64:65, 0:TT], AF.Exp,
                        scale=-1.0), wait=[lnA[0]], inc=[("s", 1)])[0]
                    lnB = o.add("scalar", lambda e: e.activation(
                        EROWF[96:97, 0:TT], PB[64:65, 0:TT], AF.Ln),
                        wait=list(wln), inc=[("s", 1)])
                    hold["exB"] = o.add("scalar", lambda e: e.activation(
                        E[32:33, E_ROW:E_ROW + TT], EROWF[96:97, 0:TT], AF.Exp,
                        scale=-1.0), wait=[lnB[0]], inc=[("s", 1)])[0]
                    ctrel[0] = [cpA[0], lnA[0]]
                    ctrel[1] = [cpB[0], lnB[0]]

                def epi_tensor():
                    nonlocal bcrel, rowrel
                    wbc = [hold["exA"], ones_init]
                    if bcrel is not None:
                        wbc.append(bcrel)
                    o.add("tensor", lambda e: e.matmul(
                        PD[0:64, 0:TT], ones_r[0:1, 0:64],
                        E[0:1, E_ROW:E_ROW + TT],
                        start=True, stop=True, tile_position=(0, 0)),
                        wait=wbc, inc=[("t", 1)])
                    bc = o.add("tensor", lambda e: e.matmul(
                        PD[64:128, 0:TT], E[32:33, E_OR:E_OR + 64],
                        E[32:33, E_ROW:E_ROW + TT],
                        start=True, stop=True, tile_position=(32, 64)),
                        wait=[hold["exB"]], inc=[("t", 1)])
                    rowrel = bc[0]
                    rb = o.add("vector", lambda e: e.tensor_copy(
                        E[:, E_XB:E_XB + TT], PD[:, 0:TT]),
                        wait=[bc[0]], inc=[("v", 1)])
                    o.add("vector", lambda e, hp=hp, tt=tt: e.tensor_mul(
                        A[0:64, hp * S + tt * TT:hp * S + (tt + 1) * TT],
                        A[0:64, hp * S + tt * TT:hp * S + (tt + 1) * TT],
                        E[0:64, E_XB:E_XB + TT]),
                        wait=[rb[0]], inc=[("v", 1)])
                    mb = o.add("vector", lambda e, hp=hp, tt=tt: e.tensor_mul(
                        A[64:128, hp * S + tt * TT:hp * S + (tt + 1) * TT],
                        A[64:128, hp * S + tt * TT:hp * S + (tt + 1) * TT],
                        E[64:128, E_XB:E_XB + TT]),
                        wait=[], inc=[("v", 1)])
                    bcrel = mb[0]
                return epi_scalar, epi_tensor
            pend = make_epi()
        if pend is not None:
            pend[0]()
            pend[1]()
        o.fence()

        pan = wo_pan if lay == 0 else wo1_pan
        bias = (lambda m: col("bo", m)) if lay == 0 else None
        tvw = [("t", kb.cnt["t"]), ("v", kb.cnt["v"])]
        gemm_stage(o, [pan[m] for m in range(DC)], DC, DC,
                   lambda k, tt: atile(A, k, tt),
                   ev_act(lambda m, tt: atile(Bb, m, tt), AF.Identity, bias_col=bias),
                   E, E_E0, DC * 128, ("g0", "g1"),
                   init_rel=(tvw, tvw))
        o.fence()
        for p in range(NT):
            emit_ln_res(lambda c, p=p: atile(Bb, c, p), p, TT, o=o,
                        wait0=[("s", kb.cnt["s"])] if p == 0 else [], bar=False)
        o.run()
        kb.barrier()

    def emit_ffn0():
        o = Ops(kb)
        emit_ln(o, xsrc, lambda c, tt: atile(C, c, tt))
        o.fence()
        prev = {"w1": (None, None), "h1": (None, None),
                "h2": (None, None), "w3": (None, None)}
        gcv_all = ("gcv", kb.cnt["gcv"])
        for p in range(NT):
            fp32pass = False

            sc_entry = ("s", kb.cnt["s"])
            vc_entry = ("v", kb.cnt["v"])
            o.add("scalar", None, wait=[vc_entry])
            w12p = [w12b[m] for m in range(FC)]
            h1p = [h1b[m] for m in range(FC)]
            h2p = [h2b[m] for m in range(FC)]
            w3p = [w3b[m] for m in range(DC)]
            s1sems = ("g0", "g1")
            s1st = ("sa0", "sa1")
            s1rel = [list(x) if x else None for x in prev["w1"]]
            pdone = [None, None]
            brel = [None] * 4
            vm = None
            for m in range(FC):
                sl = m % 2
                w = list(s1rel[sl]) if s1rel[sl] is not None else []
                if p == 0 and m == sl:
                    w.append(gcv_all)
                slot = DA[:, sl * 2048:(sl + 1) * 2048]
                pdone[sl] = o.add("gpsimd", lambda e, s=w12p[m], d=slot:
                                  e.dma_start(d.rearrange("p (k o) -> p k o",
                                                          o=128), s),
                                  wait=w, inc=[(s1sems[sl], 16)])[0]
                extra = []
                mm1 = mm2 = None
                bk1, bk2 = sl, 2 + sl
                for k in range(DC):
                    wt = [pdone[sl]]
                    if k == 0:
                        wt.append(sc_entry)
                        if brel[bk1] is not None:
                            wt.append(brel[bk1])
                    mm1 = o.add("tensor", lambda e, k=k, slot=slot, bk1=bk1,
                                m=m, p=p: e.matmul(
                                    P[bk1][:, 0:TT],
                                    slot[:, k * 128:(k + 1) * 128],
                                    atile(C, k, p),
                                    start=(k == 0), stop=(k == DC - 1)),
                                wait=wt if k == 0 else [],
                                inc=[("t", 1)] if k == DC - 1 else [])
                for k in range(DC):
                    wt = []
                    if k == 0 and brel[bk2] is not None:
                        wt.append(brel[bk2])
                    mm2 = o.add("tensor", lambda e, k=k, slot=slot, bk2=bk2,
                                m=m, p=p: e.matmul(
                                    P[bk2][:, 0:TT],
                                    slot[:, (DC + k) * 128:(DC + k + 1) * 128],
                                    atile(C, k, p),
                                    start=(k == 0), stop=(k == DC - 1)),
                                wait=wt if k == 0 else [],
                                inc=[("t", 1)] if k == DC - 1 else [])
                e1 = o.add("scalar", lambda e, m=m, bk1=bk1: e.activation(
                    A[:, m * TT:(m + 1) * TT], P[bk1][:, 0:TT], AF.Silu,
                    scale=col("s1", m)), wait=[mm1[0]], inc=[("s", 1)])
                e2 = o.add("scalar", lambda e, m=m, bk2=bk2: e.activation(
                    Bb[:, m * TT:(m + 1) * TT], P[bk2][:, 0:TT], AF.Copy,
                    scale=col("s2", m)), wait=[mm2[0]], inc=[("s", 1)])
                brel[bk1] = e1[0]
                brel[bk2] = e2[0]
                vm = o.add("vector", lambda e, m=m: e.tensor_mul(
                    A[:, m * TT:(m + 1) * TT], A[:, m * TT:(m + 1) * TT],
                    Bb[:, m * TT:(m + 1) * TT]),
                    wait=[e1[0], e2[0]], inc=[("v", 1)])
                s1rel[sl] = [e1[0], e2[0]] + extra
            prev["w1"] = (s1rel[0], s1rel[1])
            t_s1 = ("t", kb.cnt["t"])
            vmul_done = vm[0]
            o.add("scalar", None, wait=[vmul_done])
            rh1, t_h1 = gemm_stage(o, h1p, FC, FC,
                                   lambda k, tt: A[:, k * TT:(k + 1) * TT],
                                   ev_act(lambda m, tt: Bb[:, m * TT:(m + 1) * TT],
                                          AF.Gelu, scale_col=lambda m: col("sh1", m)),
                                   DA, 0, FC * 128, ("g4", "g5"), nt=1,
                                   nbank=4, bank0=0, slot_offs=(4096, 8192),
                                   init_rel=prev["h1"], first_mm_wait=[vmul_done],
                                   post_panel=None)
            prev["h1"] = rh1
            sc_h1 = ("s", kb.cnt["s"])
            o.add("scalar", None, wait=[t_h1])
            rh2, t_h2 = gemm_stage(o, h2p, FC, FC,
                                   lambda k, tt: Bb[:, k * TT:(k + 1) * TT],
                                   ev_act(lambda m, tt: A[:, m * TT:(m + 1) * TT],
                                          AF.Gelu, scale_col=lambda m: col("sh2", m)),
                                   DA, 0, FC * 128, ("g6", "g7"), nt=1,
                                   nbank=4, slot_offs=(12288, 0),
                                   init_rel=(prev["h2"][0],
                                             (s1rel[0] or []) + (s1rel[1] or []) +
                                             [t_s1]),
                                   bank0=0,
                                   first_mm_wait=[sc_h1],
                                   post_panel=None)
            prev["h2"] = (rh2[0], rh2[1])
            prev["w1"] = (prev["w1"][0] + (rh2[1] or []),
                          prev["w1"][1] + (rh2[1] or []))
            sc_h2 = ("s", kb.cnt["s"])
            o.add("scalar", None, wait=[t_h2])
            iw30 = (rh1[0] or []) + [t_h1] + ((prev["w3"][0] or []) if p > 0 else [])
            iw31 = (rh1[1] or []) + [t_h1] + ((prev["w3"][1] or []) if p > 0 else [])
            rw3, t_w3 = gemm_stage(o, w3p, FC, DC,
                                   lambda k, tt: A[:, k * TT:(k + 1) * TT],
                                   ev_act(lambda m, tt: Bb[:, m * TT:(m + 1) * TT],
                                          AF.Copy, scale_col=lambda m: col("s3", m)),
                                   DA, 0, FC * 128, ("g2", "g3"), nt=1,
                                   slot_offs=(4096, 8192),
                                   init_rel=(iw30, iw31), first_mm_wait=[sc_h2],
                                   post_panel=None)
            prev["w3"] = rw3
            prev["h1"] = ((prev["h1"][0] or []) + (rw3[0] or []),
                          (prev["h1"][1] or []) + (rw3[1] or []))
            emit_ln_res(lambda c: Bb[:, c * TT:(c + 1) * TT], p, TT, o=o,
                        wait0=[("s", kb.cnt["s"])], bar=False)
        o.run()
        kb.barrier()

    def emit_compose():
        o = Ops(kb)
        Xv = A[:, 0:4096]
        Fv = [Bb[:, 0:4096], Bb[:, 4096:8192]]
        NBP = 4
        RHSs = [C[:, i * 1028: i * 1028 + 1025] for i in range(NBP)]
        OUTs = [C[:, 4112 + i * 1028: 4112 + i * 1028 + 1025] for i in range(NBP)]
        WST = [Bb[:, 8192 + i * 1024: 8192 + (i + 1) * 1024] for i in range(2)]
        st8 = {"rhs": [None] * NBP, "out": [None] * NBP,
               "pt": [None, None], "pbank": [None] * 4, "wst": [None, None]}

        def chains(init_d, lhs_d, nstg):
            tprev = ("t", kb.cnt["t"])
            ld = o.add("scalar", lambda e: e.dma_start(Xv, init_d[:]),
                       wait=[tprev], inc=[("h1", 16)])
            relc = [ld[0]] * 32
            fload = [None, None]
            fuse = [None, None]
            for j in range(nstg):
                sl = j % 2
                wf = [fuse[sl]] if fuse[sl] else [tprev]
                eng, sem = ("gpsimd", "g1") if sl == 0 else ("sync", "h0")
                fload[sl] = o.add(eng, lambda e, j=j, sl=sl: e.dma_start(
                    Fv[sl], lhs_d[j]), wait=wf, inc=[(sem, 16)])[0]
                lastmm = None
                for c in range(32):
                    bk = c % 4
                    wt = [relc[c], fload[sl]]
                    if st8["pbank"][bk] is not None:
                        wt.append(st8["pbank"][bk])
                    mm = o.add("tensor", lambda e, c=c, bk=bk, sl=sl: e.matmul(
                        P[bk][:, 0:128], Fv[sl][:, c * 128:(c + 1) * 128],
                        Xv[:, c * 128:(c + 1) * 128], start=True, stop=True),
                        wait=wt, inc=[("t", 1)])
                    lastmm = mm[0]
                    if c % 2 == 1:
                        q2 = PS4[:, (bk - 1) * 512:(bk - 1) * 512 + 1024]
                        eng = "scalar" if (c // 2) % 2 == 0 else "vector"
                        sv = "s" if (c // 2) % 2 == 0 else "v"
                        ev = o.add(eng, lambda e, c=c, q2=q2, eng=eng: (
                            e.activation(Xv[:, (c - 1) * 128:(c + 1) * 128],
                                         q2.rearrange("p (u n) -> p u n", u=2)
                                         [:, :, 0:128], AF.Copy)
                            if eng == "scalar" else
                            e.tensor_copy(Xv[:, (c - 1) * 128:(c + 1) * 128],
                                          q2.rearrange("p (u n) -> p u n", u=2)
                                          [:, :, 0:128])),
                            wait=[mm[0]], inc=[(sv, 1)])
                        st8["pbank"][bk] = ev[0]
                        st8["pbank"][bk - 1] = ev[0]
                        relc[c] = ev[0]
                        relc[c - 1] = ev[0]
                fuse[sl] = lastmm
            return relc

        def blockpass(relc, splits, rhs_load, use_out, rhs_w=1025):
            for c in range(32):
                sl = c % NBP
                wr = [st8["rhs"][sl]] if st8["rhs"][sl] else []
                rw = rhs_load(o, c, RHSs[sl][:, 0:rhs_w], wr, sl)
                evs = []
                for i, (a, b) in enumerate(splits):
                    bk = i
                    wt = [relc[c], rw]
                    if st8["pbank"][bk] is not None:
                        wt.append(st8["pbank"][bk])
                    mm = o.add("tensor", lambda e, c=c, a=a, b=b, bk=bk, sl=sl:
                               e.matmul(P[bk][:, 0:b - a],
                                        Xv[:, c * 128:(c + 1) * 128],
                                        RHSs[sl][:, a:b], start=True, stop=True),
                               wait=wt, inc=[("t", 1)])
                    st8["rhs"][sl] = mm[0]
                    wo = [mm[0]]
                    if st8["out"][sl] is not None:
                        wo += st8["out"][sl]
                    ev = o.add("vector", lambda e, a=a, b=b, bk=bk, sl=sl:
                               e.tensor_copy(OUTs[sl][:, a:b], P[bk][:, 0:b - a]),
                               wait=wo, inc=[("v", 1)])
                    st8["pbank"][bk] = ev[0]
                    evs.append(ev[0])
                st8["out"][sl] = use_out(c, OUTs[sl], evs)

        relc = chains(ub_initb, ub_lhsb, 6)

        def ld_wexp(o, c, dst, wr, sl):
            eng = "gpsimd" if sl < 2 else "scalar"
            return o.add(eng, lambda e, c=c, dst=dst: e.dma_start(
                dst, wexpb[c * 128:(c + 1) * 128, :]), wait=wr,
                inc=[(("g4", "g5", "y3", "sc0")[sl], 16)])[0]

        def st_t1(c, OUT, evs):
            st = o.add("sync", lambda e, c=c, OUT=OUT: e.dma_start(
                t1s[c * 128:(c + 1) * 128, :], OUT), wait=evs,
                inc=[(("sd0", "sd1", "se0", "se1")[c % 4], 16)])
            return [st[0]]
        blockpass(relc, [(0, 512), (512, 1024), (1024, 1025)], ld_wexp, st_t1)
        t1_done = [(sn, kb.cnt[sn]) for sn in ("sd0", "sd1", "se0", "se1")]
        relc = chains(up_initb, up_lhsb, 4)

        def ld_t1p(o, c, dst, wr, sl):
            eng = "sync" if sl < 2 else "scalar"
            return o.add(eng, lambda e, c=c, dst=dst: e.dma_start(
                dst, t1s.rearrange("(q cc r) m -> q r cc m",
                                   cc=32, r=4)[:, :, c, :]),
                wait=wr + t1_done, inc=[(("y1", "y2", "y3", "sc0")[sl], 16)])[0]

        def use_wpp(c, OUT, evs):
            bv = o.add("vector", lambda e, c=c, OUT=OUT: e.tensor_add(
                col("bppp", c), OUT[:, 1024:1025], col("upb", c)),
                wait=[evs[2]], inc=[("v", 1)])
            wsl = c % 2
            last_tm = None
            tvs = []
            for k in range(DC):
                psl = k % 2
                PTk = PTS[psl]
                wt = [evs[k // 4], ("gw", GW_ALL)]
                if st8["pt"][psl] is not None:
                    wt.append(st8["pt"][psl])
                tm = o.add("tensor", lambda e, k=k, OUT=OUT, PTk=PTk: e.transpose(
                    PTk, OUT[:, k * 128:(k + 1) * 128], E[:, E_ID:E_ID + 128]),
                    wait=wt, inc=[("t", 1)])
                wv = [tm[0]]
                if k < 2 and st8["wst"][wsl] is not None:
                    wv.append(st8["wst"][wsl])
                tv = o.add("scalar", lambda e, k=k, psl=psl, PTk=PTk, wsl=wsl:
                           e.activation(WST[wsl][:, k * 128:(k + 1) * 128],
                                        PTk, AF.Copy), wait=wv, inc=[("s", 1)])
                st8["pt"][psl] = tv[0]
                tvs.append(tv[0])
                last_tm = tm[0]
            st = o.add("gpsimd", lambda e, c=c, wsl=wsl: e.dma_start(
                wup_pan[c], WST[wsl].rearrange("p (k o) -> p k o", o=128)),
                wait=tvs, inc=[("g8" if wsl == 0 else "g9", 16)])
            st8["wst"][wsl] = st[0]
            return [last_tm, bv[0]]
        blockpass(relc, [(0, 512), (512, 1024), (1024, 1025)], ld_t1p, use_wpp)
        relc = chains(dp_initb, dp_lhsb, 4)

        def ld_wcon(o, c, dst, wr, sl):
            eng = "gpsimd" if sl < 2 else "scalar"
            return o.add(eng, lambda e, c=c, dst=dst: e.dma_start(
                dst[:, 0:1024], wconb[c * 128:(c + 1) * 128, :]), wait=wr,
                inc=[(("g4", "g5", "y3", "sc0")[sl], 16)])[0]

        def st_u1(c, OUT, evs):
            st = o.add("sync", lambda e, c=c, OUT=OUT: e.dma_start(
                u1s[c * 128:(c + 1) * 128, :], OUT[:, 0:1024]),
                wait=evs, inc=[(("sd0", "sd1", "se0", "se1")[c % 4], 16)])
            return [st[0]]
        blockpass(relc, [(0, 512), (512, 1024)], ld_wcon, st_u1, rhs_w=1024)
        u1_done = [(sn, kb.cnt[sn]) for sn in ("sd0", "sd1", "se0", "se1")]
        relc = chains(db_initb, db_lhsb, 6)

        def ld_u1p(o, c, dst, wr, sl):
            eng = "sync" if sl < 2 else "scalar"
            return o.add(eng, lambda e, c=c, dst=dst: e.dma_start(
                dst[:, 0:1024], u1s.rearrange("(q cc r) m -> q r cc m",
                                              cc=32, r=4)[:, :, c, :]),
                wait=wr + u1_done, inc=[(("y1", "y2", "y3", "sc0")[sl], 16)])[0]

        def st_wcon(c, OUT, evs):
            eng = "scalar" if c % 2 == 0 else "gpsimd"
            sem = ("sa0", "g8", "sa1", "g9")[c % 4]
            st = None
            for m in range(DC):
                st = o.add(eng, lambda e, c=c, m=m, OUT=OUT: e.dma_start(
                    wcon_pan.rearrange("m (cc r) q d -> q r cc m d",
                                       r=4)[:, :, c, m, :],
                    OUT[:, m * 128:(m + 1) * 128]),
                    wait=evs, inc=[(sem, 16)])
            return [st[0]]
        blockpass(relc, [(0, 512), (512, 1024)], ld_u1p, st_wcon, rhs_w=1024)
        o.run()
        kb.barrier()

    def emit_ffn1():
        o = Ops(kb)
        emit_ln(o, xsrc, lambda c, tt: atile(C, c, tt))
        o.fence()
        S1o = 4096
        prev_up = (None, None)
        prev_dn = (None, None)
        for p in range(NT):
            sc_entry = ("s", kb.cnt["s"])
            vc_entry = ("v", kb.cnt["v"])
            o.add("scalar", None, wait=[vc_entry])
            rup, t_up = gemm_stage(o, [wup_pan[m] for m in range(FC)], DC, FC,
                                   lambda k, tt, p=p: atile(C, k, p),
                                   ev_act(lambda m, tt: A[:, m * TT:(m + 1) * TT],
                                          AF.Gelu, bias_col=lambda m: col("bppp", m)),
                                   DA, 0, DC * 128, ("g0", "g1", "g4", "g5"), nt=1,
                                   nbank=4,
                                   init_rel=prev_up, first_mm_wait=[sc_entry])
            prev_up = rup
            sc_up = ("s", kb.cnt["s"])
            rdn, t_dn = gemm_stage(o, [wcon_pan[m] for m in range(DC)], FC, DC,
                                   lambda k, tt: A[:, k * TT:(k + 1) * TT],
                                   ev_act(lambda m, tt: Bb[:, m * TT:(m + 1) * TT],
                                          AF.Identity, bias_col=lambda m: col("bout", m)),
                                   DA, S1o, FC * 128, ("g2", "g3", "g6"), nt=1,
                                   nbank=4, bank0=0,
                                   init_rel=prev_dn, first_mm_wait=[sc_up])
            prev_dn = rdn
            emit_ln_res(lambda c: Bb[:, c * TT:(c + 1) * TT], p, TT, o=o,
                        wait0=[("s", kb.cnt["s"])], bar=False)
        o.run()
        kb.barrier()

    emit_attention(0)
    emit_ffn0()
    emit_attention(1)
    emit_compose()
    emit_ffn1()

    o = Ops(kb)
    o.add("sync", lambda e: e.dma_start(
        yT.rearrange("(c p) s -> p c s", p=128),
        X[:].rearrange("p (c s) -> p c s", c=DC)), inc=[("yo", 16)])
    o.add("sync", None, wait=[("yo", kb.cnt["yo"])])
    o.run()
    return nc



def _pan(wT, KC, MC):
    return np.ascontiguousarray(
        wT.reshape(KC, 128, MC, 128).transpose(2, 1, 0, 3)).astype(np.float32)


def _cols(v):
    v = np.asarray(v, dtype=np.float32)
    return np.ascontiguousarray(v.reshape(-1, 128).T)


def _factor_blocks(W_s, stage, space, transpose):
    f = np.arange(FH)
    st = 1 << stage
    if space == "orig":
        pos = f
        of = f
    else:
        pos = np.arange(FH)
        of = _SIG[pos]
    kk = ((of >> (stage + 1)) << stage) | (of & (st - 1))
    bb = (of >> stage) & 1
    par = pos ^ (st if space == "orig" else _PSTRIDE[stage])
    blocks = np.zeros((32, 128, 128), dtype=np.float32)
    bidx, r = pos >> 7, pos & 127
    pr = par & 127
    assert np.all((par >> 7) == bidx)
    blocks[bidx, r, r] = W_s[kk, bb, bb]
    blocks[bidx, r, pr] = W_s[kk, bb, 1 - bb]
    if transpose:
        blocks = blocks.transpose(0, 2, 1)
    return np.ascontiguousarray(blocks.transpose(1, 0, 2))


def _rope_tables_L0(S):
    inv = 1.0 / 10000.0 ** (np.arange(0, HD, 2, dtype=np.float64) / HD)
    pos = np.arange(S, dtype=np.float64)
    ang = pos[:, None] * inv[None, :]
    sin2 = np.concatenate([np.sin(ang), np.sin(ang)], 1)
    cos2 = np.concatenate([np.cos(ang), np.cos(ang)], 1)
    return (np.tile(cos2.T, (2, 1)).astype(np.float32),
            np.tile(sin2.T, (2, 1)).astype(np.float32))


def _rope_tables_L1(S):
    half = HD // 2
    inv = 1.0 / 10000.0 ** (np.arange(half, dtype=np.float64) / half)
    t = np.arange(S, dtype=np.float64)
    fr = t[:, None] * inv[None, :]
    c = np.cos(fr).T
    s = np.sin(fr).T
    return (np.tile(np.concatenate([c, c], 0), (2, 1)).astype(np.float32),
            np.tile(np.concatenate([s, s], 0), (2, 1)).astype(np.float32))


def _pi_L0():
    Pi = np.zeros((64, 64), dtype=np.float32)
    j = np.arange(32)
    Pi[j, 2 * j + 1] = -1.0
    Pi[32 + j, 2 * j] = 1.0
    blk = np.zeros((128, 128), dtype=np.float32)
    blk[0:64, 0:64] = Pi
    blk[64:128, 64:128] = Pi
    return np.ascontiguousarray(blk.T)


def _pi_L1():
    Pi = np.zeros((64, 64), dtype=np.float32)
    j = np.arange(32)
    Pi[j, j + 32] = -1.0
    Pi[j + 32, j] = 1.0
    blk = np.zeros((128, 128), dtype=np.float32)
    blk[0:64, 0:64] = Pi
    blk[64:128, 64:128] = Pi
    return np.ascontiguousarray(blk.T)


def host_inputs(g, S):
    sh = {}
    sh["wq_pan"] = _pan(g["L0_attn_Wq"].T, DC, DC)
    sh["wk_pan"] = _pan(g["L0_attn_Wk"].T, DC, 2)
    sh["wvT"] = np.ascontiguousarray(g["L0_attn_Wv"].T)
    sh["wo_pan"] = _pan(g["L0_attn_Wo"].T, DC, DC)
    sh["bq_c"] = _cols(g["L0_attn_bq"])
    sh["bk_c"] = _cols(g["L0_attn_bk"])
    sh["bv_r"] = np.ascontiguousarray(g["L0_attn_bv"].reshape(1, 256))
    sh["bo_c"] = _cols(g["L0_attn_bo"])
    sh["w12_pan"] = np.ascontiguousarray(np.concatenate(
        [_pan(g["L0_ffn_w1_w"].T, DC, FC), _pan(g["L0_ffn_w2_w"].T, DC, FC)],
        axis=2))
    sh["h1_pan"] = _pan(g["L0_ffn_h1_w"].T, FC, FC)
    sh["h2_pan"] = _pan(g["L0_ffn_h2_w"].T, FC, FC)
    sh["w3_pan"] = _pan(g["L0_ffn_w3_w"].T, FC, DC)
    sh["s1_c"] = _cols(g["L0_ffn_w1_s"])
    sh["s2_c"] = _cols(g["L0_ffn_w2_s"])
    sh["sh1_c"] = _cols(g["L0_ffn_h1_s"])
    sh["sh2_c"] = _cols(g["L0_ffn_h2_s"])
    sh["s3_c"] = _cols(g["L0_ffn_w3_s"])
    wqkvT = g["L1_attn_w_qkv"].T
    sh["qk1_pan"] = _pan(wqkvT[:, 0:2048], DC, 16)
    sh["wv1T"] = np.ascontiguousarray(wqkvT[:, 2048:3072])
    sh["wo1_pan"] = _pan(g["L1_attn_w_out"].T, DC, DC)
    sh["wexpTaug"] = np.ascontiguousarray(
        np.concatenate([g["L1_ffn_W_exp"].T,
                        g["L1_ffn_b_exp"].reshape(-1, 1)], axis=1)).astype(np.float32)
    sh["wconP"] = np.ascontiguousarray(g["L1_ffn_W_con"][_SIG, :])
    sh["upb_c"] = _cols(g["L1_ffn_up_bias"][_SIG])
    sh["bcon_c"] = _cols(g["L1_ffn_b_con"])
    sh["dbias_c"] = _cols(g["L1_ffn_down_bias"])
    uw = g["L1_ffn_up_weights"].reshape(12, FH // 2, 2, 2)
    dw = g["L1_ffn_down_weights"].reshape(12, FH // 2, 2, 2)
    sh["ub_init"] = _factor_blocks(uw[6], 6, "orig", True)
    sh["ub_lhs"] = np.stack([_factor_blocks(uw[j], j, "orig", False)
                             for j in [5, 4, 3, 2, 1, 0]])
    sh["up_init"] = _factor_blocks(uw[11], 11, "perm", True)
    sh["up_lhs"] = np.stack([_factor_blocks(uw[j], j, "perm", False)
                             for j in [10, 9, 8, 7]])
    sh["db_init"] = _factor_blocks(dw[0], 0, "orig", False)
    sh["db_lhs"] = np.stack([_factor_blocks(dw[j], j, "orig", True)
                             for j in [1, 2, 3, 4, 5, 6]])
    sh["dp_init"] = _factor_blocks(dw[7], 7, "perm", False)
    sh["dp_lhs"] = np.stack([_factor_blocks(dw[j], j, "perm", True)
                             for j in [8, 9, 10, 11]])
    sh["cos0"], sh["sin0"] = _rope_tables_L0(S)
    sh["cos1"], sh["sin1"] = _rope_tables_L1(S)
    sh["pi0T"] = _pi_L0()
    sh["pi1T"] = _pi_L1()
    sh["identw"] = np.eye(128, dtype=np.float32)
    TT = min(512, S)
    PJ = TT // 128
    W = TT + 128 * (PJ - 1)
    sp = np.arange(128)[:, None]
    u = np.arange(W)[None, :]
    sh["cmaskx"] = (sp <= u - 128 * (PJ - 1)).astype(np.float32)
    return sh


_CACHE = {}


def kernel(**inputs):
    S = int(inputs["L0_attn_hidden_states"].shape[1])
    if S not in _CACHE:
        _CACHE[S] = build_program(S)
    nc = _CACHE[S]
    g = {k: np.ascontiguousarray(np.asarray(v), dtype=np.float32)
         for k, v in inputs.items()}
    sh = host_inputs(g, S)
    x = g["L0_attn_hidden_states"]
    Bn = x.shape[0]
    in_maps = []
    for b in range(Bn):
        m = dict(sh)
        m["xT"] = np.ascontiguousarray(x[b].T)
        in_maps.append(m)
    res = run_bass_kernel_spmd(nc, in_maps, core_ids=list(range(Bn)))
    out = np.stack([np.asarray(r["yT"]).T for r in res.results], axis=0)
    return np.ascontiguousarray(out.astype(np.float32))


# version 26
# speedup vs baseline: 1.0007x; 1.0007x over previous
import numpy as np

import concourse.bass as bass
import concourse.mybir as mybir
from concourse.bass_utils import run_bass_kernel_spmd

F32 = mybir.dt.float32
BF16 = mybir.dt.bfloat16
AF = mybir.ActivationFunctionType
ALU = mybir.AluOpType

D = 1024
DC = 8
FH = 4096
FC = 32
HD = 64
NH = 16
EPS = 1e-5
NEGB = -20.0


def _sigma_vec():
    f = np.arange(FH)
    b = [(f >> i) & 1 for i in range(12)]
    return (b[0] | (b[1] << 1)
            | (b[7] << 2) | (b[8] << 3) | (b[9] << 4) | (b[10] << 5) | (b[11] << 6)
            | (b[2] << 7) | (b[3] << 8) | (b[4] << 9) | (b[5] << 10) | (b[6] << 11))


_SIG = _sigma_vec()
_PSTRIDE = {s: 1 << (2 + s - 7) for s in range(7, 12)}


class KB:
    def __init__(self, nc):
        self.nc = nc
        self.sems = {}
        self.cnt = {}
        self.hw = {}

    def sem(self, name):
        self.sems[name] = self.nc.alloc_semaphore(name)
        self.cnt[name] = 0

    def will_inc(self, name, n=1):
        self.cnt[name] += n
        return self.cnt[name]

    def run(self, lists):
        with self.nc.Block() as blk:
            for ename, ops in lists.items():
                if not ops:
                    continue

                def make(ops, ename):
                    def body(eng):
                        for waits, fn, incs in ops:
                            for (sname, val) in waits:
                                key = (ename, sname)
                                if self.hw.get(key, -1) >= val:
                                    continue
                                self.hw[key] = val
                                eng.wait_ge(self.sems[sname], val)
                            instr = fn(eng) if fn is not None else None
                            for (sname, n) in incs:
                                instr.then_inc(self.sems[sname], n)
                    return body

                getattr(blk, ename)(make(ops, ename))

    def barrier(self):
        n = self.will_inc("bar", 5)
        flush = {"gpsimd": ("g0", "g1", "g2", "g3", "g4", "g5", "g6", "g7", "gcv", "gw", "g8", "g9"),
                 "sync": ("y0", "y1", "y2", "y3", "yo", "sa0", "sa1", "sb0", "sb1", "sc0", "sc1", "sd0", "sd1", "se0", "se1", "h0", "h1")}
        ops = {}
        for ename in ("tensor", "scalar", "vector", "gpsimd", "sync"):
            def mk(ename):
                def body(eng):
                    for sn in flush.get(ename, ()):
                        val = self.cnt[sn]
                        key = (ename, sn)
                        if val and self.hw.get(key, -1) < val:
                            self.hw[key] = val
                            eng.wait_ge(self.sems[sn], val)
                    eng.sem_inc(self.sems["bar"], 1)
                    eng.wait_ge(self.sems["bar"], n)
                return body
            ops[ename] = mk(ename)
        with self.nc.Block() as blk:
            for ename, body in ops.items():
                getattr(blk, ename)(body)


class Ops:
    def __init__(self, kb):
        self.kb = kb
        self.lists = {k: [] for k in ("tensor", "scalar", "vector", "gpsimd", "sync")}

    def fence(self, engines=("tensor", "scalar", "vector", "gpsimd", "sync"),
              sems=("t", "s", "v")):
        w = [(sn, self.kb.cnt[sn]) for sn in sems if self.kb.cnt[sn]]
        for eng in engines:
            self.add(eng, None, wait=list(w))

    def add(self, eng, fn, wait=(), inc=()):
        out = []
        inc2 = []
        for (s, n) in inc:
            inc2.append((s, n))
            out.append((s, self.kb.will_inc(s, n)))
        self.lists[eng].append((list(wait), fn, inc2))
        return out

    def run(self):
        self.kb.run(self.lists)


def build_program(S):
    TT = min(512, S)
    NT = S // TT
    SC = S // 128
    PJ = TT // 128

    nc = bass.Bass("TRN2", target_bir_lowering=False)
    kb = KB(nc)

    def din(name, shape):
        return nc.dram_tensor(name, list(shape), F32, kind="ExternalInput")

    xT = din("xT", [D, S])
    yT = nc.dram_tensor("yT", [D, S], F32, kind="ExternalOutput")

    wq_pan = din("wq_pan", [DC, 128, DC, 128])
    wk_pan = din("wk_pan", [2, 128, DC, 128])
    wvT = din("wvT", [D, 256])
    wo_pan = din("wo_pan", [DC, 128, DC, 128])
    bq_c = din("bq_c", [128, DC])
    bk_c = din("bk_c", [128, 2])
    bv_r = din("bv_r", [1, 256])
    bo_c = din("bo_c", [128, DC])

    w12_pan = din("w12_pan", [FC, 128, 2 * DC, 128])
    h1_pan = din("h1_pan", [FC, 128, FC, 128])
    h2_pan = din("h2_pan", [FC, 128, FC, 128])
    w3_pan = din("w3_pan", [DC, 128, FC, 128])
    s1_c = din("s1_c", [128, FC])
    s2_c = din("s2_c", [128, FC])
    sh1_c = din("sh1_c", [128, FC])
    sh2_c = din("sh2_c", [128, FC])
    s3_c = din("s3_c", [128, DC])

    qk1_pan = din("qk1_pan", [16, 128, DC, 128])
    wv1T = din("wv1T", [D, D])
    wo1_pan = din("wo1_pan", [DC, 128, DC, 128])

    wexpTaug = din("wexpTaug", [FH, D + 1])
    wconP = din("wconP", [FH, D])
    upb_c = din("upb_c", [128, FC])
    bcon_c = din("bcon_c", [128, DC])
    dbias_c = din("dbias_c", [128, DC])
    ub_init = din("ub_init", [128, 32, 128])
    ub_lhs = din("ub_lhs", [6, 128, 32, 128])
    up_init = din("up_init", [128, 32, 128])
    up_lhs = din("up_lhs", [4, 128, 32, 128])
    db_init = din("db_init", [128, 32, 128])
    db_lhs = din("db_lhs", [6, 128, 32, 128])
    dp_init = din("dp_init", [128, 32, 128])
    dp_lhs = din("dp_lhs", [4, 128, 32, 128])

    cos0 = din("cos0", [128, S])
    sin0 = din("sin0", [128, S])
    cos1 = din("cos1", [128, S])
    sin1 = din("sin1", [128, S])
    pi0T = din("pi0T", [128, 128])
    pi1T = din("pi1T", [128, 128])
    identw = din("identw", [128, 128])
    cmaskx = din("cmaskx", [128, TT + 128 * (PJ - 1)])

    t1s = nc.dram_tensor("t1s", [FH, D + 1], BF16, kind="Internal")
    u1s = nc.dram_tensor("u1s", [FH, D], BF16, kind="Internal")
    wup_pan = nc.dram_tensor("wup_pan", [FC, 128, DC, 128], BF16, kind="Internal")
    wcon_pan = nc.dram_tensor("wcon_pan", [DC, 128, FC, 128], BF16, kind="Internal")
    w12b = nc.dram_tensor("w12b", [FC, 128, 2 * DC, 128], BF16, kind="Internal")
    h1b = nc.dram_tensor("h1b", [FC, 128, FC, 128], BF16, kind="Internal")
    h2b = nc.dram_tensor("h2b", [FC, 128, FC, 128], BF16, kind="Internal")
    w3b = nc.dram_tensor("w3b", [DC, 128, FC, 128], BF16, kind="Internal")
    ub_initb = nc.dram_tensor("ub_initb", [128, 32, 128], BF16, kind="Internal")
    ub_lhsb = nc.dram_tensor("ub_lhsb", [6, 128, 32, 128], BF16, kind="Internal")
    up_initb = nc.dram_tensor("up_initb", [128, 32, 128], BF16, kind="Internal")
    up_lhsb = nc.dram_tensor("up_lhsb", [4, 128, 32, 128], BF16, kind="Internal")
    db_initb = nc.dram_tensor("db_initb", [128, 32, 128], BF16, kind="Internal")
    db_lhsb = nc.dram_tensor("db_lhsb", [6, 128, 32, 128], BF16, kind="Internal")
    dp_initb = nc.dram_tensor("dp_initb", [128, 32, 128], BF16, kind="Internal")
    dp_lhsb = nc.dram_tensor("dp_lhsb", [4, 128, 32, 128], BF16, kind="Internal")
    wexpb = nc.dram_tensor("wexpb", [FH, D + 1], BF16, kind="Internal")
    wconb = nc.dram_tensor("wconb", [FH, D], BF16, kind="Internal")

    AS = max(DC * S, FC * TT, 12560)
    X = nc.alloc_sbuf_tensor("X", [128, DC * S], F32)
    A = nc.alloc_sbuf_tensor("A", [128, AS], BF16)
    Bb = nc.alloc_sbuf_tensor("Bb", [128, AS], BF16)
    C = nc.alloc_sbuf_tensor("C", [128, max(DC * S, 8500)], BF16)
    DA = nc.alloc_sbuf_tensor("DA", [128, max(SC * 16 * 65, 4 * FC * 128)], BF16)
    E = nc.alloc_sbuf_tensor("E", [128, 7744], BF16)

    E_E0 = 0
    E_CM = 2048
    E_XB = 2944
    E_NT = 3968
    E_ROW = 4992
    E_COLS = 6528
    E_BVB = 6980
    E_ID = 7236
    E_PI = 7364
    E_ONE = 7492
    E_OR = 7493
    E_EPS = 7622
    E_NEGB = 7624

    def Ef32(off, n):
        return E[:, off:off + 2 * n].bitcast(F32)

    COLS = {}
    _co = E_COLS
    for nm, w in [("bq", DC), ("bk", 2), ("bo", DC), ("s1", FC), ("s2", FC),
                  ("sh1", FC), ("sh2", FC), ("s3", DC), ("bppp", FC),
                  ("bout", DC), ("upb", FC)]:
        COLS[nm] = _co
        _co += 2 * w
    assert _co <= E_BVB, _co

    def col(nm, j, w=1):
        off = COLS[nm]
        return E[:, off + 2 * j: off + 2 * (j + w)].bitcast(F32)

    def lowp(f):
        def g(e):
            with nc.allow_low_precision(reason="intended bf16 operand rounding"):
                return f(e)
        return g

    PS4 = nc.alloc_psum_tensor("PS4", [128, 2048], F32)
    PA = nc.alloc_psum_tensor("PA", [128, 512], F32)
    PB = nc.alloc_psum_tensor("PB", [128, 512], F32)
    PC = nc.alloc_psum_tensor("PC", [128, 512], F32)
    PD = nc.alloc_psum_tensor("PD", [128, 512], F32)
    P = [PS4[:, 0:512], PS4[:, 512:1024], PS4[:, 1024:1536], PS4[:, 1536:2048],
         PA[:], PB[:]]
    PTS = [PD[:, 0:64].bitcast(BF16), PC[:, 0:64].bitcast(BF16)]

    for s in ("t", "s", "v", "g0", "g1", "g2", "g3", "gw", "y0", "y1", "yo", "bar", "g4", "g5", "g6", "g7", "gcv", "y2", "y3", "sa0", "sa1", "sb0", "sb1", "sc0", "sc1", "sd0", "sd1", "se0", "se1", "g8", "g9", "h0", "h1"):
        kb.sem(s)

    ones_c = E[:, E_ONE:E_ONE + 1]
    ones_r = E[0:1, E_OR:E_OR + 128]
    xb = E[:, E_XB:E_XB + 512]
    rowE = E[0:1, E_ROW:E_ROW + 512]
    rowT = E[0:1, E_ROW + 512:E_ROW + 1024]
    rowD = E[0:1, E_ROW + 1024:E_ROW + 1536]
    ntmp = Ef32(E_NT, 512)

    def atile(arena, c, tt):
        return arena[:, c * S + tt * TT:c * S + (tt + 1) * TT]

    def xsrc(c, tt):
        return X[:, c * S + tt * TT:c * S + (tt + 1) * TT]

    o = Ops(kb)
    o.add("sync", lambda e: e.dma_start(
        X[:].rearrange("p (c s) -> p c s", c=DC),
        xT.rearrange("(c p) s -> p c s", p=128)), inc=[("y0", 16)])
    o.add("gpsimd", lambda e: e.dma_start(E[:, E_ID:E_ID + 128], identw[:]),
          inc=[("gw", 16)])
    o.add("gpsimd", lambda e: e.dma_start(
        E[:, E_CM:E_CM + TT + 128 * (PJ - 1)], cmaskx[:]), inc=[("gw", 16)])
    GW_ALL = kb.cnt["gw"]
    for nm, src in [("bq", bq_c), ("bk", bk_c), ("bo", bo_c), ("s1", s1_c),
                    ("s2", s2_c), ("sh1", sh1_c), ("sh2", sh2_c), ("s3", s3_c),
                    ("upb", upb_c)]:
        w = src.shape[1]
        o.add("sync", lambda e, s=src, n=nm, w=w: e.dma_start(col(n, 0, w), s[:]),
              inc=[("y0", 16)])
    o.add("sync", lambda e: e.dma_start(Ef32(E_NT, DC), bcon_c[:]), inc=[("y0", 16)])
    o.add("sync", lambda e: e.dma_start(Ef32(E_NT + 2 * DC, DC), dbias_c[:]),
          inc=[("y0", 16)])
    y0c = kb.cnt["y0"]
    o.add("vector", lambda e: e.tensor_add(col("bout", 0, DC), Ef32(E_NT, DC),
                                           Ef32(E_NT + 2 * DC, DC)),
          wait=[("y0", y0c)], inc=[("v", 1)])
    o.add("vector", lambda e: e.memset(ones_c, 1.0), inc=[("v", 1)])
    o.add("vector", lambda e: e.memset(ones_r, 1.0), inc=[("v", 1)])
    o.add("vector", lambda e: e.memset(Ef32(E_EPS, 1), EPS), inc=[("v", 1)])
    o.add("vector", lambda e: e.memset(Ef32(E_NEGB, 1), NEGB), inc=[("v", 1)])
    o.run()
    kb.barrier()

    o = Ops(kb)
    o.add("sync", lambda e: e.dma_start(Ef32(E_NT, 256)[0:1, :], bv_r[:]),
          inc=[("y0", 16)])
    bvw = ("y0", kb.cnt["y0"])
    cp = o.add("scalar", lambda e: e.activation(
        E[0:1, E_XB:E_XB + 256], Ef32(E_NT, 256)[0:1, :], AF.Copy),
        wait=[bvw], inc=[("s", 1)])
    mm = o.add("tensor", lambda e: e.matmul(
        P[0][:, 0:256], ones_r, E[0:1, E_XB:E_XB + 256], start=True, stop=True),
        wait=[cp[0]], inc=[("t", 1)])
    o.add("vector", lambda e: e.tensor_copy(E[:, E_BVB:E_BVB + 256], P[0][:, 0:256]),
          wait=[mm[0]], inc=[("v", 1)])
    o.run()
    kb.barrier()

    LNST = {"relA": [None, None], "relB": [None, None]}

    def ln_core(o, src_tile, TTw, dst_store, wait0=(), par=0):
        xbuf = [E[:, E_XB:E_XB + 512], E[:, E_XB + 512:E_XB + 1024]]
        sbuf = [E[:, E_E0:E_E0 + 512], E[:, E_E0 + 512:E_E0 + 1024]]
        m2buf = E[0:1, E_E0 + 1024 + 512 * par:E_E0 + 1536 + 512 * par]
        q0, q1 = P[2 * par], P[2 * par + 1]
        relA = LNST["relA"]
        relB = LNST["relB"]
        for c in range(DC):
            pb = c % 2
            wcp = list(wait0) if c == 0 else []
            if relA[pb] is not None:
                wcp.append(relA[pb])
            cp = o.add("scalar", lambda e, c=c, pb=pb: e.activation(
                xbuf[pb][:, 0:TTw], src_tile(c), AF.Copy),
                wait=wcp, inc=[("s", 1)])
            wsq = [cp[0]]
            if relB[pb] is not None:
                wsq.append(relB[pb])
            sq = o.add("scalar", lambda e, pb=pb: e.activation(
                sbuf[pb][:, 0:TTw], xbuf[pb][:, 0:TTw], AF.Square),
                wait=wsq, inc=[("s", 1)])
            mA = o.add("tensor", lambda e, c=c, pb=pb, q0=q0: e.matmul(
                q0[0:1, 0:TTw], ones_c, xbuf[pb][:, 0:TTw],
                start=(c == 0), stop=(c == DC - 1)),
                wait=[cp[0]] + (list(wait0) if c == 0 else []), inc=[("t", 1)])
            relA[pb] = mA[0]
            mB = o.add("tensor", lambda e, c=c, pb=pb, q1=q1: e.matmul(
                q1[0:1, 0:TTw], ones_c, sbuf[pb][:, 0:TTw],
                start=(c == 0), stop=(c == DC - 1)),
                wait=[sq[0]] + ([("v", kb.cnt["v"])] if c == 0 else []),
                inc=[("t", 1)])
            relB[pb] = mB[0]
        tdone = ("t", kb.cnt["t"])
        rowEp = E[0:1, E_ROW + 512 * par:E_ROW + 512 * par + 512]
        rowDp = E[0:1, E_ROW + 1024:E_ROW + 1536]
        me = o.add("scalar", lambda e, q0=q0: e.activation(
            rowEp[:, 0:TTw], q0[0:1, 0:TTw], AF.Copy, scale=1.0 / D),
            wait=[tdone], inc=[("s", 1)])
        m2s = o.add("scalar", lambda e: e.activation(
            m2buf[:, 0:TTw], rowEp[:, 0:TTw], AF.Square),
            wait=[me[0]], inc=[("s", 1)])
        va = o.add("vector", lambda e, q1=q1: e.scalar_tensor_tensor(
            PC[0:1, 0:TTw], q1[0:1, 0:TTw], 1.0 / D, m2buf[:, 0:TTw],
            ALU.mult, ALU.subtract), wait=[tdone, m2s[0]], inc=[("v", 1)])
        sd = o.add("scalar", lambda e: e.activation(
            PD[0:1, 0:TTw], PC[0:1, 0:TTw], AF.Sqrt,
            bias=Ef32(E_EPS, 1)[0:1, :]),
            wait=[va[0]], inc=[("s", 1)])
        rs = o.add("vector", lowp(lambda e: e.reciprocal(
            rowDp[:, 0:TTw], PD[0:1, 0:TTw])), wait=[sd[0]], inc=[("v", 1)])
        bm = o.add("tensor", lambda e: e.matmul(
            PA[:, 0:TTw], ones_r, rowEp[:, 0:TTw], start=True, stop=True),
            wait=[me[0], ("v", kb.cnt["v"])], inc=[("t", 1)])
        br = o.add("tensor", lambda e: e.matmul(
            PB[:, 0:TTw], ones_r, rowDp[:, 0:TTw], start=True, stop=True),
            wait=[rs[0]], inc=[("t", 1)])
        last_v = None
        for c in range(DC):
            w = [bm[0], br[0]] + ([last_v] if last_v else [])
            sb = o.add("vector", lambda e, c=c: e.tensor_sub(
                ntmp[:, 0:TTw], src_tile(c), PA[:, 0:TTw]),
                wait=w, inc=[("v", 1)])
            last_v = dst_store(c, sb[0])
        return me[0], last_v

    def emit_ln(o, src, dst, wait0=()):
        for tt in range(NT):
            def mk_store(tt):
                def store(c, sb):
                    r = o.add("vector", lambda e, c=c: e.tensor_mul(
                        dst(c, tt), ntmp[:, 0:TT], PB[:, 0:TT]),
                        wait=[sb], inc=[("v", 1)])
                    return r[0]
                return store
            ln_core(o, lambda c, tt=tt: src(c, tt), TT, mk_store(tt),
                    par=tt % 2, wait0=wait0 if tt == 0 else ())

    def emit_ln_res(src_of_c, p, TTw, o=None, wait0=(), bar=True):
        own = o is None
        if own:
            o = Ops(kb)

        def store(c, sb):
            mu = o.add("vector", lambda e: e.tensor_mul(
                ntmp[:, 0:TTw], ntmp[:, 0:TTw], PB[:, 0:TTw]),
                wait=[sb], inc=[("v", 1)])
            r = o.add("vector", lambda e, c=c: e.tensor_add(
                X[:, c * S + p * TTw:c * S + (p + 1) * TTw],
                X[:, c * S + p * TTw:c * S + (p + 1) * TTw], ntmp[:, 0:TTw]),
                wait=[mu[0]], inc=[("v", 1)])
            return r[0]
        ln_core(o, src_of_c, TTw, store, wait0=wait0, par=p % 2)
        if own:
            o.run()
        if bar:
            kb.barrier()

    def gemm_stage(o, pans, KC, MC, rhs, evac, arena, slot_off, slot_sz,
                   slot_sems, nt=None, nbank=2, bank0=0, init_rel=None,
                   post_panel=None, first_mm_wait=(), dma_eng="gpsimd",
                   slot_offs=None):
        nt = NT if nt is None else nt
        ns = len(slot_sems)
        if slot_offs is None:
            slot_offs = [slot_off + i * slot_sz for i in range(ns)]
        if init_rel is None:
            init_rel = (None,) * ns
        rel = []
        for x in init_rel:
            if x is None:
                rel.append(None)
            elif isinstance(x, list):
                rel.append(list(x))
            else:
                rel.append([x])
        while len(rel) < ns:
            rel.append(None)
        brel = [None] * nbank
        pdone = [None] * ns
        for m in range(MC):
            sl = m % ns
            sem = slot_sems[sl]
            w = list(rel[sl]) if rel[sl] is not None else []
            slot = arena[:, slot_offs[sl]: slot_offs[sl] + slot_sz]
            pdone[sl] = o.add(dma_eng, lambda e, s=pans[m], d=slot: e.dma_start(
                d.rearrange("p (k o) -> p k o", o=128), s),
                wait=w, inc=[(sem, 16)])[0]
            extra_rel = []
            if post_panel is not None:
                extra_rel = post_panel(o, m, slot, pdone[sl])
            for tt in range(nt):
                bi = (m * nt + tt) % nbank
                bk = bank0 + bi
                wt = [pdone[sl]]
                if m == 0 and tt == 0:
                    wt += list(first_mm_wait)
                if brel[bi] is not None:
                    wt.append(brel[bi])
                mm = None
                for k in range(KC):
                    lhs = slot[:, k * 128:(k + 1) * 128]
                    mm = o.add("tensor", lambda e, lhs=lhs, k=k, tt=tt, bk=bk:
                               e.matmul(P[bk][:, 0:TT], lhs, rhs(k, tt),
                                        start=(k == 0), stop=(k == KC - 1)),
                               wait=wt if k == 0 else [],
                               inc=[("t", 1)] if k == KC - 1 else [])
                ev = evac(o, m, tt, P[bk][:, 0:TT], mm[0])
                brel[bi] = ev
                if tt == nt - 1:
                    rel[sl] = [ev] + extra_rel
        return rel, ("t", kb.cnt["t"])

    def ev_act(dst, func, scale_col=None, bias_col=None):
        def f(o, m, tt, ps, wt):
            kw = {}
            if scale_col is not None:
                kw["scale"] = scale_col(m)
            if bias_col is not None:
                kw["bias"] = bias_col(m)
            r = o.add("scalar", lambda e, d=dst(m, tt), ps=ps, kw=kw: e.activation(
                d, ps, func, **kw), wait=[wt], inc=[("s", 1)])
            return r[0]
        return f

    def emit_attention(lay):
        o = Ops(kb)
        emit_ln(o, xsrc, lambda c, tt: atile(A, c, tt))
        ln_tv = [("t", kb.cnt["t"]), ("v", kb.cnt["v"])]
        o.fence()

        nhv = 4 if lay == 0 else NH
        vw = 64 * nhv
        Vt = DA[:, 0:SC * nhv * 65].rearrange("p (s h d) -> p s h d", h=nhv, d=65)
        for k in range(DC):
            src = wvT[k * 128:(k + 1) * 128, :] if lay == 0 else \
                wv1T[k * 128:(k + 1) * 128, :]
            o.add("gpsimd", lambda e, s=src, k=k: e.dma_start(
                Bb[:, k * vw:(k + 1) * vw], s), inc=[("g2", 16)])
        vld = ("g2", kb.cnt["g2"])
        ms = o.add("vector", lambda e: e.memset(Vt[:, :, :, 64:65], 1.0),
                   inc=[("v", 1)])
        wpsum = min(512, vw)
        hph = wpsum // 64
        nbk = min(4, SC)
        brel = [None] * nbk
        for hf in range(max(1, vw // 512)):
            for g in range((SC + nbk - 1) // nbk):
                for sci in range(min(nbk, SC - g * nbk)):
                    sc = g * nbk + sci
                    mm = None
                    for k in range(DC):
                        w0 = []
                        if k == 0:
                            w0 = [vld]
                            if brel[sci] is not None:
                                w0.append(brel[sci])
                        mm = o.add("tensor", lambda e, k=k, sc=sc, hf=hf, sci=sci:
                                   e.matmul(P[sci][:, 0:wpsum],
                                            A[:, k * S + sc * 128:k * S + (sc + 1) * 128],
                                            Bb[:, k * vw + hf * 512:
                                               k * vw + hf * 512 + wpsum],
                                            start=(k == 0), stop=(k == DC - 1)),
                                   wait=w0, inc=[("t", 1)] if k == DC - 1 else [])
                    dst = Vt[:, sc, hf * hph:(hf + 1) * hph, 0:64]
                    _ = ms
                    if lay == 0:
                        ev = o.add("vector", lambda e, d=dst, sci=sci: e.tensor_tensor(
                            d, P[sci][:, 0:wpsum].rearrange("p (h d) -> p h d", d=64),
                            E[:, E_BVB:E_BVB + 256].rearrange("p (h d) -> p h d", d=64),
                            ALU.add), wait=[mm[0]], inc=[("v", 1)])
                    else:
                        ev = o.add("vector", lambda e, d=dst, sci=sci: e.tensor_copy(
                            d, P[sci][:, 0:wpsum].rearrange("p (h d) -> p h d", d=64)),
                            wait=[mm[0]], inc=[("v", 1)])
                    brel[sci] = ev[0]
        o.fence()

        if lay == 0:
            r, _ = gemm_stage(o, [wq_pan[m] for m in range(DC)], DC, DC,
                              lambda k, tt: atile(A, k, tt),
                              ev_act(lambda m, tt: atile(Bb, m, tt), AF.Identity,
                                     bias_col=lambda m: col("bq", m)),
                              E, E_E0, DC * 128, ("g0", "g1"),
                              init_rel=(ln_tv, ln_tv))
            gemm_stage(o, [wk_pan[m] for m in range(2)], DC, 2,
                       lambda k, tt: atile(A, k, tt),
                       ev_act(lambda m, tt: atile(C, m + 6, tt), AF.Identity,
                              bias_col=lambda m: col("bk", m)),
                       E, E_E0, DC * 128, ("g0", "g1"), nbank=2, bank0=2,
                       init_rel=tuple(r))
        else:
            gemm_stage(o, [qk1_pan[m] for m in range(16)], DC, 16,
                       lambda k, tt: (atile(A, k, tt)),
                       ev_act(lambda m, tt: (atile(Bb, m, tt) if m < 8
                                             else atile(C, m - 8, tt)), AF.Copy),
                       E, E_E0, DC * 128, ("g0", "g1"))
        o.fence()

        cosd, sind, pid = (cos0, sin0, pi0T) if lay == 0 else (cos1, sin1, pi1T)
        o.add("sync", lambda e: e.dma_start(A[:, 0:2 * S].bitcast(F32), cosd[:]),
              wait=[("t", kb.cnt["t"])], inc=[("y0", 16)])
        o.add("sync", lambda e: e.dma_start(A[:, 2 * S:4 * S].bitcast(F32), sind[:]),
              inc=[("y0", 16)])
        o.add("gpsimd", lambda e: e.dma_start(E[:, E_PI:E_PI + 128], pid[:]),
              inc=[("g2", 16)])
        tabw = ("y0", kb.cnt["y0"])
        piw = ("g2", kb.cnt["g2"])
        ctab = A[:, 0:2 * S].bitcast(F32)
        stab = A[:, 2 * S:4 * S].bitcast(F32)
        kch = [6, 7] if lay == 0 else list(range(DC))
        chunks = [(Bb, c) for c in range(DC)] + [(C, c) for c in kch]
        t1b = [E[:, E_XB:E_XB + 512], E[:, E_XB + 512:E_XB + 1024],
               E[:, E_E0:E_E0 + 512], E[:, E_E0 + 512:E_E0 + 1024]]
        brel = [None] * 4
        t1rel = [None] * 4
        iters = []
        for i, (arena, c) in enumerate(chunks):
            for tt in range(NT):
                iters.append((arena, c, tt))
        stash = [None, None]
        for i, (arena, c, tt) in enumerate(iters):
            bk = i % 4
            tsl = i % 4
            mm = o.add("tensor", lambda e, a=arena, c=c, tt=tt, bk=bk: e.matmul(
                P[bk][:, 0:TT], E[:, E_PI:E_PI + 128], atile(a, c, tt),
                start=True, stop=True),
                wait=[piw] + ([brel[bk]] if brel[bk] else []), inc=[("t", 1)])
            t1 = o.add("vector", lambda e, a=arena, c=c, tt=tt, tsl=tsl:
                       e.tensor_mul(t1b[tsl][:, 0:TT], atile(a, c, tt),
                                    ctab[:, tt * TT:(tt + 1) * TT]),
                wait=[tabw] + ([t1rel[tsl]] if t1rel[tsl] else []),
                inc=[("v", 1)])
            if stash[1] is not None:
                stash[1]()
            def mk_tail(a=arena, c=c, tt=tt, bk=bk, tsl=tsl, mmr=mm[0], t1r=t1[0]):
                def tail():
                    t2 = o.add("vector", lambda e: e.tensor_mul(
                        P[bk][:, 0:TT], P[bk][:, 0:TT],
                        stab[:, tt * TT:(tt + 1) * TT]),
                        wait=[mmr], inc=[("v", 1)])
                    ev = o.add("vector", lambda e: e.tensor_add(
                        atile(a, c, tt), t1b[tsl][:, 0:TT], P[bk][:, 0:TT]),
                        wait=[t1r, t2[0]], inc=[("v", 1)])
                    brel[bk] = ev[0]
                    t1rel[tsl] = ev[0]
                return tail
            stash[1] = stash[0]
            stash[0] = mk_tail()
        for st_ in (stash[1], stash[0]):
            if st_ is not None:
                st_()
        o.fence()

        if lay == 0:
            for wave in (range(6), range(6, DC)):
                for cp in wave:
                    kv = cp // 2
                    scc, so = 6 + kv // 2, 64 * (kv % 2)
                    for half in range(2):
                        if scc == cp and so == 64 * half:
                            continue
                        o.add("sync", lambda e, cp=cp, half=half, scc=scc, so=so:
                              e.dma_start(C[64 * half:64 * half + 64,
                                            cp * S:(cp + 1) * S],
                                          C[so:so + 64, scc * S:(scc + 1) * S]),
                              wait=[("y1", kb.cnt["y1"])] if cp == 6 else [],
                              inc=[("y1", 16)])
            o.add("sync", None, wait=[("y1", kb.cnt["y1"])])
            o.fence()

        if lay == 0:
            for m in range(FC):
                o.add("gpsimd", lambda e, m=m: e.dma_start(w12b[m], w12_pan[m]),
                      inc=[("gcv", 16)])
            for m in range(FC):
                o.add("gpsimd", lambda e, m=m: e.dma_start(h1b[m], h1_pan[m]),
                      inc=[("gcv", 16)])
            for m in range(FC):
                o.add("gpsimd", lambda e, m=m: e.dma_start(h2b[m], h2_pan[m]),
                      inc=[("gcv", 16)])
            for m in range(DC):
                o.add("gpsimd", lambda e, m=m: e.dma_start(w3b[m], w3_pan[m]),
                      inc=[("gcv", 16)])
        else:
            for dst, src in ((ub_initb, ub_init), (up_initb, up_init),
                             (db_initb, db_init), (dp_initb, dp_init)):
                o.add("gpsimd", lambda e, dst=dst, src=src: e.dma_start(
                    dst[:], src[:]), inc=[("gcv", 16)])
            for dst, src, n in ((ub_lhsb, ub_lhs, 6), (up_lhsb, up_lhs, 4),
                                (db_lhsb, db_lhs, 6), (dp_lhsb, dp_lhs, 4)):
                for j in range(n):
                    o.add("gpsimd", lambda e, dst=dst, src=src, j=j:
                          e.dma_start(dst[j], src[j]), inc=[("gcv", 16)])
            for r in range(8):
                o.add("gpsimd", lambda e, r=r: e.dma_start(
                    wexpb[r * 512:(r + 1) * 512, :],
                    wexpTaug[r * 512:(r + 1) * 512, :]), inc=[("gcv", 16)])
            for r in range(8):
                o.add("gpsimd", lambda e, r=r: e.dma_start(
                    wconb[r * 512:(r + 1) * 512, :],
                    wconP[r * 512:(r + 1) * 512, :]), inc=[("gcv", 16)])
        yrep = ("y1", kb.cnt["y1"])
        EB = [E[:, E_E0:E_E0 + 1024], E[:, E_E0 + 1024:E_E0 + 2048],
              E[:, E_NT:E_NT + 1024]]
        EROWF = E[:, E_ROW:E_ROW + 1024].bitcast(F32)
        ones_rf = E[:, E_BVB:E_BVB + 256].bitcast(F32)[0:33, 0:128]
        o.add("vector", lambda e: e.memset(E[32:33, E_OR:E_OR + 128], 1.0),
              wait=[("t", kb.cnt["t"]), ("s", kb.cnt["s"])], inc=[("v", 1)])
        ones_init = ("v", kb.cnt["v"])
        psrel = [None, None]
        erel = [None, None, None]
        ctrel = [None, None]
        bcrel = None
        rowrel = None
        jobs = []
        for hp in range(DC):
            for tt in range(NT):
                jobs.append((hp, tt))
        pend = None
        for j, (hp, tt) in enumerate(jobs):
            kvA = hp // 2 if lay == 0 else 2 * hp
            kvB = hp // 2 if lay == 0 else 2 * hp + 1
            scmax = SC if lay == 0 else min(SC, (tt + 1) * PJ)
            expd = [None] * scmax
            lastpv = None

            def sc_mms(sc):
                g = sc % 2
                w0 = [psrel[g]] if psrel[g] else []
                if j == 0 and sc == 0:
                    w0.append(yrep)
                o.add("tensor", lambda e, sc=sc, g=g, hp=hp, tt=tt: e.matmul(
                    P[2 * g][:, 0:TT],
                    C[0:64, hp * S + sc * 128:hp * S + (sc + 1) * 128],
                    Bb[0:64, hp * S + tt * TT:hp * S + (tt + 1) * TT],
                    start=True, stop=True), wait=w0, inc=[("t", 1)])
                return o.add("tensor", lambda e, sc=sc, g=g, hp=hp, tt=tt: e.matmul(
                    P[2 * g + 1][:, 0:TT],
                    C[64:128, hp * S + sc * 128:hp * S + (sc + 1) * 128],
                    Bb[64:128, hp * S + tt * TT:hp * S + (tt + 1) * TT],
                    start=True, stop=True), wait=[], inc=[("t", 1)])

            def exp_op(sc, mmrel):
                g = sc % 2
                eb = sc % 3
                we = [mmrel]
                if erel[eb] is not None:
                    we.append(erel[eb])
                ex = o.add("scalar", lambda e, g=g, eb=eb: e.activation(
                    EB[eb].rearrange("p (u n) -> p u n", u=2)[:, :, 0:TT],
                    PS4[:, g * 1024:(g + 1) * 1024]
                    .rearrange("p (u n) -> p u n", u=2)[:, :, 0:TT],
                    AF.Exp, scale=0.125, bias=Ef32(E_NEGB, 1)),
                    wait=we, inc=[("s", 1)])
                psrel[g] = ex[0]
                ed = ex[0]
                if lay == 1 and sc >= tt * PJ:
                    off = 128 * (PJ - 1) - 128 * (sc - tt * PJ)
                    mk = o.add("vector", lambda e, eb=eb, off=off: e.tensor_mul(
                        EB[eb][:, 0:TT], EB[eb][:, 0:TT],
                        E[:, E_CM + off:E_CM + off + TT]),
                        wait=[ed, ("gw", GW_ALL)], inc=[("v", 1)])
                    mk = o.add("vector", lambda e, eb=eb, off=off: e.tensor_mul(
                        EB[eb][:, 512:512 + TT], EB[eb][:, 512:512 + TT],
                        E[:, E_CM + off:E_CM + off + TT]),
                        wait=[], inc=[("v", 1)])
                    ed = mk[0]
                return ed

            def pv_mms(sc, ed):
                eb = sc % 3
                first = (sc == 0)
                last = (sc == scmax - 1)
                pvw = [ed]
                if first and ctrel[0] is not None:
                    pvw.extend(ctrel[0])
                o.add("tensor", lambda e, sc=sc, eb=eb, kvA=kvA,
                      first=first, last=last: e.matmul(
                          PA[0:65, 0:TT], Vt[:, sc, kvA, :], EB[eb][:, 0:TT],
                          start=first, stop=last),
                      wait=pvw, inc=[("t", 1)])
                w2 = []
                if first and ctrel[1] is not None:
                    w2.extend(ctrel[1])
                r = o.add("tensor", lambda e, sc=sc, eb=eb, kvB=kvB,
                          first=first, last=last: e.matmul(
                              PB[0:65, 0:TT], Vt[:, sc, kvB, :],
                              EB[eb][:, 512:512 + TT],
                              start=first, stop=last),
                          wait=w2, inc=[("t", 1)])
                erel[eb] = r[0]
                return r

            mm0 = sc_mms(0)
            expd[0] = exp_op(0, mm0[0])
            if pend is not None:
                pend[0]()
            for sc in range(1, scmax):
                mm = sc_mms(sc)
                if sc == 1 and pend is not None:
                    pend[1]()
                    pend = None
                lastpv = pv_mms(sc - 1, expd[sc - 1])
                expd[sc] = exp_op(sc, mm[0])
            lastpv = pv_mms(scmax - 1, expd[scmax - 1])

            def make_epi(jj=j, hp=hp, tt=tt, pvrel=lastpv[0]):
                hold = {}

                def epi_scalar():
                    nonlocal rowrel
                    cpA = o.add("vector", lambda e, hp=hp, tt=tt: e.tensor_copy(
                        A[0:64, hp * S + tt * TT:hp * S + (tt + 1) * TT],
                        PA[0:64, 0:TT]), wait=[pvrel], inc=[("v", 1)])
                    cpB = o.add("vector", lambda e, hp=hp, tt=tt: e.tensor_copy(
                        A[64:128, hp * S + tt * TT:hp * S + (tt + 1) * TT],
                        PB[0:64, 0:TT]), wait=[], inc=[("v", 1)])
                    wln = [pvrel]
                    if rowrel is not None:
                        wln.append(rowrel)
                    lnA = o.add("scalar", lambda e: e.activation(
                        EROWF[64:65, 0:TT], PA[64:65, 0:TT], AF.Ln),
                        wait=wln, inc=[("s", 1)])
                    hold["exA"] = o.add("scalar", lambda e: e.activation(
                        E[0:1, E_ROW:E_ROW + TT], EROWF[64:65, 0:TT], AF.Exp,
                        scale=-1.0), wait=[lnA[0]], inc=[("s", 1)])[0]
                    lnB = o.add("scalar", lambda e: e.activation(
                        EROWF[96:97, 0:TT], PB[64:65, 0:TT], AF.Ln),
                        wait=list(wln), inc=[("s", 1)])
                    hold["exB"] = o.add("scalar", lambda e: e.activation(
                        E[32:33, E_ROW:E_ROW + TT], EROWF[96:97, 0:TT], AF.Exp,
                        scale=-1.0), wait=[lnB[0]], inc=[("s", 1)])[0]
                    ctrel[0] = [cpA[0], lnA[0]]
                    ctrel[1] = [cpB[0], lnB[0]]

                def epi_tensor():
                    nonlocal bcrel, rowrel
                    wbc = [hold["exA"], ones_init]
                    if bcrel is not None:
                        wbc.append(bcrel)
                    o.add("tensor", lambda e: e.matmul(
                        PD[0:64, 0:TT], ones_r[0:1, 0:64],
                        E[0:1, E_ROW:E_ROW + TT],
                        start=True, stop=True, tile_position=(0, 0)),
                        wait=wbc, inc=[("t", 1)])
                    bc = o.add("tensor", lambda e: e.matmul(
                        PD[64:128, 0:TT], E[32:33, E_OR:E_OR + 64],
                        E[32:33, E_ROW:E_ROW + TT],
                        start=True, stop=True, tile_position=(32, 64)),
                        wait=[hold["exB"]], inc=[("t", 1)])
                    rowrel = bc[0]
                    rb = o.add("vector", lambda e: e.tensor_copy(
                        E[:, E_XB:E_XB + TT], PD[:, 0:TT]),
                        wait=[bc[0]], inc=[("v", 1)])
                    o.add("vector", lambda e, hp=hp, tt=tt: e.tensor_mul(
                        A[0:64, hp * S + tt * TT:hp * S + (tt + 1) * TT],
                        A[0:64, hp * S + tt * TT:hp * S + (tt + 1) * TT],
                        E[0:64, E_XB:E_XB + TT]),
                        wait=[rb[0]], inc=[("v", 1)])
                    mb = o.add("vector", lambda e, hp=hp, tt=tt: e.tensor_mul(
                        A[64:128, hp * S + tt * TT:hp * S + (tt + 1) * TT],
                        A[64:128, hp * S + tt * TT:hp * S + (tt + 1) * TT],
                        E[64:128, E_XB:E_XB + TT]),
                        wait=[], inc=[("v", 1)])
                    bcrel = mb[0]
                return epi_scalar, epi_tensor
            pend = make_epi()
        if pend is not None:
            pend[0]()
            pend[1]()
        o.fence()

        pan = wo_pan if lay == 0 else wo1_pan
        bias = (lambda m: col("bo", m)) if lay == 0 else None
        tvw = [("t", kb.cnt["t"]), ("v", kb.cnt["v"])]
        gemm_stage(o, [pan[m] for m in range(DC)], DC, DC,
                   lambda k, tt: atile(A, k, tt),
                   ev_act(lambda m, tt: atile(Bb, m, tt), AF.Identity, bias_col=bias),
                   E, E_E0, DC * 128, ("g0", "g1"),
                   init_rel=(tvw, tvw))
        o.fence()
        for p in range(NT):
            emit_ln_res(lambda c, p=p: atile(Bb, c, p), p, TT, o=o,
                        wait0=[("s", kb.cnt["s"])] if p == 0 else [], bar=False)
        o.run()
        kb.barrier()

    def emit_ffn0():
        o = Ops(kb)
        emit_ln(o, xsrc, lambda c, tt: atile(C, c, tt))
        o.fence()
        prev = {"w1": (None, None), "h1": (None, None),
                "h2": (None, None), "w3": (None, None)}
        gcv_all = ("gcv", kb.cnt["gcv"])
        for p in range(NT):
            fp32pass = False

            sc_entry = ("s", kb.cnt["s"])
            vc_entry = ("v", kb.cnt["v"])
            o.add("scalar", None, wait=[vc_entry])
            w12p = [w12b[m] for m in range(FC)]
            h1p = [h1b[m] for m in range(FC)]
            h2p = [h2b[m] for m in range(FC)]
            w3p = [w3b[m] for m in range(DC)]
            s1sems = ("g0", "g1")
            s1st = ("sa0", "sa1")
            s1rel = [list(x) if x else None for x in prev["w1"]]
            pdone = [None, None]
            brel = [None] * 4
            vm = None
            for m in range(FC):
                sl = m % 2
                w = list(s1rel[sl]) if s1rel[sl] is not None else []
                if p == 0 and m == sl:
                    w.append(gcv_all)
                slot = DA[:, sl * 2048:(sl + 1) * 2048]
                pdone[sl] = o.add("gpsimd", lambda e, s=w12p[m], d=slot:
                                  e.dma_start(d.rearrange("p (k o) -> p k o",
                                                          o=128), s),
                                  wait=w, inc=[(s1sems[sl], 16)])[0]
                extra = []
                mm1 = mm2 = None
                bk1, bk2 = sl, 2 + sl
                for k in range(DC):
                    wt = [pdone[sl]]
                    if k == 0:
                        wt.append(sc_entry)
                        if brel[bk1] is not None:
                            wt.append(brel[bk1])
                    mm1 = o.add("tensor", lambda e, k=k, slot=slot, bk1=bk1,
                                m=m, p=p: e.matmul(
                                    P[bk1][:, 0:TT],
                                    slot[:, k * 128:(k + 1) * 128],
                                    atile(C, k, p),
                                    start=(k == 0), stop=(k == DC - 1)),
                                wait=wt if k == 0 else [],
                                inc=[("t", 1)] if k == DC - 1 else [])
                for k in range(DC):
                    wt = []
                    if k == 0 and brel[bk2] is not None:
                        wt.append(brel[bk2])
                    mm2 = o.add("tensor", lambda e, k=k, slot=slot, bk2=bk2,
                                m=m, p=p: e.matmul(
                                    P[bk2][:, 0:TT],
                                    slot[:, (DC + k) * 128:(DC + k + 1) * 128],
                                    atile(C, k, p),
                                    start=(k == 0), stop=(k == DC - 1)),
                                wait=wt if k == 0 else [],
                                inc=[("t", 1)] if k == DC - 1 else [])
                e1 = o.add("scalar", lambda e, m=m, bk1=bk1: e.activation(
                    A[:, m * TT:(m + 1) * TT], P[bk1][:, 0:TT], AF.Silu,
                    scale=col("s1", m)), wait=[mm1[0]], inc=[("s", 1)])
                e2 = o.add("scalar", lambda e, m=m, bk2=bk2: e.activation(
                    Bb[:, m * TT:(m + 1) * TT], P[bk2][:, 0:TT], AF.Copy,
                    scale=col("s2", m)), wait=[mm2[0]], inc=[("s", 1)])
                brel[bk1] = e1[0]
                brel[bk2] = e2[0]
                vm = o.add("vector", lambda e, m=m: e.tensor_mul(
                    A[:, m * TT:(m + 1) * TT], A[:, m * TT:(m + 1) * TT],
                    Bb[:, m * TT:(m + 1) * TT]),
                    wait=[e1[0], e2[0]], inc=[("v", 1)])
                s1rel[sl] = [e1[0], e2[0]] + extra
            prev["w1"] = (s1rel[0], s1rel[1])
            t_s1 = ("t", kb.cnt["t"])
            vmul_done = vm[0]
            o.add("scalar", None, wait=[vmul_done])
            rh1, t_h1 = gemm_stage(o, h1p, FC, FC,
                                   lambda k, tt: A[:, k * TT:(k + 1) * TT],
                                   ev_act(lambda m, tt: Bb[:, m * TT:(m + 1) * TT],
                                          AF.Gelu, scale_col=lambda m: col("sh1", m)),
                                   DA, 0, FC * 128, ("g4", "g5"), nt=1,
                                   nbank=4, bank0=0, slot_offs=(4096, 8192),
                                   init_rel=prev["h1"], first_mm_wait=[vmul_done],
                                   post_panel=None)
            prev["h1"] = rh1
            sc_h1 = ("s", kb.cnt["s"])
            o.add("scalar", None, wait=[t_h1])
            rh2, t_h2 = gemm_stage(o, h2p, FC, FC,
                                   lambda k, tt: Bb[:, k * TT:(k + 1) * TT],
                                   ev_act(lambda m, tt: A[:, m * TT:(m + 1) * TT],
                                          AF.Gelu, scale_col=lambda m: col("sh2", m)),
                                   DA, 0, FC * 128, ("g6", "g7"), nt=1,
                                   nbank=4, slot_offs=(12288, 0),
                                   init_rel=(prev["h2"][0],
                                             (s1rel[0] or []) + (s1rel[1] or []) +
                                             [t_s1]),
                                   bank0=0,
                                   first_mm_wait=[sc_h1],
                                   post_panel=None)
            prev["h2"] = (rh2[0], rh2[1])
            prev["w1"] = (prev["w1"][0] + (rh2[1] or []),
                          prev["w1"][1] + (rh2[1] or []))
            sc_h2 = ("s", kb.cnt["s"])
            o.add("scalar", None, wait=[t_h2])
            iw30 = (rh1[0] or []) + [t_h1] + ((prev["w3"][0] or []) if p > 0 else [])
            iw31 = (rh1[1] or []) + [t_h1] + ((prev["w3"][1] or []) if p > 0 else [])
            rw3, t_w3 = gemm_stage(o, w3p, FC, DC,
                                   lambda k, tt: A[:, k * TT:(k + 1) * TT],
                                   ev_act(lambda m, tt: Bb[:, m * TT:(m + 1) * TT],
                                          AF.Copy, scale_col=lambda m: col("s3", m)),
                                   DA, 0, FC * 128, ("g2", "g3"), nt=1,
                                   slot_offs=(4096, 8192),
                                   init_rel=(iw30, iw31), first_mm_wait=[sc_h2],
                                   post_panel=None)
            prev["w3"] = rw3
            prev["h1"] = ((prev["h1"][0] or []) + (rw3[0] or []),
                          (prev["h1"][1] or []) + (rw3[1] or []))
            emit_ln_res(lambda c: Bb[:, c * TT:(c + 1) * TT], p, TT, o=o,
                        wait0=[("s", kb.cnt["s"])], bar=False)
        o.run()
        kb.barrier()

    def emit_compose():
        o = Ops(kb)
        Xv = A[:, 0:4096]
        Fv = [Bb[:, 0:4096], Bb[:, 4096:8192]]
        NBP = 4
        RHSs = [C[:, i * 1028: i * 1028 + 1025] for i in range(NBP)]
        OUTs = [C[:, 4112 + i * 1028: 4112 + i * 1028 + 1025] for i in range(NBP)]
        WST = [Bb[:, 8192 + i * 1024: 8192 + (i + 1) * 1024] for i in range(2)]
        st8 = {"rhs": [None] * NBP, "out": [None] * NBP,
               "pt": [None, None], "pbank": [None] * 4, "wst": [None, None]}

        def chains(init_d, lhs_d, nstg):
            tprev = ("t", kb.cnt["t"])
            ld = o.add("scalar", lambda e: e.dma_start(Xv, init_d[:]),
                       wait=[tprev], inc=[("h1", 16)])
            relc = [ld[0]] * 32
            fload = [None, None]
            fuse = [None, None]
            for j in range(nstg):
                sl = j % 2
                wf = [fuse[sl]] if fuse[sl] else [tprev]
                eng, sem = ("gpsimd", "g1") if sl == 0 else ("sync", "h0")
                fload[sl] = o.add(eng, lambda e, j=j, sl=sl: e.dma_start(
                    Fv[sl], lhs_d[j]), wait=wf, inc=[(sem, 16)])[0]
                lastmm = None
                for c in range(32):
                    bk = c % 4
                    wt = [relc[c], fload[sl]]
                    if st8["pbank"][bk] is not None:
                        wt.append(st8["pbank"][bk])
                    mm = o.add("tensor", lambda e, c=c, bk=bk, sl=sl: e.matmul(
                        P[bk][:, 0:128], Fv[sl][:, c * 128:(c + 1) * 128],
                        Xv[:, c * 128:(c + 1) * 128], start=True, stop=True),
                        wait=wt, inc=[("t", 1)])
                    lastmm = mm[0]
                    if c % 2 == 1:
                        q2 = PS4[:, (bk - 1) * 512:(bk - 1) * 512 + 1024]
                        eng = "scalar" if (c // 2) % 2 == 0 else "vector"
                        sv = "s" if (c // 2) % 2 == 0 else "v"
                        ev = o.add(eng, lambda e, c=c, q2=q2, eng=eng: (
                            e.activation(Xv[:, (c - 1) * 128:(c + 1) * 128],
                                         q2.rearrange("p (u n) -> p u n", u=2)
                                         [:, :, 0:128], AF.Copy)
                            if eng == "scalar" else
                            e.tensor_copy(Xv[:, (c - 1) * 128:(c + 1) * 128],
                                          q2.rearrange("p (u n) -> p u n", u=2)
                                          [:, :, 0:128])),
                            wait=[mm[0]], inc=[(sv, 1)])
                        st8["pbank"][bk] = ev[0]
                        st8["pbank"][bk - 1] = ev[0]
                        relc[c] = ev[0]
                        relc[c - 1] = ev[0]
                fuse[sl] = lastmm
            return relc

        def blockpass(relc, splits, rhs_load, use_out, rhs_w=1025):
            for c in range(32):
                sl = c % NBP
                wr = [st8["rhs"][sl]] if st8["rhs"][sl] else []
                rw = rhs_load(o, c, RHSs[sl][:, 0:rhs_w], wr, sl)
                evs = []
                for i, (a, b) in enumerate(splits):
                    bk = i
                    wt = [relc[c], rw]
                    if st8["pbank"][bk] is not None:
                        wt.append(st8["pbank"][bk])
                    mm = o.add("tensor", lambda e, c=c, a=a, b=b, bk=bk, sl=sl:
                               e.matmul(P[bk][:, 0:b - a],
                                        Xv[:, c * 128:(c + 1) * 128],
                                        RHSs[sl][:, a:b], start=True, stop=True),
                               wait=wt, inc=[("t", 1)])
                    st8["rhs"][sl] = mm[0]
                    wo = [mm[0]]
                    if st8["out"][sl] is not None:
                        wo += st8["out"][sl]
                    ev = o.add("vector", lambda e, a=a, b=b, bk=bk, sl=sl:
                               e.tensor_copy(OUTs[sl][:, a:b], P[bk][:, 0:b - a]),
                               wait=wo, inc=[("v", 1)])
                    st8["pbank"][bk] = ev[0]
                    evs.append(ev[0])
                st8["out"][sl] = use_out(c, OUTs[sl], evs)

        relc = chains(ub_initb, ub_lhsb, 6)

        def ld_wexp(o, c, dst, wr, sl):
            eng = "gpsimd" if sl < 2 else "scalar"
            return o.add(eng, lambda e, c=c, dst=dst: e.dma_start(
                dst, wexpb[c * 128:(c + 1) * 128, :]), wait=wr,
                inc=[(("g4", "g5", "y3", "sc0")[sl], 16)])[0]

        def st_t1(c, OUT, evs):
            st = o.add("sync", lambda e, c=c, OUT=OUT: e.dma_start(
                t1s[c * 128:(c + 1) * 128, :], OUT), wait=evs,
                inc=[(("sd0", "sd1", "se0", "se1")[c % 4], 16)])
            return [st[0]]
        blockpass(relc, [(0, 512), (512, 1024), (1024, 1025)], ld_wexp, st_t1)
        t1_done = [(sn, kb.cnt[sn]) for sn in ("sd0", "sd1", "se0", "se1")]
        relc = chains(up_initb, up_lhsb, 4)

        def ld_t1p(o, c, dst, wr, sl):
            eng = "sync" if sl < 2 else "scalar"
            return o.add(eng, lambda e, c=c, dst=dst: e.dma_start(
                dst, t1s.rearrange("(q cc r) m -> q r cc m",
                                   cc=32, r=4)[:, :, c, :]),
                wait=wr + t1_done, inc=[(("y1", "y2", "y3", "sc0")[sl], 16)])[0]

        def use_wpp(c, OUT, evs):
            bv = o.add("vector", lambda e, c=c, OUT=OUT: e.tensor_add(
                col("bppp", c), OUT[:, 1024:1025], col("upb", c)),
                wait=[evs[2]], inc=[("v", 1)])
            wsl = c % 2
            last_tm = None
            tvs = []
            for k in range(DC):
                psl = k % 2
                PTk = PTS[psl]
                wt = [evs[k // 4], ("gw", GW_ALL)]
                if st8["pt"][psl] is not None:
                    wt.append(st8["pt"][psl])
                tm = o.add("tensor", lambda e, k=k, OUT=OUT, PTk=PTk: e.transpose(
                    PTk, OUT[:, k * 128:(k + 1) * 128], E[:, E_ID:E_ID + 128]),
                    wait=wt, inc=[("t", 1)])
                wv = [tm[0]]
                if k < 2 and st8["wst"][wsl] is not None:
                    wv.append(st8["wst"][wsl])
                tv = o.add("scalar", lambda e, k=k, psl=psl, PTk=PTk, wsl=wsl:
                           e.activation(WST[wsl][:, k * 128:(k + 1) * 128],
                                        PTk, AF.Copy), wait=wv, inc=[("s", 1)])
                st8["pt"][psl] = tv[0]
                tvs.append(tv[0])
                last_tm = tm[0]
            st = o.add("gpsimd", lambda e, c=c, wsl=wsl: e.dma_start(
                wup_pan[c], WST[wsl].rearrange("p (k o) -> p k o", o=128)),
                wait=tvs, inc=[("g8" if wsl == 0 else "g9", 16)])
            st8["wst"][wsl] = st[0]
            return [last_tm, bv[0]]
        blockpass(relc, [(0, 512), (512, 1024), (1024, 1025)], ld_t1p, use_wpp)
        relc = chains(dp_initb, dp_lhsb, 4)

        def ld_wcon(o, c, dst, wr, sl):
            eng = "gpsimd" if sl < 2 else "scalar"
            return o.add(eng, lambda e, c=c, dst=dst: e.dma_start(
                dst[:, 0:1024], wconb[c * 128:(c + 1) * 128, :]), wait=wr,
                inc=[(("g4", "g5", "y3", "sc0")[sl], 16)])[0]

        def st_u1(c, OUT, evs):
            st = o.add("sync", lambda e, c=c, OUT=OUT: e.dma_start(
                u1s[c * 128:(c + 1) * 128, :], OUT[:, 0:1024]),
                wait=evs, inc=[(("sd0", "sd1", "se0", "se1")[c % 4], 16)])
            return [st[0]]
        blockpass(relc, [(0, 512), (512, 1024)], ld_wcon, st_u1, rhs_w=1024)
        u1_done = [(sn, kb.cnt[sn]) for sn in ("sd0", "sd1", "se0", "se1")]
        relc = chains(db_initb, db_lhsb, 6)

        def ld_u1p(o, c, dst, wr, sl):
            eng = "sync" if sl < 2 else "scalar"
            return o.add(eng, lambda e, c=c, dst=dst: e.dma_start(
                dst[:, 0:1024], u1s.rearrange("(q cc r) m -> q r cc m",
                                              cc=32, r=4)[:, :, c, :]),
                wait=wr + u1_done, inc=[(("y1", "y2", "y3", "sc0")[sl], 16)])[0]

        def st_wcon(c, OUT, evs):
            eng = "scalar" if c % 2 == 0 else "gpsimd"
            sem = ("sa0", "g8", "sa1", "g9")[c % 4]
            st = None
            for m in range(DC):
                st = o.add(eng, lambda e, c=c, m=m, OUT=OUT: e.dma_start(
                    wcon_pan.rearrange("m (cc r) q d -> q r cc m d",
                                       r=4)[:, :, c, m, :],
                    OUT[:, m * 128:(m + 1) * 128]),
                    wait=evs, inc=[(sem, 16)])
            return [st[0]]
        blockpass(relc, [(0, 512), (512, 1024)], ld_u1p, st_wcon, rhs_w=1024)
        o.run()
        kb.barrier()

    def emit_ffn1():
        o = Ops(kb)
        emit_ln(o, xsrc, lambda c, tt: atile(C, c, tt))
        o.fence()
        S1o = 4096
        prev_up = (None, None)
        prev_dn = (None, None)
        for p in range(NT):
            sc_entry = ("s", kb.cnt["s"])
            vc_entry = ("v", kb.cnt["v"])
            o.add("scalar", None, wait=[vc_entry])
            rup, t_up = gemm_stage(o, [wup_pan[m] for m in range(FC)], DC, FC,
                                   lambda k, tt, p=p: atile(C, k, p),
                                   ev_act(lambda m, tt: A[:, m * TT:(m + 1) * TT],
                                          AF.Gelu, bias_col=lambda m: col("bppp", m)),
                                   DA, 0, DC * 128, ("g0", "g1", "g4", "g5"), nt=1,
                                   nbank=4,
                                   init_rel=prev_up, first_mm_wait=[sc_entry])
            prev_up = rup
            sc_up = ("s", kb.cnt["s"])
            rdn, t_dn = gemm_stage(o, [wcon_pan[m] for m in range(DC)], FC, DC,
                                   lambda k, tt: A[:, k * TT:(k + 1) * TT],
                                   ev_act(lambda m, tt: Bb[:, m * TT:(m + 1) * TT],
                                          AF.Identity, bias_col=lambda m: col("bout", m)),
                                   DA, S1o, FC * 128, ("g2", "g3", "g6"), nt=1,
                                   nbank=4, bank0=0,
                                   init_rel=prev_dn, first_mm_wait=[sc_up])
            prev_dn = rdn
            emit_ln_res(lambda c: Bb[:, c * TT:(c + 1) * TT], p, TT, o=o,
                        wait0=[("s", kb.cnt["s"])], bar=False)
        o.run()
        kb.barrier()

    emit_attention(0)
    emit_ffn0()
    emit_attention(1)
    emit_compose()
    emit_ffn1()

    o = Ops(kb)
    o.add("sync", lambda e: e.dma_start(
        yT.rearrange("(c p) s -> p c s", p=128),
        X[:].rearrange("p (c s) -> p c s", c=DC)), inc=[("yo", 16)])
    o.add("sync", None, wait=[("yo", kb.cnt["yo"])])
    o.run()
    return nc



def _pan(wT, KC, MC):
    return np.ascontiguousarray(
        wT.reshape(KC, 128, MC, 128).transpose(2, 1, 0, 3)).astype(np.float32)


def _cols(v):
    v = np.asarray(v, dtype=np.float32)
    return np.ascontiguousarray(v.reshape(-1, 128).T)


def _factor_blocks(W_s, stage, space, transpose):
    f = np.arange(FH)
    st = 1 << stage
    if space == "orig":
        pos = f
        of = f
    else:
        pos = np.arange(FH)
        of = _SIG[pos]
    kk = ((of >> (stage + 1)) << stage) | (of & (st - 1))
    bb = (of >> stage) & 1
    par = pos ^ (st if space == "orig" else _PSTRIDE[stage])
    blocks = np.zeros((32, 128, 128), dtype=np.float32)
    bidx, r = pos >> 7, pos & 127
    pr = par & 127
    assert np.all((par >> 7) == bidx)
    blocks[bidx, r, r] = W_s[kk, bb, bb]
    blocks[bidx, r, pr] = W_s[kk, bb, 1 - bb]
    if transpose:
        blocks = blocks.transpose(0, 2, 1)
    return np.ascontiguousarray(blocks.transpose(1, 0, 2))


def _rope_tables_L0(S):
    inv = 1.0 / 10000.0 ** (np.arange(0, HD, 2, dtype=np.float64) / HD)
    pos = np.arange(S, dtype=np.float64)
    ang = pos[:, None] * inv[None, :]
    sin2 = np.concatenate([np.sin(ang), np.sin(ang)], 1)
    cos2 = np.concatenate([np.cos(ang), np.cos(ang)], 1)
    return (np.tile(cos2.T, (2, 1)).astype(np.float32),
            np.tile(sin2.T, (2, 1)).astype(np.float32))


def _rope_tables_L1(S):
    half = HD // 2
    inv = 1.0 / 10000.0 ** (np.arange(half, dtype=np.float64) / half)
    t = np.arange(S, dtype=np.float64)
    fr = t[:, None] * inv[None, :]
    c = np.cos(fr).T
    s = np.sin(fr).T
    return (np.tile(np.concatenate([c, c], 0), (2, 1)).astype(np.float32),
            np.tile(np.concatenate([s, s], 0), (2, 1)).astype(np.float32))


def _pi_L0():
    Pi = np.zeros((64, 64), dtype=np.float32)
    j = np.arange(32)
    Pi[j, 2 * j + 1] = -1.0
    Pi[32 + j, 2 * j] = 1.0
    blk = np.zeros((128, 128), dtype=np.float32)
    blk[0:64, 0:64] = Pi
    blk[64:128, 64:128] = Pi
    return np.ascontiguousarray(blk.T)


def _pi_L1():
    Pi = np.zeros((64, 64), dtype=np.float32)
    j = np.arange(32)
    Pi[j, j + 32] = -1.0
    Pi[j + 32, j] = 1.0
    blk = np.zeros((128, 128), dtype=np.float32)
    blk[0:64, 0:64] = Pi
    blk[64:128, 64:128] = Pi
    return np.ascontiguousarray(blk.T)


def host_inputs(g, S):
    sh = {}
    sh["wq_pan"] = _pan(g["L0_attn_Wq"].T, DC, DC)
    sh["wk_pan"] = _pan(g["L0_attn_Wk"].T, DC, 2)
    sh["wvT"] = np.ascontiguousarray(g["L0_attn_Wv"].T)
    sh["wo_pan"] = _pan(g["L0_attn_Wo"].T, DC, DC)
    sh["bq_c"] = _cols(g["L0_attn_bq"])
    sh["bk_c"] = _cols(g["L0_attn_bk"])
    sh["bv_r"] = np.ascontiguousarray(g["L0_attn_bv"].reshape(1, 256))
    sh["bo_c"] = _cols(g["L0_attn_bo"])
    sh["w12_pan"] = np.ascontiguousarray(np.concatenate(
        [_pan(g["L0_ffn_w1_w"].T, DC, FC), _pan(g["L0_ffn_w2_w"].T, DC, FC)],
        axis=2))
    sh["h1_pan"] = _pan(g["L0_ffn_h1_w"].T, FC, FC)
    sh["h2_pan"] = _pan(g["L0_ffn_h2_w"].T, FC, FC)
    sh["w3_pan"] = _pan(g["L0_ffn_w3_w"].T, FC, DC)
    sh["s1_c"] = _cols(g["L0_ffn_w1_s"])
    sh["s2_c"] = _cols(g["L0_ffn_w2_s"])
    sh["sh1_c"] = _cols(g["L0_ffn_h1_s"])
    sh["sh2_c"] = _cols(g["L0_ffn_h2_s"])
    sh["s3_c"] = _cols(g["L0_ffn_w3_s"])
    wqkvT = g["L1_attn_w_qkv"].T
    sh["qk1_pan"] = _pan(wqkvT[:, 0:2048], DC, 16)
    sh["wv1T"] = np.ascontiguousarray(wqkvT[:, 2048:3072])
    sh["wo1_pan"] = _pan(g["L1_attn_w_out"].T, DC, DC)
    sh["wexpTaug"] = np.ascontiguousarray(
        np.concatenate([g["L1_ffn_W_exp"].T,
                        g["L1_ffn_b_exp"].reshape(-1, 1)], axis=1)).astype(np.float32)
    sh["wconP"] = np.ascontiguousarray(g["L1_ffn_W_con"][_SIG, :])
    sh["upb_c"] = _cols(g["L1_ffn_up_bias"][_SIG])
    sh["bcon_c"] = _cols(g["L1_ffn_b_con"])
    sh["dbias_c"] = _cols(g["L1_ffn_down_bias"])
    uw = g["L1_ffn_up_weights"].reshape(12, FH // 2, 2, 2)
    dw = g["L1_ffn_down_weights"].reshape(12, FH // 2, 2, 2)
    sh["ub_init"] = _factor_blocks(uw[6], 6, "orig", True)
    sh["ub_lhs"] = np.stack([_factor_blocks(uw[j], j, "orig", False)
                             for j in [5, 4, 3, 2, 1, 0]])
    sh["up_init"] = _factor_blocks(uw[11], 11, "perm", True)
    sh["up_lhs"] = np.stack([_factor_blocks(uw[j], j, "perm", False)
                             for j in [10, 9, 8, 7]])
    sh["db_init"] = _factor_blocks(dw[0], 0, "orig", False)
    sh["db_lhs"] = np.stack([_factor_blocks(dw[j], j, "orig", True)
                             for j in [1, 2, 3, 4, 5, 6]])
    sh["dp_init"] = _factor_blocks(dw[7], 7, "perm", False)
    sh["dp_lhs"] = np.stack([_factor_blocks(dw[j], j, "perm", True)
                             for j in [8, 9, 10, 11]])
    sh["cos0"], sh["sin0"] = _rope_tables_L0(S)
    sh["cos1"], sh["sin1"] = _rope_tables_L1(S)
    sh["pi0T"] = _pi_L0()
    sh["pi1T"] = _pi_L1()
    sh["identw"] = np.eye(128, dtype=np.float32)
    TT = min(512, S)
    PJ = TT // 128
    W = TT + 128 * (PJ - 1)
    sp = np.arange(128)[:, None]
    u = np.arange(W)[None, :]
    sh["cmaskx"] = (sp <= u - 128 * (PJ - 1)).astype(np.float32)
    return sh


_CACHE = {}


def kernel(**inputs):
    S = int(inputs["L0_attn_hidden_states"].shape[1])
    if S not in _CACHE:
        _CACHE[S] = build_program(S)
    nc = _CACHE[S]
    g = {k: np.ascontiguousarray(np.asarray(v), dtype=np.float32)
         for k, v in inputs.items()}
    sh = host_inputs(g, S)
    x = g["L0_attn_hidden_states"]
    Bn = x.shape[0]
    in_maps = []
    for b in range(Bn):
        m = dict(sh)
        m["xT"] = np.ascontiguousarray(x[b].T)
        in_maps.append(m)
    res = run_bass_kernel_spmd(nc, in_maps, core_ids=list(range(Bn)))
    out = np.stack([np.asarray(r["yT"]).T for r in res.results], axis=0)
    return np.ascontiguousarray(out.astype(np.float32))


# version 27 (final)
# speedup vs baseline: 1.0276x; 1.0268x over previous
import numpy as np

import concourse.bass as bass
import concourse.mybir as mybir
from concourse.bass_utils import run_bass_kernel_spmd

F32 = mybir.dt.float32
BF16 = mybir.dt.bfloat16
AF = mybir.ActivationFunctionType
ALU = mybir.AluOpType

D = 1024
DC = 8
FH = 4096
FC = 32
HD = 64
NH = 16
EPS = 1e-5
NEGB = -20.0


def _sigma_vec():
    f = np.arange(FH)
    b = [(f >> i) & 1 for i in range(12)]
    return (b[0] | (b[1] << 1)
            | (b[7] << 2) | (b[8] << 3) | (b[9] << 4) | (b[10] << 5) | (b[11] << 6)
            | (b[2] << 7) | (b[3] << 8) | (b[4] << 9) | (b[5] << 10) | (b[6] << 11))


_SIG = _sigma_vec()
_PSTRIDE = {s: 1 << (2 + s - 7) for s in range(7, 12)}


class KB:
    def __init__(self, nc):
        self.nc = nc
        self.sems = {}
        self.cnt = {}
        self.hw = {}

    def sem(self, name):
        self.sems[name] = self.nc.alloc_semaphore(name)
        self.cnt[name] = 0

    def will_inc(self, name, n=1):
        self.cnt[name] += n
        return self.cnt[name]

    def run(self, lists):
        with self.nc.Block() as blk:
            for ename, ops in lists.items():
                if not ops:
                    continue

                def make(ops, ename):
                    def body(eng):
                        for waits, fn, incs in ops:
                            for (sname, val) in waits:
                                key = (ename, sname)
                                if self.hw.get(key, -1) >= val:
                                    continue
                                self.hw[key] = val
                                eng.wait_ge(self.sems[sname], val)
                            instr = fn(eng) if fn is not None else None
                            for (sname, n) in incs:
                                instr.then_inc(self.sems[sname], n)
                    return body

                getattr(blk, ename)(make(ops, ename))

    def barrier(self):
        n = self.will_inc("bar", 5)
        flush = {"gpsimd": ("g0", "g1", "g2", "g3", "g4", "g5", "g6", "g7", "gcv", "gw", "g8", "g9"),
                 "sync": ("y0", "y1", "y2", "y3", "yo", "sa0", "sa1", "sb0", "sb1", "sc0", "sc1", "sd0", "sd1", "se0", "se1", "h0", "h1")}
        ops = {}
        for ename in ("tensor", "scalar", "vector", "gpsimd", "sync"):
            def mk(ename):
                def body(eng):
                    for sn in flush.get(ename, ()):
                        val = self.cnt[sn]
                        key = (ename, sn)
                        if val and self.hw.get(key, -1) < val:
                            self.hw[key] = val
                            eng.wait_ge(self.sems[sn], val)
                    eng.sem_inc(self.sems["bar"], 1)
                    eng.wait_ge(self.sems["bar"], n)
                return body
            ops[ename] = mk(ename)
        with self.nc.Block() as blk:
            for ename, body in ops.items():
                getattr(blk, ename)(body)


class Ops:
    def __init__(self, kb):
        self.kb = kb
        self.lists = {k: [] for k in ("tensor", "scalar", "vector", "gpsimd", "sync")}

    def fence(self, engines=("tensor", "scalar", "vector", "gpsimd", "sync"),
              sems=("t", "s", "v")):
        w = [(sn, self.kb.cnt[sn]) for sn in sems if self.kb.cnt[sn]]
        for eng in engines:
            self.add(eng, None, wait=list(w))

    def add(self, eng, fn, wait=(), inc=()):
        out = []
        inc2 = []
        for (s, n) in inc:
            inc2.append((s, n))
            out.append((s, self.kb.will_inc(s, n)))
        self.lists[eng].append((list(wait), fn, inc2))
        return out

    def run(self):
        self.kb.run(self.lists)


def build_program(S):
    TT = min(512, S)
    NT = S // TT
    SC = S // 128
    PJ = TT // 128

    nc = bass.Bass("TRN2", target_bir_lowering=False)
    kb = KB(nc)

    def din(name, shape):
        return nc.dram_tensor(name, list(shape), F32, kind="ExternalInput")

    xT = din("xT", [D, S])
    yT = nc.dram_tensor("yT", [D, S], F32, kind="ExternalOutput")

    wq_pan = din("wq_pan", [DC, 128, DC, 128])
    wk_pan = din("wk_pan", [2, 128, DC, 128])
    wvT = din("wvT", [D, 256])
    wo_pan = din("wo_pan", [DC, 128, DC, 128])
    bq_c = din("bq_c", [128, DC])
    bk_c = din("bk_c", [128, 2])
    bv_r = din("bv_r", [1, 256])
    bo_c = din("bo_c", [128, DC])

    w12_pan = din("w12_pan", [FC, 128, 2 * DC, 128])
    h1_pan = din("h1_pan", [FC, 128, FC, 128])
    h2_pan = din("h2_pan", [FC, 128, FC, 128])
    w3_pan = din("w3_pan", [DC, 128, FC, 128])
    s1_c = din("s1_c", [128, FC])
    s2_c = din("s2_c", [128, FC])
    sh1_c = din("sh1_c", [128, FC])
    sh2_c = din("sh2_c", [128, FC])
    s3_c = din("s3_c", [128, DC])

    qk1_pan = din("qk1_pan", [16, 128, DC, 128])
    wv1T = din("wv1T", [D, D])
    wo1_pan = din("wo1_pan", [DC, 128, DC, 128])

    wexpTaug = din("wexpTaug", [FH, D + 1])
    wconP = din("wconP", [FH, D])
    upb_c = din("upb_c", [128, FC])
    bcon_c = din("bcon_c", [128, DC])
    dbias_c = din("dbias_c", [128, DC])
    ub_init = din("ub_init", [128, 32, 128])
    ub_lhs = din("ub_lhs", [6, 128, 32, 128])
    up_init = din("up_init", [128, 32, 128])
    up_lhs = din("up_lhs", [4, 128, 32, 128])
    db_init = din("db_init", [128, 32, 128])
    db_lhs = din("db_lhs", [6, 128, 32, 128])
    dp_init = din("dp_init", [128, 32, 128])
    dp_lhs = din("dp_lhs", [4, 128, 32, 128])

    cos0 = din("cos0", [128, S])
    sin0 = din("sin0", [128, S])
    cos1 = din("cos1", [128, S])
    sin1 = din("sin1", [128, S])
    pi0T = din("pi0T", [128, 128])
    pi1T = din("pi1T", [128, 128])
    identw = din("identw", [128, 128])
    cmaskx = din("cmaskx", [128, TT + 128 * (PJ - 1)])

    t1s = nc.dram_tensor("t1s", [FH, D + 1], BF16, kind="Internal")
    u1s = nc.dram_tensor("u1s", [FH, D], BF16, kind="Internal")
    wup_pan = nc.dram_tensor("wup_pan", [FC, 128, DC, 128], BF16, kind="Internal")
    wcon_pan = nc.dram_tensor("wcon_pan", [DC, 128, FC, 128], BF16, kind="Internal")
    w12b = nc.dram_tensor("w12b", [FC, 128, 2 * DC, 128], BF16, kind="Internal")
    h1b = nc.dram_tensor("h1b", [FC, 128, FC, 128], BF16, kind="Internal")
    h2b = nc.dram_tensor("h2b", [FC, 128, FC, 128], BF16, kind="Internal")
    w3b = nc.dram_tensor("w3b", [DC, 128, FC, 128], BF16, kind="Internal")
    ub_initb = nc.dram_tensor("ub_initb", [128, 32, 128], BF16, kind="Internal")
    ub_lhsb = nc.dram_tensor("ub_lhsb", [6, 128, 32, 128], BF16, kind="Internal")
    up_initb = nc.dram_tensor("up_initb", [128, 32, 128], BF16, kind="Internal")
    up_lhsb = nc.dram_tensor("up_lhsb", [4, 128, 32, 128], BF16, kind="Internal")
    db_initb = nc.dram_tensor("db_initb", [128, 32, 128], BF16, kind="Internal")
    db_lhsb = nc.dram_tensor("db_lhsb", [6, 128, 32, 128], BF16, kind="Internal")
    dp_initb = nc.dram_tensor("dp_initb", [128, 32, 128], BF16, kind="Internal")
    dp_lhsb = nc.dram_tensor("dp_lhsb", [4, 128, 32, 128], BF16, kind="Internal")
    wexpb = nc.dram_tensor("wexpb", [FH, D + 1], BF16, kind="Internal")
    wconb = nc.dram_tensor("wconb", [FH, D], BF16, kind="Internal")

    AS = max(DC * S, FC * TT, 12560)
    X = nc.alloc_sbuf_tensor("X", [128, DC * S], F32)
    A = nc.alloc_sbuf_tensor("A", [128, AS], BF16)
    Bb = nc.alloc_sbuf_tensor("Bb", [128, AS], BF16)
    C = nc.alloc_sbuf_tensor("C", [128, max(DC * S, 8500)], BF16)
    DA = nc.alloc_sbuf_tensor("DA", [128, max(SC * 16 * 65, 4 * FC * 128)], BF16)
    E = nc.alloc_sbuf_tensor("E", [128, 7744], BF16)

    E_E0 = 0
    E_CM = 2048
    E_XB = 2944
    E_NT = 3968
    E_ROW = 4992
    E_COLS = 6528
    E_BVB = 6980
    E_ID = 7236
    E_PI = 7364
    E_ONE = 7492
    E_OR = 7493
    E_EPS = 7622
    E_NEGB = 7624

    def Ef32(off, n):
        return E[:, off:off + 2 * n].bitcast(F32)

    COLS = {}
    _co = E_COLS
    for nm, w in [("bq", DC), ("bk", 2), ("bo", DC), ("s1", FC), ("s2", FC),
                  ("sh1", FC), ("sh2", FC), ("s3", DC), ("bppp", FC),
                  ("bout", DC), ("upb", FC)]:
        COLS[nm] = _co
        _co += 2 * w
    assert _co <= E_BVB, _co

    def col(nm, j, w=1):
        off = COLS[nm]
        return E[:, off + 2 * j: off + 2 * (j + w)].bitcast(F32)

    def lowp(f):
        def g(e):
            with nc.allow_low_precision(reason="intended bf16 operand rounding"):
                return f(e)
        return g

    PS4 = nc.alloc_psum_tensor("PS4", [128, 2048], F32)
    PA = nc.alloc_psum_tensor("PA", [128, 512], F32)
    PB = nc.alloc_psum_tensor("PB", [128, 512], F32)
    PC = nc.alloc_psum_tensor("PC", [128, 512], F32)
    PD = nc.alloc_psum_tensor("PD", [128, 512], F32)
    P = [PS4[:, 0:512], PS4[:, 512:1024], PS4[:, 1024:1536], PS4[:, 1536:2048],
         PA[:], PB[:]]
    PTS = [PD[:, 0:64].bitcast(BF16), PC[:, 0:64].bitcast(BF16)]

    for s in ("t", "s", "v", "g0", "g1", "g2", "g3", "gw", "y0", "y1", "yo", "bar", "g4", "g5", "g6", "g7", "gcv", "y2", "y3", "sa0", "sa1", "sb0", "sb1", "sc0", "sc1", "sd0", "sd1", "se0", "se1", "g8", "g9", "h0", "h1"):
        kb.sem(s)

    ones_c = E[:, E_ONE:E_ONE + 1]
    ones_r = E[0:1, E_OR:E_OR + 128]
    xb = E[:, E_XB:E_XB + 512]
    rowE = E[0:1, E_ROW:E_ROW + 512]
    rowT = E[0:1, E_ROW + 512:E_ROW + 1024]
    rowD = E[0:1, E_ROW + 1024:E_ROW + 1536]
    ntmp = Ef32(E_NT, 512)

    def atile(arena, c, tt):
        return arena[:, c * S + tt * TT:c * S + (tt + 1) * TT]

    def xsrc(c, tt):
        return X[:, c * S + tt * TT:c * S + (tt + 1) * TT]

    o = Ops(kb)
    o.add("sync", lambda e: e.dma_start(
        X[:].rearrange("p (c s) -> p c s", c=DC),
        xT.rearrange("(c p) s -> p c s", p=128)), inc=[("y0", 16)])
    o.add("gpsimd", lambda e: e.dma_start(E[:, E_ID:E_ID + 128], identw[:]),
          inc=[("gw", 16)])
    o.add("gpsimd", lambda e: e.dma_start(
        E[:, E_CM:E_CM + TT + 128 * (PJ - 1)], cmaskx[:]), inc=[("gw", 16)])
    GW_ALL = kb.cnt["gw"]
    for nm, src in [("bq", bq_c), ("bk", bk_c), ("bo", bo_c), ("s1", s1_c),
                    ("s2", s2_c), ("sh1", sh1_c), ("sh2", sh2_c), ("s3", s3_c),
                    ("upb", upb_c)]:
        w = src.shape[1]
        o.add("sync", lambda e, s=src, n=nm, w=w: e.dma_start(col(n, 0, w), s[:]),
              inc=[("y0", 16)])
    o.add("sync", lambda e: e.dma_start(Ef32(E_NT, DC), bcon_c[:]), inc=[("y0", 16)])
    o.add("sync", lambda e: e.dma_start(Ef32(E_NT + 2 * DC, DC), dbias_c[:]),
          inc=[("y0", 16)])
    y0c = kb.cnt["y0"]
    o.add("vector", lambda e: e.tensor_add(col("bout", 0, DC), Ef32(E_NT, DC),
                                           Ef32(E_NT + 2 * DC, DC)),
          wait=[("y0", y0c)], inc=[("v", 1)])
    o.add("vector", lambda e: e.memset(ones_c, 1.0), inc=[("v", 1)])
    o.add("vector", lambda e: e.memset(ones_r, 1.0), inc=[("v", 1)])
    o.add("vector", lambda e: e.memset(Ef32(E_EPS, 1), EPS), inc=[("v", 1)])
    o.add("vector", lambda e: e.memset(Ef32(E_NEGB, 1), NEGB), inc=[("v", 1)])
    o.run()
    kb.barrier()

    o = Ops(kb)
    o.add("sync", lambda e: e.dma_start(Ef32(E_NT, 256)[0:1, :], bv_r[:]),
          inc=[("y0", 16)])
    bvw = ("y0", kb.cnt["y0"])
    cp = o.add("scalar", lambda e: e.activation(
        E[0:1, E_XB:E_XB + 256], Ef32(E_NT, 256)[0:1, :], AF.Copy),
        wait=[bvw], inc=[("s", 1)])
    mm = o.add("tensor", lambda e: e.matmul(
        P[0][:, 0:256], ones_r, E[0:1, E_XB:E_XB + 256], start=True, stop=True),
        wait=[cp[0]], inc=[("t", 1)])
    o.add("vector", lambda e: e.tensor_copy(E[:, E_BVB:E_BVB + 256], P[0][:, 0:256]),
          wait=[mm[0]], inc=[("v", 1)])
    o.run()
    kb.barrier()

    LNST = {"relA": [None, None], "relB": [None, None]}

    def ln_core(o, src_tile, TTw, dst_store, wait0=(), par=0):
        xbuf = [E[:, E_XB:E_XB + 512], E[:, E_XB + 512:E_XB + 1024]]
        sbuf = [E[:, E_E0:E_E0 + 512], E[:, E_E0 + 512:E_E0 + 1024]]
        m2buf = E[0:1, E_E0 + 1024 + 512 * par:E_E0 + 1536 + 512 * par]
        q0, q1 = P[2 * par], P[2 * par + 1]
        relA = LNST["relA"]
        relB = LNST["relB"]
        for c in range(DC):
            pb = c % 2
            wcp = list(wait0) if c == 0 else []
            if relA[pb] is not None:
                wcp.append(relA[pb])
            cp = o.add("scalar", lambda e, c=c, pb=pb: e.activation(
                xbuf[pb][:, 0:TTw], src_tile(c), AF.Copy),
                wait=wcp, inc=[("s", 1)])
            wsq = [cp[0]]
            if relB[pb] is not None:
                wsq.append(relB[pb])
            sq = o.add("scalar", lambda e, pb=pb: e.activation(
                sbuf[pb][:, 0:TTw], xbuf[pb][:, 0:TTw], AF.Square),
                wait=wsq, inc=[("s", 1)])
            mA = o.add("tensor", lambda e, c=c, pb=pb, q0=q0: e.matmul(
                q0[0:1, 0:TTw], ones_c, xbuf[pb][:, 0:TTw],
                start=(c == 0), stop=(c == DC - 1)),
                wait=[cp[0]] + (list(wait0) if c == 0 else []), inc=[("t", 1)])
            relA[pb] = mA[0]
            mB = o.add("tensor", lambda e, c=c, pb=pb, q1=q1: e.matmul(
                q1[0:1, 0:TTw], ones_c, sbuf[pb][:, 0:TTw],
                start=(c == 0), stop=(c == DC - 1)),
                wait=[sq[0]] + ([("v", kb.cnt["v"])] if c == 0 else []),
                inc=[("t", 1)])
            relB[pb] = mB[0]
        tdone = ("t", kb.cnt["t"])
        rowEp = E[0:1, E_ROW + 512 * par:E_ROW + 512 * par + 512]
        rowDp = E[0:1, E_ROW + 1024:E_ROW + 1536]
        me = o.add("scalar", lambda e, q0=q0: e.activation(
            rowEp[:, 0:TTw], q0[0:1, 0:TTw], AF.Copy, scale=1.0 / D),
            wait=[tdone], inc=[("s", 1)])
        m2s = o.add("scalar", lambda e: e.activation(
            m2buf[:, 0:TTw], rowEp[:, 0:TTw], AF.Square),
            wait=[me[0]], inc=[("s", 1)])
        va = o.add("vector", lambda e, q1=q1: e.scalar_tensor_tensor(
            PC[0:1, 0:TTw], q1[0:1, 0:TTw], 1.0 / D, m2buf[:, 0:TTw],
            ALU.mult, ALU.subtract), wait=[tdone, m2s[0]], inc=[("v", 1)])
        sd = o.add("scalar", lambda e: e.activation(
            PD[0:1, 0:TTw], PC[0:1, 0:TTw], AF.Sqrt,
            bias=Ef32(E_EPS, 1)[0:1, :]),
            wait=[va[0]], inc=[("s", 1)])
        rs = o.add("vector", lowp(lambda e: e.reciprocal(
            rowDp[:, 0:TTw], PD[0:1, 0:TTw])), wait=[sd[0]], inc=[("v", 1)])
        bm = o.add("tensor", lambda e: e.matmul(
            PA[:, 0:TTw], ones_r, rowEp[:, 0:TTw], start=True, stop=True),
            wait=[me[0], ("v", kb.cnt["v"])], inc=[("t", 1)])
        br = o.add("tensor", lambda e: e.matmul(
            PB[:, 0:TTw], ones_r, rowDp[:, 0:TTw], start=True, stop=True),
            wait=[rs[0]], inc=[("t", 1)])
        last_v = None
        for c in range(DC):
            w = [bm[0], br[0]] + ([last_v] if last_v else [])
            sb = o.add("vector", lambda e, c=c: e.tensor_sub(
                ntmp[:, 0:TTw], src_tile(c), PA[:, 0:TTw]),
                wait=w, inc=[("v", 1)])
            last_v = dst_store(c, sb[0])
        return me[0], last_v

    def emit_ln(o, src, dst, wait0=()):
        for tt in range(NT):
            def mk_store(tt):
                def store(c, sb):
                    r = o.add("vector", lambda e, c=c: e.tensor_mul(
                        dst(c, tt), ntmp[:, 0:TT], PB[:, 0:TT]),
                        wait=[sb], inc=[("v", 1)])
                    return r[0]
                return store
            ln_core(o, lambda c, tt=tt: src(c, tt), TT, mk_store(tt),
                    par=tt % 2, wait0=wait0 if tt == 0 else ())

    def emit_ln_res(src_of_c, p, TTw, o=None, wait0=(), bar=True):
        own = o is None
        if own:
            o = Ops(kb)

        def store(c, sb):
            mu = o.add("vector", lambda e: e.tensor_mul(
                ntmp[:, 0:TTw], ntmp[:, 0:TTw], PB[:, 0:TTw]),
                wait=[sb], inc=[("v", 1)])
            r = o.add("vector", lambda e, c=c: e.tensor_add(
                X[:, c * S + p * TTw:c * S + (p + 1) * TTw],
                X[:, c * S + p * TTw:c * S + (p + 1) * TTw], ntmp[:, 0:TTw]),
                wait=[mu[0]], inc=[("v", 1)])
            return r[0]
        ln_core(o, src_of_c, TTw, store, wait0=wait0, par=p % 2)
        if own:
            o.run()
        if bar:
            kb.barrier()

    def gemm_stage(o, pans, KC, MC, rhs, evac, arena, slot_off, slot_sz,
                   slot_sems, nt=None, nbank=2, bank0=0, init_rel=None,
                   post_panel=None, first_mm_wait=(), dma_eng="gpsimd",
                   slot_offs=None):
        nt = NT if nt is None else nt
        ns = len(slot_sems)
        if slot_offs is None:
            slot_offs = [slot_off + i * slot_sz for i in range(ns)]
        if init_rel is None:
            init_rel = (None,) * ns
        rel = []
        for x in init_rel:
            if x is None:
                rel.append(None)
            elif isinstance(x, list):
                rel.append(list(x))
            else:
                rel.append([x])
        while len(rel) < ns:
            rel.append(None)
        brel = [None] * nbank
        pdone = [None] * ns
        for m in range(MC):
            sl = m % ns
            sem = slot_sems[sl]
            w = list(rel[sl]) if rel[sl] is not None else []
            slot = arena[:, slot_offs[sl]: slot_offs[sl] + slot_sz]
            pdone[sl] = o.add(dma_eng, lambda e, s=pans[m], d=slot: e.dma_start(
                d.rearrange("p (k o) -> p k o", o=128), s),
                wait=w, inc=[(sem, 16)])[0]
            extra_rel = []
            if post_panel is not None:
                extra_rel = post_panel(o, m, slot, pdone[sl])
            for tt in range(nt):
                bi = (m * nt + tt) % nbank
                bk = bank0 + bi
                wt = [pdone[sl]]
                if m == 0 and tt == 0:
                    wt += list(first_mm_wait)
                if brel[bi] is not None:
                    wt.append(brel[bi])
                mm = None
                for k in range(KC):
                    lhs = slot[:, k * 128:(k + 1) * 128]
                    mm = o.add("tensor", lambda e, lhs=lhs, k=k, tt=tt, bk=bk:
                               e.matmul(P[bk][:, 0:TT], lhs, rhs(k, tt),
                                        start=(k == 0), stop=(k == KC - 1)),
                               wait=wt if k == 0 else [],
                               inc=[("t", 1)] if k == KC - 1 else [])
                ev = evac(o, m, tt, P[bk][:, 0:TT], mm[0])
                brel[bi] = ev
                if tt == nt - 1:
                    rel[sl] = [ev] + extra_rel
        return rel, ("t", kb.cnt["t"])

    def ev_act(dst, func, scale_col=None, bias_col=None):
        def f(o, m, tt, ps, wt):
            kw = {}
            if scale_col is not None:
                kw["scale"] = scale_col(m)
            if bias_col is not None:
                kw["bias"] = bias_col(m)
            r = o.add("scalar", lambda e, d=dst(m, tt), ps=ps, kw=kw: e.activation(
                d, ps, func, **kw), wait=[wt], inc=[("s", 1)])
            return r[0]
        return f

    def emit_attention(lay):
        o = Ops(kb)
        emit_ln(o, xsrc, lambda c, tt: atile(A, c, tt))
        ln_tv = [("t", kb.cnt["t"]), ("v", kb.cnt["v"])]
        o.fence()

        nhv = 4 if lay == 0 else NH
        vw = 64 * nhv
        Vt = DA[:, 0:SC * nhv * 65].rearrange("p (s h d) -> p s h d", h=nhv, d=65)
        for k in range(DC):
            src = wvT[k * 128:(k + 1) * 128, :] if lay == 0 else \
                wv1T[k * 128:(k + 1) * 128, :]
            o.add("gpsimd", lambda e, s=src, k=k: e.dma_start(
                Bb[:, k * vw:(k + 1) * vw], s), inc=[("g2", 16)])
        vld = ("g2", kb.cnt["g2"])
        ms = o.add("vector", lambda e: e.memset(Vt[:, :, :, 64:65], 1.0),
                   inc=[("v", 1)])
        wpsum = min(512, vw)
        hph = wpsum // 64
        nbk = min(4, SC)
        brel = [None] * nbk
        for hf in range(max(1, vw // 512)):
            for g in range((SC + nbk - 1) // nbk):
                for sci in range(min(nbk, SC - g * nbk)):
                    sc = g * nbk + sci
                    mm = None
                    for k in range(DC):
                        w0 = []
                        if k == 0:
                            w0 = [vld]
                            if brel[sci] is not None:
                                w0.append(brel[sci])
                        mm = o.add("tensor", lambda e, k=k, sc=sc, hf=hf, sci=sci:
                                   e.matmul(P[sci][:, 0:wpsum],
                                            A[:, k * S + sc * 128:k * S + (sc + 1) * 128],
                                            Bb[:, k * vw + hf * 512:
                                               k * vw + hf * 512 + wpsum],
                                            start=(k == 0), stop=(k == DC - 1)),
                                   wait=w0, inc=[("t", 1)] if k == DC - 1 else [])
                    dst = Vt[:, sc, hf * hph:(hf + 1) * hph, 0:64]
                    _ = ms
                    if lay == 0:
                        ev = o.add("vector", lambda e, d=dst, sci=sci: e.tensor_tensor(
                            d, P[sci][:, 0:wpsum].rearrange("p (h d) -> p h d", d=64),
                            E[:, E_BVB:E_BVB + 256].rearrange("p (h d) -> p h d", d=64),
                            ALU.add), wait=[mm[0]], inc=[("v", 1)])
                    else:
                        ev = o.add("vector", lambda e, d=dst, sci=sci: e.tensor_copy(
                            d, P[sci][:, 0:wpsum].rearrange("p (h d) -> p h d", d=64)),
                            wait=[mm[0]], inc=[("v", 1)])
                    brel[sci] = ev[0]
        o.fence()

        if lay == 0:
            r, _ = gemm_stage(o, [wq_pan[m] for m in range(DC)], DC, DC,
                              lambda k, tt: atile(A, k, tt),
                              ev_act(lambda m, tt: atile(Bb, m, tt), AF.Identity,
                                     bias_col=lambda m: col("bq", m)),
                              E, E_E0, DC * 128, ("g0", "g1"),
                              init_rel=(ln_tv, ln_tv))
            gemm_stage(o, [wk_pan[m] for m in range(2)], DC, 2,
                       lambda k, tt: atile(A, k, tt),
                       ev_act(lambda m, tt: atile(C, m + 6, tt), AF.Identity,
                              bias_col=lambda m: col("bk", m)),
                       E, E_E0, DC * 128, ("g0", "g1"), nbank=2, bank0=2,
                       init_rel=tuple(r))
        else:
            gemm_stage(o, [qk1_pan[m] for m in range(16)], DC, 16,
                       lambda k, tt: (atile(A, k, tt)),
                       ev_act(lambda m, tt: (atile(Bb, m, tt) if m < 8
                                             else atile(C, m - 8, tt)), AF.Copy),
                       E, E_E0, DC * 128, ("g0", "g1"))
        o.fence()

        cosd, sind, pid = (cos0, sin0, pi0T) if lay == 0 else (cos1, sin1, pi1T)
        o.add("sync", lambda e: e.dma_start(A[:, 0:2 * S].bitcast(F32), cosd[:]),
              wait=[("t", kb.cnt["t"])], inc=[("y0", 16)])
        o.add("sync", lambda e: e.dma_start(A[:, 2 * S:4 * S].bitcast(F32), sind[:]),
              inc=[("y0", 16)])
        o.add("gpsimd", lambda e: e.dma_start(E[:, E_PI:E_PI + 128], pid[:]),
              inc=[("g2", 16)])
        tabw = ("y0", kb.cnt["y0"])
        piw = ("g2", kb.cnt["g2"])
        ctab = A[:, 0:2 * S].bitcast(F32)
        stab = A[:, 2 * S:4 * S].bitcast(F32)
        kch = [6, 7] if lay == 0 else list(range(DC))
        chunks = [(Bb, c) for c in range(DC)] + [(C, c) for c in kch]
        t1b = [E[:, E_XB:E_XB + 512], E[:, E_XB + 512:E_XB + 1024],
               E[:, E_E0:E_E0 + 512], E[:, E_E0 + 512:E_E0 + 1024]]
        brel = [None] * 4
        t1rel = [None] * 4
        iters = []
        for i, (arena, c) in enumerate(chunks):
            for tt in range(NT):
                iters.append((arena, c, tt))
        stash = [None, None]
        for i, (arena, c, tt) in enumerate(iters):
            bk = i % 4
            tsl = i % 4
            mm = o.add("tensor", lambda e, a=arena, c=c, tt=tt, bk=bk: e.matmul(
                P[bk][:, 0:TT], E[:, E_PI:E_PI + 128], atile(a, c, tt),
                start=True, stop=True),
                wait=[piw] + ([brel[bk]] if brel[bk] else []), inc=[("t", 1)])
            t1 = o.add("vector", lambda e, a=arena, c=c, tt=tt, tsl=tsl:
                       e.tensor_mul(t1b[tsl][:, 0:TT], atile(a, c, tt),
                                    ctab[:, tt * TT:(tt + 1) * TT]),
                wait=[tabw] + ([t1rel[tsl]] if t1rel[tsl] else []),
                inc=[("v", 1)])
            if stash[1] is not None:
                stash[1]()
            def mk_tail(a=arena, c=c, tt=tt, bk=bk, tsl=tsl, mmr=mm[0], t1r=t1[0]):
                def tail():
                    t2 = o.add("vector", lambda e: e.tensor_mul(
                        P[bk][:, 0:TT], P[bk][:, 0:TT],
                        stab[:, tt * TT:(tt + 1) * TT]),
                        wait=[mmr], inc=[("v", 1)])
                    ev = o.add("vector", lambda e: e.tensor_add(
                        atile(a, c, tt), t1b[tsl][:, 0:TT], P[bk][:, 0:TT]),
                        wait=[t1r, t2[0]], inc=[("v", 1)])
                    brel[bk] = ev[0]
                    t1rel[tsl] = ev[0]
                return tail
            stash[1] = stash[0]
            stash[0] = mk_tail()
        for st_ in (stash[1], stash[0]):
            if st_ is not None:
                st_()
        o.fence()

        if lay == 0:
            for wave in (range(6), range(6, DC)):
                for cp in wave:
                    kv = cp // 2
                    scc, so = 6 + kv // 2, 64 * (kv % 2)
                    for half in range(2):
                        if scc == cp and so == 64 * half:
                            continue
                        o.add("sync", lambda e, cp=cp, half=half, scc=scc, so=so:
                              e.dma_start(C[64 * half:64 * half + 64,
                                            cp * S:(cp + 1) * S],
                                          C[so:so + 64, scc * S:(scc + 1) * S]),
                              wait=[("y1", kb.cnt["y1"])] if cp == 6 else [],
                              inc=[("y1", 16)])
            o.add("sync", None, wait=[("y1", kb.cnt["y1"])])
            o.fence()

        if lay == 0:
            for m in range(FC):
                o.add("gpsimd", lambda e, m=m: e.dma_start(w12b[m], w12_pan[m]),
                      inc=[("gcv", 16)])
            for m in range(FC):
                o.add("gpsimd", lambda e, m=m: e.dma_start(h1b[m], h1_pan[m]),
                      inc=[("gcv", 16)])
            for m in range(FC):
                o.add("gpsimd", lambda e, m=m: e.dma_start(h2b[m], h2_pan[m]),
                      inc=[("gcv", 16)])
            for m in range(DC):
                o.add("gpsimd", lambda e, m=m: e.dma_start(w3b[m], w3_pan[m]),
                      inc=[("gcv", 16)])
        else:
            for dst, src in ((ub_initb, ub_init), (up_initb, up_init),
                             (db_initb, db_init), (dp_initb, dp_init)):
                o.add("gpsimd", lambda e, dst=dst, src=src: e.dma_start(
                    dst[:], src[:]), inc=[("gcv", 16)])
            for dst, src, n in ((ub_lhsb, ub_lhs, 6), (up_lhsb, up_lhs, 4),
                                (db_lhsb, db_lhs, 6), (dp_lhsb, dp_lhs, 4)):
                for j in range(n):
                    o.add("gpsimd", lambda e, dst=dst, src=src, j=j:
                          e.dma_start(dst[j], src[j]), inc=[("gcv", 16)])
            for r in range(8):
                o.add("gpsimd", lambda e, r=r: e.dma_start(
                    wexpb[r * 512:(r + 1) * 512, :],
                    wexpTaug[r * 512:(r + 1) * 512, :]), inc=[("gcv", 16)])
            for r in range(8):
                o.add("gpsimd", lambda e, r=r: e.dma_start(
                    wconb[r * 512:(r + 1) * 512, :],
                    wconP[r * 512:(r + 1) * 512, :]), inc=[("gcv", 16)])
        yrep = ("y1", kb.cnt["y1"])
        EB = [E[:, E_E0:E_E0 + 1024], E[:, E_E0 + 1024:E_E0 + 2048],
              E[:, E_NT:E_NT + 1024]]
        EROWF = E[:, E_ROW:E_ROW + 1024].bitcast(F32)
        ones_rf = E[:, E_BVB:E_BVB + 256].bitcast(F32)[0:33, 0:128]
        o.add("vector", lambda e: e.memset(E[32:33, E_OR:E_OR + 128], 1.0),
              wait=[("t", kb.cnt["t"]), ("s", kb.cnt["s"])], inc=[("v", 1)])
        ones_init = ("v", kb.cnt["v"])
        psrel = [None, None]
        erel = [None, None, None]
        ctrel = [None, None]
        bcrel = None
        rowrel = None
        jobs = []
        for hp in range(DC):
            for tt in range(NT):
                jobs.append((hp, tt))
        pend = None
        for j, (hp, tt) in enumerate(jobs):
            kvA = hp // 2 if lay == 0 else 2 * hp
            kvB = hp // 2 if lay == 0 else 2 * hp + 1
            scmax = SC if lay == 0 else min(SC, (tt + 1) * PJ)
            expd = [None] * scmax
            lastpv = None

            def sc_mms(sc):
                g = sc % 2
                w0 = [psrel[g]] if psrel[g] else []
                if j == 0 and sc == 0:
                    w0.append(yrep)
                o.add("tensor", lambda e, sc=sc, g=g, hp=hp, tt=tt: e.matmul(
                    P[2 * g][:, 0:TT],
                    C[0:64, hp * S + sc * 128:hp * S + (sc + 1) * 128],
                    Bb[0:64, hp * S + tt * TT:hp * S + (tt + 1) * TT],
                    start=True, stop=True), wait=w0, inc=[("t", 1)])
                return o.add("tensor", lambda e, sc=sc, g=g, hp=hp, tt=tt: e.matmul(
                    P[2 * g + 1][:, 0:TT],
                    C[64:128, hp * S + sc * 128:hp * S + (sc + 1) * 128],
                    Bb[64:128, hp * S + tt * TT:hp * S + (tt + 1) * TT],
                    start=True, stop=True), wait=[], inc=[("t", 1)])

            def exp_op(sc, mmrel):
                g = sc % 2
                eb = sc % 3
                we = [mmrel]
                if erel[eb] is not None:
                    we.append(erel[eb])
                ex = o.add("scalar", lambda e, g=g, eb=eb: e.activation(
                    EB[eb].rearrange("p (u n) -> p u n", u=2)[:, :, 0:TT],
                    PS4[:, g * 1024:(g + 1) * 1024]
                    .rearrange("p (u n) -> p u n", u=2)[:, :, 0:TT],
                    AF.Exp, scale=0.125, bias=Ef32(E_NEGB, 1)),
                    wait=we, inc=[("s", 1)])
                psrel[g] = ex[0]
                ed = ex[0]
                if lay == 1 and sc >= tt * PJ:
                    off = 128 * (PJ - 1) - 128 * (sc - tt * PJ)
                    mk = o.add("vector", lambda e, eb=eb, off=off: e.tensor_mul(
                        EB[eb][:, 0:TT], EB[eb][:, 0:TT],
                        E[:, E_CM + off:E_CM + off + TT]),
                        wait=[ed, ("gw", GW_ALL)], inc=[("v", 1)])
                    mk = o.add("vector", lambda e, eb=eb, off=off: e.tensor_mul(
                        EB[eb][:, 512:512 + TT], EB[eb][:, 512:512 + TT],
                        E[:, E_CM + off:E_CM + off + TT]),
                        wait=[], inc=[("v", 1)])
                    ed = mk[0]
                return ed

            def pv_mms(sc, ed):
                eb = sc % 3
                first = (sc == 0)
                last = (sc == scmax - 1)
                pvw = [ed]
                if first and ctrel[0] is not None:
                    pvw.extend(ctrel[0])
                o.add("tensor", lambda e, sc=sc, eb=eb, kvA=kvA,
                      first=first, last=last: e.matmul(
                          PA[0:65, 0:TT], Vt[:, sc, kvA, :], EB[eb][:, 0:TT],
                          start=first, stop=last),
                      wait=pvw, inc=[("t", 1)])
                w2 = []
                if first and ctrel[1] is not None:
                    w2.extend(ctrel[1])
                r = o.add("tensor", lambda e, sc=sc, eb=eb, kvB=kvB,
                          first=first, last=last: e.matmul(
                              PB[0:65, 0:TT], Vt[:, sc, kvB, :],
                              EB[eb][:, 512:512 + TT],
                              start=first, stop=last),
                          wait=w2, inc=[("t", 1)])
                erel[eb] = r[0]
                return r

            mm0 = sc_mms(0)
            if pend is not None:
                pend[0]()
            expd[0] = exp_op(0, mm0[0])
            for sc in range(1, scmax):
                mm = sc_mms(sc)
                if sc == 1 and pend is not None:
                    pend[1]()
                    pend = None
                lastpv = pv_mms(sc - 1, expd[sc - 1])
                expd[sc] = exp_op(sc, mm[0])
            lastpv = pv_mms(scmax - 1, expd[scmax - 1])

            def make_epi(jj=j, hp=hp, tt=tt, pvrel=lastpv[0]):
                hold = {}

                def epi_scalar():
                    nonlocal rowrel
                    cpA = o.add("vector", lambda e, hp=hp, tt=tt: e.tensor_copy(
                        A[0:64, hp * S + tt * TT:hp * S + (tt + 1) * TT],
                        PA[0:64, 0:TT]), wait=[pvrel], inc=[("v", 1)])
                    cpB = o.add("vector", lambda e, hp=hp, tt=tt: e.tensor_copy(
                        A[64:128, hp * S + tt * TT:hp * S + (tt + 1) * TT],
                        PB[0:64, 0:TT]), wait=[], inc=[("v", 1)])
                    wln = [pvrel]
                    if rowrel is not None:
                        wln.append(rowrel)
                    lnA = o.add("scalar", lambda e: e.activation(
                        EROWF[64:65, 0:TT], PA[64:65, 0:TT], AF.Ln),
                        wait=wln, inc=[("s", 1)])
                    hold["exA"] = o.add("scalar", lambda e: e.activation(
                        E[0:1, E_ROW:E_ROW + TT], EROWF[64:65, 0:TT], AF.Exp,
                        scale=-1.0), wait=[lnA[0]], inc=[("s", 1)])[0]
                    lnB = o.add("scalar", lambda e: e.activation(
                        EROWF[96:97, 0:TT], PB[64:65, 0:TT], AF.Ln),
                        wait=list(wln), inc=[("s", 1)])
                    hold["exB"] = o.add("scalar", lambda e: e.activation(
                        E[32:33, E_ROW:E_ROW + TT], EROWF[96:97, 0:TT], AF.Exp,
                        scale=-1.0), wait=[lnB[0]], inc=[("s", 1)])[0]
                    ctrel[0] = [cpA[0], lnA[0]]
                    ctrel[1] = [cpB[0], lnB[0]]

                def epi_tensor():
                    nonlocal bcrel, rowrel
                    wbc = [hold["exA"], ones_init]
                    if bcrel is not None:
                        wbc.append(bcrel)
                    o.add("tensor", lambda e: e.matmul(
                        PD[0:64, 0:TT], ones_r[0:1, 0:64],
                        E[0:1, E_ROW:E_ROW + TT],
                        start=True, stop=True, tile_position=(0, 0)),
                        wait=wbc, inc=[("t", 1)])
                    bc = o.add("tensor", lambda e: e.matmul(
                        PD[64:128, 0:TT], E[32:33, E_OR:E_OR + 64],
                        E[32:33, E_ROW:E_ROW + TT],
                        start=True, stop=True, tile_position=(32, 64)),
                        wait=[hold["exB"]], inc=[("t", 1)])
                    rowrel = bc[0]
                    rb = o.add("vector", lambda e: e.tensor_copy(
                        E[:, E_XB:E_XB + TT], PD[:, 0:TT]),
                        wait=[bc[0]], inc=[("v", 1)])
                    o.add("vector", lambda e, hp=hp, tt=tt: e.tensor_mul(
                        A[0:64, hp * S + tt * TT:hp * S + (tt + 1) * TT],
                        A[0:64, hp * S + tt * TT:hp * S + (tt + 1) * TT],
                        E[0:64, E_XB:E_XB + TT]),
                        wait=[rb[0]], inc=[("v", 1)])
                    mb = o.add("vector", lambda e, hp=hp, tt=tt: e.tensor_mul(
                        A[64:128, hp * S + tt * TT:hp * S + (tt + 1) * TT],
                        A[64:128, hp * S + tt * TT:hp * S + (tt + 1) * TT],
                        E[64:128, E_XB:E_XB + TT]),
                        wait=[], inc=[("v", 1)])
                    bcrel = mb[0]
                return epi_scalar, epi_tensor
            pend = make_epi()
        if pend is not None:
            pend[0]()
            pend[1]()
        o.fence()

        pan = wo_pan if lay == 0 else wo1_pan
        bias = (lambda m: col("bo", m)) if lay == 0 else None
        tvw = [("t", kb.cnt["t"]), ("v", kb.cnt["v"])]
        gemm_stage(o, [pan[m] for m in range(DC)], DC, DC,
                   lambda k, tt: atile(A, k, tt),
                   ev_act(lambda m, tt: atile(Bb, m, tt), AF.Identity, bias_col=bias),
                   E, E_E0, DC * 128, ("g0", "g1"),
                   init_rel=(tvw, tvw))
        o.fence()
        for p in range(NT):
            emit_ln_res(lambda c, p=p: atile(Bb, c, p), p, TT, o=o,
                        wait0=[("s", kb.cnt["s"])] if p == 0 else [], bar=False)
        o.run()
        kb.barrier()

    def emit_ffn0():
        o = Ops(kb)
        emit_ln(o, xsrc, lambda c, tt: atile(C, c, tt))
        o.fence()
        prev = {"w1": (None, None), "h1": (None, None),
                "h2": (None, None), "w3": (None, None)}
        gcv_all = ("gcv", kb.cnt["gcv"])
        for p in range(NT):
            fp32pass = False

            sc_entry = ("s", kb.cnt["s"])
            vc_entry = ("v", kb.cnt["v"])
            o.add("scalar", None, wait=[vc_entry])
            w12p = [w12b[m] for m in range(FC)]
            h1p = [h1b[m] for m in range(FC)]
            h2p = [h2b[m] for m in range(FC)]
            w3p = [w3b[m] for m in range(DC)]
            s1sems = ("g0", "g1")
            s1st = ("sa0", "sa1")
            s1rel = [list(x) if x else None for x in prev["w1"]]
            pdone = [None, None]
            brel = [None] * 4
            vm = None
            for m in range(FC):
                sl = m % 2
                w = list(s1rel[sl]) if s1rel[sl] is not None else []
                if p == 0 and m == sl:
                    w.append(gcv_all)
                slot = DA[:, sl * 2048:(sl + 1) * 2048]
                pdone[sl] = o.add("gpsimd", lambda e, s=w12p[m], d=slot:
                                  e.dma_start(d.rearrange("p (k o) -> p k o",
                                                          o=128), s),
                                  wait=w, inc=[(s1sems[sl], 16)])[0]
                extra = []
                mm1 = mm2 = None
                bk1, bk2 = sl, 2 + sl
                for k in range(DC):
                    wt = [pdone[sl]]
                    if k == 0:
                        wt.append(sc_entry)
                        if brel[bk1] is not None:
                            wt.append(brel[bk1])
                    mm1 = o.add("tensor", lambda e, k=k, slot=slot, bk1=bk1,
                                m=m, p=p: e.matmul(
                                    P[bk1][:, 0:TT],
                                    slot[:, k * 128:(k + 1) * 128],
                                    atile(C, k, p),
                                    start=(k == 0), stop=(k == DC - 1)),
                                wait=wt if k == 0 else [],
                                inc=[("t", 1)] if k == DC - 1 else [])
                for k in range(DC):
                    wt = []
                    if k == 0 and brel[bk2] is not None:
                        wt.append(brel[bk2])
                    mm2 = o.add("tensor", lambda e, k=k, slot=slot, bk2=bk2,
                                m=m, p=p: e.matmul(
                                    P[bk2][:, 0:TT],
                                    slot[:, (DC + k) * 128:(DC + k + 1) * 128],
                                    atile(C, k, p),
                                    start=(k == 0), stop=(k == DC - 1)),
                                wait=wt if k == 0 else [],
                                inc=[("t", 1)] if k == DC - 1 else [])
                e1 = o.add("scalar", lambda e, m=m, bk1=bk1: e.activation(
                    A[:, m * TT:(m + 1) * TT], P[bk1][:, 0:TT], AF.Silu,
                    scale=col("s1", m)), wait=[mm1[0]], inc=[("s", 1)])
                e2 = o.add("scalar", lambda e, m=m, bk2=bk2: e.activation(
                    Bb[:, m * TT:(m + 1) * TT], P[bk2][:, 0:TT], AF.Copy,
                    scale=col("s2", m)), wait=[mm2[0]], inc=[("s", 1)])
                brel[bk1] = e1[0]
                brel[bk2] = e2[0]
                vm = o.add("vector", lambda e, m=m: e.tensor_mul(
                    A[:, m * TT:(m + 1) * TT], A[:, m * TT:(m + 1) * TT],
                    Bb[:, m * TT:(m + 1) * TT]),
                    wait=[e1[0], e2[0]], inc=[("v", 1)])
                s1rel[sl] = [e1[0], e2[0]] + extra
            prev["w1"] = (s1rel[0], s1rel[1])
            t_s1 = ("t", kb.cnt["t"])
            vmul_done = vm[0]
            o.add("scalar", None, wait=[vmul_done])
            rh1, t_h1 = gemm_stage(o, h1p, FC, FC,
                                   lambda k, tt: A[:, k * TT:(k + 1) * TT],
                                   ev_act(lambda m, tt: Bb[:, m * TT:(m + 1) * TT],
                                          AF.Gelu, scale_col=lambda m: col("sh1", m)),
                                   DA, 0, FC * 128, ("g4", "g5"), nt=1,
                                   nbank=4, bank0=0, slot_offs=(4096, 8192),
                                   init_rel=prev["h1"], first_mm_wait=[vmul_done],
                                   post_panel=None)
            prev["h1"] = rh1
            sc_h1 = ("s", kb.cnt["s"])
            o.add("scalar", None, wait=[t_h1])
            rh2, t_h2 = gemm_stage(o, h2p, FC, FC,
                                   lambda k, tt: Bb[:, k * TT:(k + 1) * TT],
                                   ev_act(lambda m, tt: A[:, m * TT:(m + 1) * TT],
                                          AF.Gelu, scale_col=lambda m: col("sh2", m)),
                                   DA, 0, FC * 128, ("g6", "g7"), nt=1,
                                   nbank=4, slot_offs=(12288, 0),
                                   init_rel=(prev["h2"][0],
                                             (s1rel[0] or []) + (s1rel[1] or []) +
                                             [t_s1]),
                                   bank0=0,
                                   first_mm_wait=[sc_h1],
                                   post_panel=None)
            prev["h2"] = (rh2[0], rh2[1])
            prev["w1"] = (prev["w1"][0] + (rh2[1] or []),
                          prev["w1"][1] + (rh2[1] or []))
            sc_h2 = ("s", kb.cnt["s"])
            o.add("scalar", None, wait=[t_h2])
            iw30 = (rh1[0] or []) + [t_h1] + ((prev["w3"][0] or []) if p > 0 else [])
            iw31 = (rh1[1] or []) + [t_h1] + ((prev["w3"][1] or []) if p > 0 else [])
            rw3, t_w3 = gemm_stage(o, w3p, FC, DC,
                                   lambda k, tt: A[:, k * TT:(k + 1) * TT],
                                   ev_act(lambda m, tt: Bb[:, m * TT:(m + 1) * TT],
                                          AF.Copy, scale_col=lambda m: col("s3", m)),
                                   DA, 0, FC * 128, ("g2", "g3"), nt=1,
                                   slot_offs=(4096, 8192),
                                   init_rel=(iw30, iw31), first_mm_wait=[sc_h2],
                                   post_panel=None)
            prev["w3"] = rw3
            prev["h1"] = ((prev["h1"][0] or []) + (rw3[0] or []),
                          (prev["h1"][1] or []) + (rw3[1] or []))
            emit_ln_res(lambda c: Bb[:, c * TT:(c + 1) * TT], p, TT, o=o,
                        wait0=[("s", kb.cnt["s"])], bar=False)
        o.run()
        kb.barrier()

    def emit_compose():
        o = Ops(kb)
        Xv = A[:, 0:4096]
        Fv = [Bb[:, 0:4096], Bb[:, 4096:8192]]
        NBP = 4
        RHSs = [C[:, i * 1028: i * 1028 + 1025] for i in range(NBP)]
        OUTs = [C[:, 4112 + i * 1028: 4112 + i * 1028 + 1025] for i in range(NBP)]
        WST = [Bb[:, 8192 + i * 1024: 8192 + (i + 1) * 1024] for i in range(2)]
        st8 = {"rhs": [None] * NBP, "out": [None] * NBP,
               "pt": [None, None], "pbank": [None] * 4, "wst": [None, None]}

        def chains(init_d, lhs_d, nstg):
            tprev = ("t", kb.cnt["t"])
            ld = o.add("scalar", lambda e: e.dma_start(Xv, init_d[:]),
                       wait=[tprev], inc=[("h1", 16)])
            relc = [ld[0]] * 32
            fload = [None, None]
            fuse = [None, None]
            for j in range(nstg):
                sl = j % 2
                wf = [fuse[sl]] if fuse[sl] else [tprev]
                eng, sem = ("gpsimd", "g1") if sl == 0 else ("sync", "h0")
                fload[sl] = o.add(eng, lambda e, j=j, sl=sl: e.dma_start(
                    Fv[sl], lhs_d[j]), wait=wf, inc=[(sem, 16)])[0]
                lastmm = None
                for c in range(32):
                    bk = c % 4
                    wt = [relc[c], fload[sl]]
                    if st8["pbank"][bk] is not None:
                        wt.append(st8["pbank"][bk])
                    mm = o.add("tensor", lambda e, c=c, bk=bk, sl=sl: e.matmul(
                        P[bk][:, 0:128], Fv[sl][:, c * 128:(c + 1) * 128],
                        Xv[:, c * 128:(c + 1) * 128], start=True, stop=True),
                        wait=wt, inc=[("t", 1)])
                    lastmm = mm[0]
                    if c % 2 == 1:
                        q2 = PS4[:, (bk - 1) * 512:(bk - 1) * 512 + 1024]
                        eng = "scalar" if (c // 2) % 2 == 0 else "vector"
                        sv = "s" if (c // 2) % 2 == 0 else "v"
                        ev = o.add(eng, lambda e, c=c, q2=q2, eng=eng: (
                            e.activation(Xv[:, (c - 1) * 128:(c + 1) * 128],
                                         q2.rearrange("p (u n) -> p u n", u=2)
                                         [:, :, 0:128], AF.Copy)
                            if eng == "scalar" else
                            e.tensor_copy(Xv[:, (c - 1) * 128:(c + 1) * 128],
                                          q2.rearrange("p (u n) -> p u n", u=2)
                                          [:, :, 0:128])),
                            wait=[mm[0]], inc=[(sv, 1)])
                        st8["pbank"][bk] = ev[0]
                        st8["pbank"][bk - 1] = ev[0]
                        relc[c] = ev[0]
                        relc[c - 1] = ev[0]
                fuse[sl] = lastmm
            return relc

        def blockpass(relc, splits, rhs_load, use_out, rhs_w=1025):
            for c in range(32):
                sl = c % NBP
                wr = [st8["rhs"][sl]] if st8["rhs"][sl] else []
                rw = rhs_load(o, c, RHSs[sl][:, 0:rhs_w], wr, sl)
                evs = []
                for i, (a, b) in enumerate(splits):
                    bk = i
                    wt = [relc[c], rw]
                    if st8["pbank"][bk] is not None:
                        wt.append(st8["pbank"][bk])
                    mm = o.add("tensor", lambda e, c=c, a=a, b=b, bk=bk, sl=sl:
                               e.matmul(P[bk][:, 0:b - a],
                                        Xv[:, c * 128:(c + 1) * 128],
                                        RHSs[sl][:, a:b], start=True, stop=True),
                               wait=wt, inc=[("t", 1)])
                    st8["rhs"][sl] = mm[0]
                    wo = [mm[0]]
                    if st8["out"][sl] is not None:
                        wo += st8["out"][sl]
                    ev = o.add("vector", lambda e, a=a, b=b, bk=bk, sl=sl:
                               e.tensor_copy(OUTs[sl][:, a:b], P[bk][:, 0:b - a]),
                               wait=wo, inc=[("v", 1)])
                    st8["pbank"][bk] = ev[0]
                    evs.append(ev[0])
                st8["out"][sl] = use_out(c, OUTs[sl], evs)

        relc = chains(ub_initb, ub_lhsb, 6)

        def ld_wexp(o, c, dst, wr, sl):
            eng = "gpsimd" if sl < 2 else "scalar"
            return o.add(eng, lambda e, c=c, dst=dst: e.dma_start(
                dst, wexpb[c * 128:(c + 1) * 128, :]), wait=wr,
                inc=[(("g4", "g5", "y3", "sc0")[sl], 16)])[0]

        def st_t1(c, OUT, evs):
            st = o.add("sync", lambda e, c=c, OUT=OUT: e.dma_start(
                t1s[c * 128:(c + 1) * 128, :], OUT), wait=evs,
                inc=[(("sd0", "sd1", "se0", "se1")[c % 4], 16)])
            return [st[0]]
        blockpass(relc, [(0, 512), (512, 1024), (1024, 1025)], ld_wexp, st_t1)
        t1_done = [(sn, kb.cnt[sn]) for sn in ("sd0", "sd1", "se0", "se1")]
        relc = chains(up_initb, up_lhsb, 4)

        def ld_t1p(o, c, dst, wr, sl):
            eng = "sync" if sl < 2 else "scalar"
            return o.add(eng, lambda e, c=c, dst=dst: e.dma_start(
                dst, t1s.rearrange("(q cc r) m -> q r cc m",
                                   cc=32, r=4)[:, :, c, :]),
                wait=wr + t1_done, inc=[(("y1", "y2", "y3", "sc0")[sl], 16)])[0]

        def use_wpp(c, OUT, evs):
            bv = o.add("vector", lambda e, c=c, OUT=OUT: e.tensor_add(
                col("bppp", c), OUT[:, 1024:1025], col("upb", c)),
                wait=[evs[2]], inc=[("v", 1)])
            wsl = c % 2
            last_tm = None
            tvs = []
            for k in range(DC):
                psl = k % 2
                PTk = PTS[psl]
                wt = [evs[k // 4], ("gw", GW_ALL)]
                if st8["pt"][psl] is not None:
                    wt.append(st8["pt"][psl])
                tm = o.add("tensor", lambda e, k=k, OUT=OUT, PTk=PTk: e.transpose(
                    PTk, OUT[:, k * 128:(k + 1) * 128], E[:, E_ID:E_ID + 128]),
                    wait=wt, inc=[("t", 1)])
                wv = [tm[0]]
                if k < 2 and st8["wst"][wsl] is not None:
                    wv.append(st8["wst"][wsl])
                tv = o.add("scalar", lambda e, k=k, psl=psl, PTk=PTk, wsl=wsl:
                           e.activation(WST[wsl][:, k * 128:(k + 1) * 128],
                                        PTk, AF.Copy), wait=wv, inc=[("s", 1)])
                st8["pt"][psl] = tv[0]
                tvs.append(tv[0])
                last_tm = tm[0]
            st = o.add("gpsimd", lambda e, c=c, wsl=wsl: e.dma_start(
                wup_pan[c], WST[wsl].rearrange("p (k o) -> p k o", o=128)),
                wait=tvs, inc=[("g8" if wsl == 0 else "g9", 16)])
            st8["wst"][wsl] = st[0]
            return [last_tm, bv[0]]
        blockpass(relc, [(0, 512), (512, 1024), (1024, 1025)], ld_t1p, use_wpp)
        relc = chains(dp_initb, dp_lhsb, 4)

        def ld_wcon(o, c, dst, wr, sl):
            eng = "gpsimd" if sl < 2 else "scalar"
            return o.add(eng, lambda e, c=c, dst=dst: e.dma_start(
                dst[:, 0:1024], wconb[c * 128:(c + 1) * 128, :]), wait=wr,
                inc=[(("g4", "g5", "y3", "sc0")[sl], 16)])[0]

        def st_u1(c, OUT, evs):
            st = o.add("sync", lambda e, c=c, OUT=OUT: e.dma_start(
                u1s[c * 128:(c + 1) * 128, :], OUT[:, 0:1024]),
                wait=evs, inc=[(("sd0", "sd1", "se0", "se1")[c % 4], 16)])
            return [st[0]]
        blockpass(relc, [(0, 512), (512, 1024)], ld_wcon, st_u1, rhs_w=1024)
        u1_done = [(sn, kb.cnt[sn]) for sn in ("sd0", "sd1", "se0", "se1")]
        relc = chains(db_initb, db_lhsb, 6)

        def ld_u1p(o, c, dst, wr, sl):
            return o.add("sync", lambda e, c=c, dst=dst: e.dma_start(
                dst[:, 0:1024], u1s.rearrange("(q cc r) m -> q r cc m",
                                              cc=32, r=4)[:, :, c, :]),
                wait=wr + u1_done, inc=[(("y1", "y2", "sd0", "sd1")[sl], 16)])[0]

        def st_wcon(c, OUT, evs):
            eng = "scalar" if c % 2 == 0 else "gpsimd"
            sem = ("sa0", "g8", "sa1", "g9")[c % 4]
            st = None
            for m in range(DC):
                st = o.add(eng, lambda e, c=c, m=m, OUT=OUT: e.dma_start(
                    wcon_pan.rearrange("m (cc r) q d -> q r cc m d",
                                       r=4)[:, :, c, m, :],
                    OUT[:, m * 128:(m + 1) * 128]),
                    wait=evs, inc=[(sem, 16)])
            return [st[0]]
        blockpass(relc, [(0, 512), (512, 1024)], ld_u1p, st_wcon, rhs_w=1024)
        o.run()
        kb.barrier()

    def emit_ffn1():
        o = Ops(kb)
        emit_ln(o, xsrc, lambda c, tt: atile(C, c, tt))
        o.fence()
        S1o = 4096
        prev_up = (None, None)
        prev_dn = (None, None)
        for p in range(NT):
            sc_entry = ("s", kb.cnt["s"])
            vc_entry = ("v", kb.cnt["v"])
            o.add("scalar", None, wait=[vc_entry])
            rup, t_up = gemm_stage(o, [wup_pan[m] for m in range(FC)], DC, FC,
                                   lambda k, tt, p=p: atile(C, k, p),
                                   ev_act(lambda m, tt: A[:, m * TT:(m + 1) * TT],
                                          AF.Gelu, bias_col=lambda m: col("bppp", m)),
                                   DA, 0, DC * 128, ("g0", "g1", "g4", "g5"), nt=1,
                                   nbank=4,
                                   init_rel=prev_up, first_mm_wait=[sc_entry])
            prev_up = rup
            sc_up = ("s", kb.cnt["s"])
            rdn, t_dn = gemm_stage(o, [wcon_pan[m] for m in range(DC)], FC, DC,
                                   lambda k, tt: A[:, k * TT:(k + 1) * TT],
                                   ev_act(lambda m, tt: Bb[:, m * TT:(m + 1) * TT],
                                          AF.Identity, bias_col=lambda m: col("bout", m)),
                                   DA, S1o, FC * 128, ("g2", "g3", "g6"), nt=1,
                                   nbank=4, bank0=0,
                                   init_rel=prev_dn, first_mm_wait=[sc_up])
            prev_dn = rdn
            emit_ln_res(lambda c: Bb[:, c * TT:(c + 1) * TT], p, TT, o=o,
                        wait0=[("s", kb.cnt["s"])], bar=False)
        o.run()
        kb.barrier()

    emit_attention(0)
    emit_ffn0()
    emit_attention(1)
    emit_compose()
    emit_ffn1()

    o = Ops(kb)
    o.add("sync", lambda e: e.dma_start(
        yT.rearrange("(c p) s -> p c s", p=128),
        X[:].rearrange("p (c s) -> p c s", c=DC)), inc=[("yo", 16)])
    o.add("sync", None, wait=[("yo", kb.cnt["yo"])])
    o.run()
    return nc



def _pan(wT, KC, MC):
    return np.ascontiguousarray(
        wT.reshape(KC, 128, MC, 128).transpose(2, 1, 0, 3)).astype(np.float32)


def _cols(v):
    v = np.asarray(v, dtype=np.float32)
    return np.ascontiguousarray(v.reshape(-1, 128).T)


def _factor_blocks(W_s, stage, space, transpose):
    f = np.arange(FH)
    st = 1 << stage
    if space == "orig":
        pos = f
        of = f
    else:
        pos = np.arange(FH)
        of = _SIG[pos]
    kk = ((of >> (stage + 1)) << stage) | (of & (st - 1))
    bb = (of >> stage) & 1
    par = pos ^ (st if space == "orig" else _PSTRIDE[stage])
    blocks = np.zeros((32, 128, 128), dtype=np.float32)
    bidx, r = pos >> 7, pos & 127
    pr = par & 127
    assert np.all((par >> 7) == bidx)
    blocks[bidx, r, r] = W_s[kk, bb, bb]
    blocks[bidx, r, pr] = W_s[kk, bb, 1 - bb]
    if transpose:
        blocks = blocks.transpose(0, 2, 1)
    return np.ascontiguousarray(blocks.transpose(1, 0, 2))


def _rope_tables_L0(S):
    inv = 1.0 / 10000.0 ** (np.arange(0, HD, 2, dtype=np.float64) / HD)
    pos = np.arange(S, dtype=np.float64)
    ang = pos[:, None] * inv[None, :]
    sin2 = np.concatenate([np.sin(ang), np.sin(ang)], 1)
    cos2 = np.concatenate([np.cos(ang), np.cos(ang)], 1)
    return (np.tile(cos2.T, (2, 1)).astype(np.float32),
            np.tile(sin2.T, (2, 1)).astype(np.float32))


def _rope_tables_L1(S):
    half = HD // 2
    inv = 1.0 / 10000.0 ** (np.arange(half, dtype=np.float64) / half)
    t = np.arange(S, dtype=np.float64)
    fr = t[:, None] * inv[None, :]
    c = np.cos(fr).T
    s = np.sin(fr).T
    return (np.tile(np.concatenate([c, c], 0), (2, 1)).astype(np.float32),
            np.tile(np.concatenate([s, s], 0), (2, 1)).astype(np.float32))


def _pi_L0():
    Pi = np.zeros((64, 64), dtype=np.float32)
    j = np.arange(32)
    Pi[j, 2 * j + 1] = -1.0
    Pi[32 + j, 2 * j] = 1.0
    blk = np.zeros((128, 128), dtype=np.float32)
    blk[0:64, 0:64] = Pi
    blk[64:128, 64:128] = Pi
    return np.ascontiguousarray(blk.T)


def _pi_L1():
    Pi = np.zeros((64, 64), dtype=np.float32)
    j = np.arange(32)
    Pi[j, j + 32] = -1.0
    Pi[j + 32, j] = 1.0
    blk = np.zeros((128, 128), dtype=np.float32)
    blk[0:64, 0:64] = Pi
    blk[64:128, 64:128] = Pi
    return np.ascontiguousarray(blk.T)


def host_inputs(g, S):
    sh = {}
    sh["wq_pan"] = _pan(g["L0_attn_Wq"].T, DC, DC)
    sh["wk_pan"] = _pan(g["L0_attn_Wk"].T, DC, 2)
    sh["wvT"] = np.ascontiguousarray(g["L0_attn_Wv"].T)
    sh["wo_pan"] = _pan(g["L0_attn_Wo"].T, DC, DC)
    sh["bq_c"] = _cols(g["L0_attn_bq"])
    sh["bk_c"] = _cols(g["L0_attn_bk"])
    sh["bv_r"] = np.ascontiguousarray(g["L0_attn_bv"].reshape(1, 256))
    sh["bo_c"] = _cols(g["L0_attn_bo"])
    sh["w12_pan"] = np.ascontiguousarray(np.concatenate(
        [_pan(g["L0_ffn_w1_w"].T, DC, FC), _pan(g["L0_ffn_w2_w"].T, DC, FC)],
        axis=2))
    sh["h1_pan"] = _pan(g["L0_ffn_h1_w"].T, FC, FC)
    sh["h2_pan"] = _pan(g["L0_ffn_h2_w"].T, FC, FC)
    sh["w3_pan"] = _pan(g["L0_ffn_w3_w"].T, FC, DC)
    sh["s1_c"] = _cols(g["L0_ffn_w1_s"])
    sh["s2_c"] = _cols(g["L0_ffn_w2_s"])
    sh["sh1_c"] = _cols(g["L0_ffn_h1_s"])
    sh["sh2_c"] = _cols(g["L0_ffn_h2_s"])
    sh["s3_c"] = _cols(g["L0_ffn_w3_s"])
    wqkvT = g["L1_attn_w_qkv"].T
    sh["qk1_pan"] = _pan(wqkvT[:, 0:2048], DC, 16)
    sh["wv1T"] = np.ascontiguousarray(wqkvT[:, 2048:3072])
    sh["wo1_pan"] = _pan(g["L1_attn_w_out"].T, DC, DC)
    sh["wexpTaug"] = np.ascontiguousarray(
        np.concatenate([g["L1_ffn_W_exp"].T,
                        g["L1_ffn_b_exp"].reshape(-1, 1)], axis=1)).astype(np.float32)
    sh["wconP"] = np.ascontiguousarray(g["L1_ffn_W_con"][_SIG, :])
    sh["upb_c"] = _cols(g["L1_ffn_up_bias"][_SIG])
    sh["bcon_c"] = _cols(g["L1_ffn_b_con"])
    sh["dbias_c"] = _cols(g["L1_ffn_down_bias"])
    uw = g["L1_ffn_up_weights"].reshape(12, FH // 2, 2, 2)
    dw = g["L1_ffn_down_weights"].reshape(12, FH // 2, 2, 2)
    sh["ub_init"] = _factor_blocks(uw[6], 6, "orig", True)
    sh["ub_lhs"] = np.stack([_factor_blocks(uw[j], j, "orig", False)
                             for j in [5, 4, 3, 2, 1, 0]])
    sh["up_init"] = _factor_blocks(uw[11], 11, "perm", True)
    sh["up_lhs"] = np.stack([_factor_blocks(uw[j], j, "perm", False)
                             for j in [10, 9, 8, 7]])
    sh["db_init"] = _factor_blocks(dw[0], 0, "orig", False)
    sh["db_lhs"] = np.stack([_factor_blocks(dw[j], j, "orig", True)
                             for j in [1, 2, 3, 4, 5, 6]])
    sh["dp_init"] = _factor_blocks(dw[7], 7, "perm", False)
    sh["dp_lhs"] = np.stack([_factor_blocks(dw[j], j, "perm", True)
                             for j in [8, 9, 10, 11]])
    sh["cos0"], sh["sin0"] = _rope_tables_L0(S)
    sh["cos1"], sh["sin1"] = _rope_tables_L1(S)
    sh["pi0T"] = _pi_L0()
    sh["pi1T"] = _pi_L1()
    sh["identw"] = np.eye(128, dtype=np.float32)
    TT = min(512, S)
    PJ = TT // 128
    W = TT + 128 * (PJ - 1)
    sp = np.arange(128)[:, None]
    u = np.arange(W)[None, :]
    sh["cmaskx"] = (sp <= u - 128 * (PJ - 1)).astype(np.float32)
    return sh


_CACHE = {}


def kernel(**inputs):
    S = int(inputs["L0_attn_hidden_states"].shape[1])
    if S not in _CACHE:
        _CACHE[S] = build_program(S)
    nc = _CACHE[S]
    g = {k: np.ascontiguousarray(np.asarray(v), dtype=np.float32)
         for k, v in inputs.items()}
    sh = host_inputs(g, S)
    x = g["L0_attn_hidden_states"]
    Bn = x.shape[0]
    in_maps = []
    for b in range(Bn):
        m = dict(sh)
        m["xT"] = np.ascontiguousarray(x[b].T)
        in_maps.append(m)
    res = run_bass_kernel_spmd(nc, in_maps, core_ids=list(range(Bn)))
    out = np.stack([np.asarray(r["yT"]).T for r in res.results], axis=0)
    return np.ascontiguousarray(out.astype(np.float32))
